# version 01
# baseline (speedup 1.0000x reference)
import math
import numpy as np
import ml_dtypes
import concourse.bass as bass
import concourse.mybir as mybir
from concourse.ap import AP
from concourse.bass_utils import run_bass_kernel_spmd

F32 = mybir.dt.float32
BF16 = mybir.dt.bfloat16
I32 = mybir.dt.int32
ALU = mybir.AluOpType
AF = mybir.ActivationFunctionType
AX = mybir.AxisListType

B, N, D = 2, 4096, 2048
H0, HD, CH = 16, 128, 128
NCORE = 8
TPC = 1024
NCH = TPC // CH
DB = D // 128
E0, DE0, DS0 = 8, 1408, 2816
FB0 = DE0 // 128
SB0 = DS0 // 128
H1, KV1, NB1, NV1, W1 = 16, 4, 8, 8, 32
QKD = W1 * NB1
E1, F1 = 7, 1024
CAP1 = (B * N) // E1
BISECT_ITERS = 38

ENGS = ("tensor", "vector", "scalar", "gpsimd", "sync")


class _Op:
    __slots__ = ("eng", "fn", "reads", "writes", "seq", "kind", "inc_needed",
                 "deps", "_semkey", "_semval", "_issue_wait", "tag")

    def __init__(self, eng, fn, reads, writes, kind, tag=""):
        self.eng, self.fn, self.reads, self.writes = eng, fn, reads, writes
        self.kind = kind
        self.inc_needed = kind in ("dma", "cc")
        self.deps = []
        self.tag = tag


class Prog:
    def __init__(self, nc):
        self.nc = nc
        self.ops = []
        self.eng_ops = {e: [] for e in ENGS}
        self.state = {}
        self._fin = False

    def _touch(self, op):
        deps = op.deps
        for res, lo, hi, is_w in op.reads + op.writes:
            bid = id(res)
            segs = self.state.setdefault(bid, [])
            if is_w:
                keep = []
                for s in segs:
                    if s[0] < hi and lo < s[1]:
                        if s[2] is not None:
                            deps.append(s[2])
                        deps.extend(s[3])
                        if s[0] >= lo and s[1] <= hi:
                            continue
                    keep.append(s)
                keep.append([lo, hi, op, []])
                self.state[bid] = keep
            else:
                for s in segs:
                    if s[0] < hi and lo < s[1]:
                        if s[2] is not None:
                            deps.append(s[2])
                        s[3].append(op)
                segs.append([lo, hi, None, [op]])

    def _add(self, eng, fn, r, w, kind, tag=""):
        assert not self._fin
        op = _Op(eng, fn, [x + (False,) for x in r], [x + (True,) for x in w], kind, tag)
        op.seq = len(self.eng_ops[eng])
        self.eng_ops[eng].append(op)
        self.ops.append(op)
        self._touch(op)
        seen, deps = set(), []
        for d in op.deps:
            if d is op or id(d) in seen:
                continue
            seen.add(id(d))
            if d.eng == eng and d.kind == "op" and eng == "tensor" \
                    and kind == "op":
                continue
            deps.append(d)
        op.deps = deps
        return op

    def finalize(self):
        assert not self._fin
        self._fin = True
        nc = self.nc
        for op in self.ops:
            for d in op.deps:
                d.inc_needed = True
        NDS = 8
        sems = {}
        for e in ENGS:
            if any(o.kind == "op" and o.inc_needed for o in self.eng_ops[e]):
                sems[("op", e)] = nc.semaphore(f"sem_{e}").__enter__()
            n_dma = sum(1 for o in self.eng_ops[e] if o.kind == "dma")
            for j in range(min(NDS, n_dma)):
                sems[("dma", e, j)] = nc.semaphore(f"dmas_{e}_{j}").__enter__()
            if any(o.kind == "cc" for o in self.eng_ops[e]):
                sems[("cc", e)] = nc.semaphore(f"cc_{e}").__enter__()
        self.sems = sems
        for e in ENGS:
            c_op = c_dma = c_cc = 0
            for op in self.eng_ops[e]:
                if op.kind == "dma":
                    j = c_dma % NDS
                    g = c_dma // NDS
                    c_dma += 1
                    op._semkey, op._semval = ("dma", e, j), 16 * (g + 1)
                    op._issue_wait = 16 * g if g > 0 else None
                elif op.kind == "cc":
                    op._semkey, op._semval = ("cc", e), c_cc + 1
                    op._issue_wait = c_cc if c_cc > 0 else None
                    c_cc += 1
                elif op.inc_needed:
                    c_op += 1
                    op._semkey, op._semval = ("op", e), c_op
                    op._issue_wait = None
                else:
                    op._semkey = op._semval = None
                    op._issue_wait = None
        nwaits = [0]
        with nc.Block() as block:
            for e in ENGS:
                ops = self.eng_ops[e]
                if not ops:
                    continue

                def emit(eng, ops=ops, e=e):
                    waited = {}
                    for op in ops:
                        for d in op.deps:
                            sk, v = d._semkey, d._semval
                            if waited.get(sk, -1) < v:
                                eng.wait_ge(self.sems[sk], v)
                                waited[sk] = v
                                nwaits[0] += 1
                        if op._issue_wait is not None:
                            sk = op._semkey
                            if waited.get(sk, -1) < op._issue_wait:
                                eng.wait_ge(self.sems[sk], op._issue_wait)
                                waited[sk] = op._issue_wait
                                nwaits[0] += 1
                        try:
                            inst = op.fn(eng)
                        except Exception:
                            import traceback
                            print(f"EMIT FAIL tag={op.tag} eng={e} seq={op.seq}")
                            traceback.print_exc()
                            raise
                        if op._semkey is not None:
                            if op.kind == "op" and e in ("vector", "scalar",
                                                         "gpsimd"):
                                eng.drain(fusable=False).then_inc(
                                    self.sems[op._semkey], 1)
                            else:
                                inst.then_inc(self.sems[op._semkey],
                                              16 if op.kind == "dma" else 1)

                getattr(block, e)(emit)
        self.n_waits = nwaits[0]
        return self


class Pool:

    def __init__(self, nc, name, cols, dtype):
        self.t = nc.alloc_sbuf_tensor(name, [128, cols], dtype)
        self.cols = cols
        self.free_list = [(0, cols)]
        self.hw = 0

    def alloc(self, n, name=""):
        import os
        nr = (n + 31) & ~31
        for i, (a, b) in enumerate(self.free_list):
            if b - a >= nr:
                self.free_list[i] = (a + nr, b)
                if self.free_list[i][0] == self.free_list[i][1]:
                    del self.free_list[i]
                self.hw = max(self.hw, a + nr)
                if os.environ.get("PRINTALLOC"):
                    print(f"ALLOC {self.t.name} {name}: [{a},{a + nr})")
                v = View(self, a, n, name)
                v.nres = nr
                return v
        raise MemoryError(f"pool {self.t.name}: no {nr} cols (free={self.free_list})")

    def free(self, view):
        import os
        a, b = view.c0, view.c0 + getattr(view, "nres", view.n)
        if os.environ.get("PRINTALLOC"):
            print(f"FREE  {self.t.name} {view.name}: [{a},{b})")
        self.free_list.append((a, b))
        self.free_list.sort()
        merged = []
        for seg in self.free_list:
            if merged and merged[-1][1] == seg[0]:
                merged[-1] = (merged[-1][0], seg[1])
            else:
                merged.append(list(seg))
        self.free_list = [tuple(s) for s in merged]


class View:

    def __init__(self, pool, c0, n, name=""):
        self.pool, self.c0, self.n, self.name = pool, c0, n, name

    def ap(self, psl=slice(None), csl=slice(None)):
        a, b, _ = csl.indices(self.n)
        return self.pool.t[psl, self.c0 + a:self.c0 + b]

    def __getitem__(self, key):
        psl, csl = key
        return self.ap(psl, csl)

    def res(self, csl=slice(None)):
        a, b, _ = csl.indices(self.n)
        return (self.pool.t, self.c0 + a, self.c0 + b)

    def free(self):
        self.pool.free(self)


def whole(t):
    return (t, 0, 1 << 30)


class Ctx:

    def __init__(self, nc):
        self.nc = nc
        self.p = Prog(nc)
        self.inputs = {}
        self.outputs = {}
        self.aps = {}
        self.psum = {}
        self.out_dma_res = []

    def inp(self, name, shape, dtype, percore):
        t = self.nc.dram_tensor(name, list(shape), dtype, kind="ExternalInput")
        self.inputs[name] = percore
        ap = t.ap()
        self.aps[name] = ap
        return ap

    def out(self, name, shape, dtype):
        t = self.nc.dram_tensor(name, list(shape), dtype, kind="ExternalOutput")
        ap = t.ap()
        self.aps[name] = ap
        return ap

    def dram(self, name, shape, dtype):
        t = self.nc.dram_tensor(name, list(shape), dtype, kind="Internal")
        ap = t.ap()
        self.aps[name] = ap
        return ap

    def alloc_psum(self, name, shape, dtype):
        t = self.nc.alloc_psum_tensor(name, list(shape), dtype)
        self.psum[name] = t
        return t

    def in_maps(self, ncore=NCORE):
        return [{k: np.ascontiguousarray(fn(c)) for k, fn in self.inputs.items()}
                for c in range(ncore)]


def bf16(x):
    return np.asarray(x).astype(ml_dtypes.bfloat16)


def lightning_tables():
    start = 2.0 ** (-2.0 ** (-(math.log2(H0) - 3)))
    slopes = np.array([start ** (i + 1) for i in range(H0)], dtype=np.float64)
    lam = np.exp(-slopes)
    pos = np.arange(CH, dtype=np.float64)
    dq = lam[:, None] ** (pos + 1.0)
    dk = lam[:, None] ** (CH - 1.0 - pos)
    diff = pos[:, None] - pos[None, :]
    dmask = np.where(diff >= 0, lam[:, None, None] ** np.maximum(diff, 0.0), 0.0)
    lam_c = lam ** CH
    return lam.astype(np.float32), dq.astype(np.float32), dk.astype(np.float32), \
        dmask.astype(np.float32), lam_c.astype(np.float32)


def lightning_coef(core):
    lam, _, _, _, lam_c = lightning_tables()
    rho = core % 4
    base = (core // 4) * 4
    coef = np.zeros((NCORE, H0), np.float32)
    lam8 = lam_c.astype(np.float64) ** NCH
    for rp in range(rho):
        coef[base + rp] = (lam8 ** (rho - 1 - rp)).astype(np.float32)
    return np.broadcast_to(coef.reshape(1, NCORE * H0), (128, NCORE * H0)).copy()


class MK:

    def __init__(self, nc, fused=True):
        self.nc = nc
        self.ctx = Ctx(nc)
        self.p = self.ctx.p
        self.fused = fused
        self.bf = Pool(nc, "poolbf", 57 * 1024 + 512, BF16)
        self.f32 = Pool(nc, "poolf32", 22 * 1024 + 512, F32)
        self.PA = nc.alloc_psum_tensor("PA", [128, 512], F32)
        self.PB = nc.alloc_psum_tensor("PB", [128, 512], F32)
        self.PC = nc.alloc_psum_tensor("PC", [128, 512], F32)
        self.PD = nc.alloc_psum_tensor("PD", [128, 512], F32)
        self.PE_ = nc.alloc_psum_tensor("PE_", [128, 512], F32)
        self.PF = nc.alloc_psum_tensor("PF", [128, 512], F32)
        self.PT0 = nc.alloc_psum_tensor("PT0", [128, 1024], BF16)
        self.PT1 = nc.alloc_psum_tensor("PT1", [128, 1024], BF16)
        self.consts = {}

    def const(self, name, arr, dtype=None):
        if name in self.consts:
            return self.consts[name]
        arr = np.asarray(arr)
        if dtype == BF16:
            arr = bf16(arr)
        npdt = arr.dtype
        dt = BF16 if npdt == ml_dtypes.bfloat16 else (F32 if npdt == np.float32 else I32)
        assert arr.ndim == 2 and arr.shape[0] <= 128
        ap = self.ctx.inp("c_" + name, arr.shape, dt, lambda c, a=arr: a)
        pool = self.bf if dt == BF16 else self.f32
        v = pool.alloc(arr.shape[1], "c_" + name)
        rows, cols = arr.shape
        self.p._add("gpsimd",
                    lambda eng, ap=ap, v=v, rows=rows, cols=cols: eng.dma_start(
                        out=v.ap(slice(0, rows), slice(0, cols)), in_=ap),
                    r=[], w=[v.res()], kind="dma", tag="const_" + name)
        self.consts[name] = v
        return v

    def ident(self):
        return self.const("ident", np.eye(128, dtype=np.float32), BF16)

    def free_const(self, name):
        if name in self.consts:
            self.consts.pop(name).free()

    def mm(self, out_ap, lhsT_ap, rhs_ap, r, w, start=True, stop=True, tag="mm"):
        return self.p._add("tensor",
                           lambda eng: eng.matmul(out_ap, lhsT_ap, rhs_ap,
                                                  start=start, stop=stop),
                           r, w, "op", tag)

    def tr(self, out_ap, in_ap, r, w, tag="tr", np_=128):
        idv = self.ident()
        return self.p._add("tensor",
                           lambda eng, i=idv: eng.transpose(
                               out_ap, in_ap, i[0:np_, 0:np_]),
                           list(r) + [idv.res()], w, "op", tag)


def build_ln_xnT(mk, x_rows_ap, xnT, eps=1e-5, tag="ln0", x_dram_res=None,
                 loader=None):
    p, nc = mk.p, mk.nc
    xt = mk.f32.alloc(2048, tag + "_xt")
    xt2 = mk.f32.alloc(2048, tag + "_xt2")
    stats = mk.f32.alloc(16, tag + "_stats")
    sq = mk.f32.alloc(512, tag + "_sq")
    xn_tm = mk.bf.alloc(2048, tag + "_xntm")
    xr = x_dram_res or (whole(x_rows_ap.tensor) if x_rows_ap is not None else None)
    for i in range(8):
        buf = xt if i % 2 == 0 else xt2
        if loader is not None:
            loader(i, buf)
        else:
            p._add("sync", lambda eng, a=x_rows_ap, i=i, b=buf: eng.dma_start(
                out=b[:, :], in_=a[i * 128:(i + 1) * 128, :]),
                r=[xr], w=[buf.res()], kind="dma", tag=tag + "_ldx")
        p._add("vector", lambda eng, b=buf, s=stats: eng.reduce_sum(
            s[:, 0:1], b[:, :], axis=AX.X, negate=True),
            r=[buf.res()], w=[stats.res(slice(0, 1))], kind="op", tag=tag)
        p._add("vector", lambda eng, s=stats: eng.tensor_scalar_mul(
            s[:, 0:1], s[:, 0:1], 1.0 / D),
            r=[stats.res(slice(0, 1))], w=[stats.res(slice(0, 1))], kind="op", tag=tag)
        for ch in range(4):
            p._add("scalar", lambda eng, b=buf, s=stats, q=sq, ch=ch: eng.activation(
                q[:, :], b[:, ch * 512:(ch + 1) * 512], AF.Square,
                accum_out=s[:, 1 + ch:2 + ch]),
                r=[buf.res(slice(ch * 512, (ch + 1) * 512))],
                w=[sq.res(), stats.res(slice(1 + ch, 2 + ch))], kind="op", tag=tag)
        p._add("vector", lambda eng, s=stats: eng.tensor_tensor(
            s[:, 1:3], s[:, 1:3], s[:, 3:5], ALU.add),
            r=[stats.res(slice(1, 5))], w=[stats.res(slice(1, 3))], kind="op", tag=tag)
        p._add("vector", lambda eng, s=stats: eng.tensor_tensor(
            s[:, 1:2], s[:, 1:2], s[:, 2:3], ALU.add),
            r=[stats.res(slice(1, 3))], w=[stats.res(slice(1, 2))], kind="op", tag=tag)
        p._add("vector", lambda eng, s=stats: eng.tensor_tensor(
            s[:, 5:6], s[:, 0:1], s[:, 0:1], ALU.mult),
            r=[stats.res(slice(0, 1))], w=[stats.res(slice(5, 6))], kind="op", tag=tag)
        p._add("vector", lambda eng, s=stats: eng.tensor_scalar(
            s[:, 6:7], s[:, 1:2], 1.0 / D, float(eps), ALU.mult, ALU.add),
            r=[stats.res(slice(1, 2))], w=[stats.res(slice(6, 7))], kind="op", tag=tag)
        p._add("vector", lambda eng, s=stats: eng.tensor_tensor(
            s[:, 6:7], s[:, 6:7], s[:, 5:6], ALU.subtract),
            r=[stats.res(slice(5, 7))], w=[stats.res(slice(6, 7))], kind="op", tag=tag)
        p._add("scalar", lambda eng, s=stats: eng.activation(
            s[:, 7:8], s[:, 6:7], AF.Sqrt),
            r=[stats.res(slice(6, 7))], w=[stats.res(slice(7, 8))], kind="op", tag=tag)
        p._add("vector", lambda eng, s=stats: eng.reciprocal(
            s[:, 8:9], s[:, 7:8]),
            r=[stats.res(slice(7, 8))], w=[stats.res(slice(8, 9))], kind="op", tag=tag)
        p._add("vector", lambda eng, s=stats: eng.tensor_tensor(
            s[:, 9:10], s[:, 0:1], s[:, 8:9], ALU.mult),
            r=[stats.res(slice(0, 1)), stats.res(slice(8, 9))],
            w=[stats.res(slice(9, 10))], kind="op", tag=tag)
        p._add("scalar", lambda eng, b=buf, s=stats, o=xn_tm: eng.activation(
            o[:, :], b[:, :], AF.Identity, scale=s[:, 8:9], bias=s[:, 9:10]),
            r=[buf.res(), stats.res(slice(8, 10))], w=[xn_tm.res()], kind="op", tag=tag)
        for k in range(DB):
            pt = mk.PT0 if (k % 2 == 0) else mk.PT1
            psl = slice((k % 4) * 256, (k % 4) * 256 + 128)
            mk.tr(pt[:, 0:128], xn_tm[:, k * 128:(k + 1) * 128],
                  r=[xn_tm.res(slice(k * 128, (k + 1) * 128))],
                  w=[whole(pt)], tag=tag + "_tr")
            p._add("vector", lambda eng, pt=pt, k=k, i=i, o=xnT: eng.tensor_copy(
                o[:, k * 1024 + i * 128: k * 1024 + (i + 1) * 128], pt[:, 0:128]),
                r=[whole(pt)], w=[xnT.res(slice(k * 1024 + i * 128, k * 1024 + (i + 1) * 128))],
                kind="op", tag=tag + "_trevac")
    for t in (xt, xt2, stats, sq, xn_tm):
        t.free()


def build_lightning(mk, xnT, wqkvT_ap, bqkv_c, attnT, cc_rg, tag="lg"):
    p, nc = mk.p, mk.nc
    lam, dq, dk, dmask, lam_c = lightning_tables()
    dq_arr = np.zeros((128, H0 * CH), np.float32)
    for h in range(H0):
        dq_arr[:, h * CH:(h + 1) * CH] = dq[h][None, :]
    dq_t = mk.const("lg_dq", dq_arr, BF16)
    dk_t = mk.const("lg_dk", dk.T, F32)
    dmT = mk.const("lg_dmT", np.ascontiguousarray(
        dmask.transpose(2, 0, 1).reshape(CH, H0 * CH)), BF16)
    coef_ap = mk.ctx.inp("lg_coef", (128, NCORE * H0), F32, lightning_coef)
    coefv = mk.f32.alloc(NCORE * H0, "lg_coef")
    p._add("gpsimd", lambda eng: eng.dma_start(out=coefv[:, :], in_=coef_ap),
           r=[], w=[coefv.res()], kind="dma", tag=tag)

    import os
    HPP = int(os.environ.get('HPP', '4'))
    cc_in = mk.ctx.dram("lg_ccin", [HPP * 128, 128], F32)
    cc_out = mk.ctx.dram("lg_ccout", [NCORE * HPP * 128, 128], F32)

    kT = mk.bf.alloc(HPP * 1024, "lg_kT")
    vb = mk.bf.alloc(HPP * 1024, "lg_vb")
    S = mk.f32.alloc(128, "lg_S")
    Sbf = mk.bf.alloc(128, "lg_Sbf")
    A_sb = mk.f32.alloc(128, "lg_A")
    wbuf = [mk.bf.alloc(128, f"lg_w{i}") for i in range(4)]
    scT = mk.bf.alloc(128, "lg_scT")
    qT = mk.bf.alloc(1024, "lg_qT")
    qs = mk.bf.alloc(128, "lg_qs")
    ktmp = mk.bf.alloc(128, "lg_ktmp")
    Ae = mk.f32.alloc(128, "lg_Ae")

    def project(cols, bias_idx, dst_view, dst_c0, tag2):
        for kb in range(DB):
            wb = wbuf[kb % 4]
            p._add("gpsimd", lambda eng, kb=kb, wb=wb: eng.dma_start(
                out=wb[:, :], in_=wqkvT_ap[kb * 128:(kb + 1) * 128, cols:cols + 128]),
                r=[], w=[wb.res()], kind="dma", tag=tag2 + "_wdma")
            for th in range(2):
                ps = mk.PA if th == 0 else mk.PB
                mk.mm(ps[:, :], wb[:, :],
                      xnT[:, kb * 1024 + th * 512: kb * 1024 + (th + 1) * 512],
                      r=[wb.res(), xnT.res(slice(kb * 1024 + th * 512,
                                                 kb * 1024 + (th + 1) * 512))],
                      w=[whole(ps)], start=(kb == 0), stop=(kb == DB - 1), tag=tag2)
        for th in range(2):
            ps = mk.PA if th == 0 else mk.PB
            p._add("scalar", lambda eng, ps=ps, th=th: eng.activation(
                dst_view[:, dst_c0 + th * 512: dst_c0 + (th + 1) * 512], ps[:, :],
                AF.Identity, bias=bqkv_c[:, bias_idx:bias_idx + 1]),
                r=[whole(ps), bqkv_c.res(slice(bias_idx, bias_idx + 1))],
                w=[dst_view.res(slice(dst_c0 + th * 512, dst_c0 + (th + 1) * 512))],
                kind="op", tag=tag2 + "_evac")

    def kv_chunk(hl, h, j, dst_psum):
        jc = slice(hl * 1024 + j * 128, hl * 1024 + (j + 1) * 128)
        pt = mk.PT0 if j % 2 == 0 else mk.PT1
        mk.tr(pt[:, 0:128], kT[:, jc], r=[kT.res(jc)], w=[whole(pt)],
              tag=tag + "_ktr")
        p._add("scalar", lambda eng, pt=pt, h=h: eng.activation(
            ktmp[:, :], pt[:, 0:128], AF.Copy, scale=dk_t[:, h:h + 1]),
            r=[whole(pt), dk_t.res(slice(h, h + 1))], w=[ktmp.res()],
            kind="op", tag=tag + "_ks")
        mk.mm(dst_psum[:, 0:128], ktmp[:, :], vb[:, jc],
              r=[ktmp.res(), vb.res(jc)], w=[whole(dst_psum)], tag=tag + "_kv")

    for pas in range(H0 // HPP):
        h0 = pas * HPP
        for hl in range(HPP):
            h = h0 + hl
            project(D + h * 128, DB + h, kT, hl * 1024, f"{tag}_k{h}")
            project(2 * D + h * 128, 2 * DB + h, vb, hl * 1024, f"{tag}_v{h}")
            for j in range(NCH):
                jc = slice(hl * 1024 + j * 128, hl * 1024 + (j + 1) * 128)
                pt = mk.PT0 if j % 2 == 0 else mk.PT1
                mk.tr(pt[:, 0:128], vb[:, jc], r=[vb.res(jc)], w=[whole(pt)],
                      tag=tag + "_vtr")
                p._add("vector", lambda eng, pt=pt, jc=jc: eng.tensor_copy(
                    vb[:, jc], pt[:, 0:128]),
                    r=[whole(pt)], w=[vb.res(jc)], kind="op", tag=tag + "_vte")
            p._add("vector", lambda eng: eng.memset(S[:, :], 0.0),
                   r=[], w=[S.res()], kind="op", tag=tag)
            for j in range(NCH):
                kv_chunk(hl, h, j, mk.PC)
                p._add("vector", lambda eng, h=h: eng.scalar_tensor_tensor(
                    S[:, :], S[:, :], float(lam_c[h]), mk.PC[:, 0:128],
                    ALU.mult, ALU.add),
                    r=[S.res(), whole(mk.PC)], w=[S.res()], kind="op",
                    tag=tag + "_scan")
            p._add("vector", lambda eng: eng.tensor_copy(A_sb[:, :], S[:, :]),
                   r=[S.res()], w=[A_sb.res()], kind="op", tag=tag)
            p._add("sync", lambda eng, hl=hl: eng.dma_start(
                out=cc_in.tensor.ap()[hl * 128:(hl + 1) * 128, :], in_=A_sb[:, :]),
                r=[A_sb.res()], w=[whole(cc_in.tensor)], kind="dma",
                tag=tag + "_Aout")

        p._add("gpsimd", lambda eng: eng.collective_compute(
            "AllGather", ALU.bypass, replica_groups=cc_rg,
            ins=[cc_in.tensor.ap().opt()], outs=[cc_out.tensor.ap().opt()]),
            r=[whole(cc_in.tensor)], w=[whole(cc_out.tensor)], kind="cc",
            tag=tag + "_cc")

        for hl in range(HPP):
            h = h0 + hl
            project(h * 128, h, qT, 0, f"{tag}_q{h}")
            p._add("vector", lambda eng: eng.memset(S[:, :], 0.0),
                   r=[], w=[S.res()], kind="op", tag=tag)
            for r_ in range(NCORE):
                p._add("sync", lambda eng, r_=r_, hl=hl: eng.dma_start(
                    out=Ae[:, :],
                    in_=cc_out.tensor.ap()[(r_ * HPP + hl) * 128:
                                           (r_ * HPP + hl + 1) * 128, :]),
                    r=[whole(cc_out.tensor)], w=[Ae.res()], kind="dma",
                    tag=tag + "_Ald")
                p._add("vector", lambda eng, r_=r_, h=h: eng.scalar_tensor_tensor(
                    S[:, :], Ae[:, :], coefv[:, r_ * H0 + h:r_ * H0 + h + 1],
                    S[:, :], ALU.mult, ALU.add),
                    r=[Ae.res(), coefv.res(slice(r_ * H0 + h, r_ * H0 + h + 1)),
                       S.res()],
                    w=[S.res()], kind="op", tag=tag + "_sent")
            for j in range(NCH):
                jc = slice(hl * 1024 + j * 128, hl * 1024 + (j + 1) * 128)
                qc = slice(j * 128, (j + 1) * 128)
                oc = slice(h * 1024 + j * 128, h * 1024 + (j + 1) * 128)
                p._add("scalar", lambda eng: eng.activation(
                    Sbf[:, :], S[:, :], AF.Copy),
                    r=[S.res()], w=[Sbf.res()], kind="op", tag=tag + "_sbf")
                p._add("vector", lambda eng, h=h, qc=qc: eng.tensor_tensor(
                    qs[:, :], qT[:, qc], dq_t[:, h * 128:(h + 1) * 128], ALU.mult),
                    r=[qT.res(qc), dq_t.res(slice(h * 128, (h + 1) * 128))],
                    w=[qs.res()], kind="op", tag=tag + "_qs")
                mk.mm(mk.PD[:, 0:128], kT[:, jc], qT[:, qc],
                      r=[kT.res(jc), qT.res(qc)], w=[whole(mk.PD)],
                      tag=tag + "_sc")
                p._add("vector", lambda eng, h=h: eng.tensor_tensor(
                    scT[:, :], mk.PD[:, 0:128], dmT[:, h * 128:(h + 1) * 128],
                    ALU.mult),
                    r=[whole(mk.PD), dmT.res(slice(h * 128, (h + 1) * 128))],
                    w=[scT.res()], kind="op", tag=tag + "_msk")
                mk.mm(mk.PE_[:, 0:128], vb[:, jc], scT[:, :],
                      r=[vb.res(jc), scT.res()], w=[whole(mk.PE_)],
                      start=True, stop=False, tag=tag + "_intra")
                mk.mm(mk.PE_[:, 0:128], Sbf[:, :], qs[:, :],
                      r=[Sbf.res(), qs.res()], w=[whole(mk.PE_)],
                      start=False, stop=True, tag=tag + "_inter")
                p._add("vector", lambda eng, oc=oc: eng.tensor_copy(
                    attnT[:, oc], mk.PE_[:, 0:128]),
                    r=[whole(mk.PE_)], w=[attnT.res(oc)], kind="op",
                    tag=tag + "_oevac")
                kv_chunk(hl, h, j, mk.PC)
                p._add("vector", lambda eng, h=h: eng.scalar_tensor_tensor(
                    S[:, :], S[:, :], float(lam_c[h]), mk.PC[:, 0:128],
                    ALU.mult, ALU.add),
                    r=[S.res(), whole(mk.PC)], w=[S.res()], kind="op",
                    tag=tag + "_scan2")
    if not os.environ.get("KEEPBUFS"):
        for t in (kT, vb, S, Sbf, A_sb, scT, qT, qs, ktmp, Ae, coefv):
            t.free()
        for wbv in wbuf:
            wbv.free()



def load_wcol(mk, dram_ap, col0, width, kblocks, dst, queue="gpsimd", tag="wld",
              src_res=None, split=1):
    per = (kblocks + split - 1) // split
    for s0 in range(0, kblocks, per):
        s1 = min(kblocks, s0 + per)
        src = dram_ap[s0 * 128:s1 * 128, col0:col0 + width].rearrange(
            "(kb p) f -> p kb f", kb=s1 - s0)
        dstap = dst[:, s0 * width:s1 * width].rearrange(
            "p (kb f) -> p kb f", kb=s1 - s0)
        mk.p._add(queue, lambda eng, src=src, dstap=dstap: eng.dma_start(
            out=dstap, in_=src),
            r=[src_res or whole(dram_ap.tensor)],
            w=[dst.res(slice(s0 * width, s1 * width))], kind="dma", tag=tag)


def trf32(mk, out_ap, in_ap, r, w, tag="trf"):
    idv = mk.const("identf32", np.eye(128, dtype=np.float32))
    return mk.p._add("tensor",
                     lambda eng, i=idv: eng.transpose(out_ap, in_ap, i[0:128, :]),
                     list(r) + [idv.res()], w, "op", tag)


def ffn_expert(mk, bufs, gate_ap, up_ap, down_ap, fblocks, cb_view, dst_update,
               gcol0, tag2, src_res_g=None, src_res_u=None, src_res_d=None,
               ucol0=None):
    if ucol0 is None:
        ucol0 = gcol0
    for f0 in range(0, fblocks, FB0):
        fpass = min(FB0, fblocks - f0)
        _ffn_pass(mk, bufs, gate_ap, up_ap, down_ap, f0, fpass, fblocks,
                  cb_view, dst_update, gcol0, ucol0, tag2, src_res_g,
                  src_res_u, src_res_d)

def _ffn_pass(mk, bufs, gate_ap, up_ap, down_ap, f0, fpass, fblocks, cb_view,
              dst_update, gcol0, ucol0, tag2, src_res_g, src_res_u,
              src_res_d):
    p = mk.p
    xnT, h_all, sg, t1, wg, wu, wd = (bufs["xnT"], bufs["h"], bufs["sg"],
                                      bufs["t1"], bufs["wg"], bufs["wu"],
                                      bufs["wd"])
    for fb in range(fpass):
        wgv, wuv = wg[fb % 2], wu[fb % 2]
        load_wcol(mk, gate_ap, gcol0 + (f0 + fb) * 128, 128, DB, wgv,
                  tag=tag2 + "_g", src_res=src_res_g)
        load_wcol(mk, up_ap, ucol0 + (f0 + fb) * 128, 128, DB, wuv,
                  tag=tag2 + "_u", src_res=src_res_u)
        for th in range(2):
            pg = mk.PC if th == 0 else mk.PD
            pu = mk.PE_ if th == 0 else mk.PF
            for kb in range(DB):
                rc = slice(kb * 1024 + th * 512, kb * 1024 + (th + 1) * 512)
                mk.mm(pg[:, :], wgv[:, kb * 128:(kb + 1) * 128], xnT[:, rc],
                      r=[wgv.res(slice(kb * 128, (kb + 1) * 128)), xnT.res(rc)],
                      w=[whole(pg)], start=(kb == 0), stop=(kb == DB - 1),
                      tag=tag2 + "_gm")
                mk.mm(pu[:, :], wuv[:, kb * 128:(kb + 1) * 128], xnT[:, rc],
                      r=[wuv.res(slice(kb * 128, (kb + 1) * 128)), xnT.res(rc)],
                      w=[whole(pu)], start=(kb == 0), stop=(kb == DB - 1),
                      tag=tag2 + "_um")
            p._add("scalar", lambda eng, pg=pg: eng.activation(
                sg[:, :], pg[:, :], AF.Silu),
                r=[whole(pg)], w=[sg.res()], kind="op", tag=tag2 + "_silu")
            hc = slice(fb * 1024 + th * 512, fb * 1024 + (th + 1) * 512)
            if cb_view is not None:
                p._add("vector", lambda eng, pu=pu: eng.tensor_tensor(
                    t1[:, :], pu[:, :], sg[:, :], ALU.mult),
                    r=[whole(pu), sg.res()], w=[t1.res()], kind="op",
                    tag=tag2 + "_hu")
                p._add("vector", lambda eng, hc=hc, th=th: eng.tensor_tensor(
                    h_all[:, hc], t1[:, :],
                    cb_view[:, th * 512:(th + 1) * 512], ALU.mult),
                    r=[t1.res(), cb_view.res(slice(th * 512, (th + 1) * 512))],
                    w=[h_all.res(hc)], kind="op", tag=tag2 + "_hc")
            else:
                p._add("vector", lambda eng, pu=pu, hc=hc: eng.tensor_tensor(
                    h_all[:, hc], pu[:, :], sg[:, :], ALU.mult),
                    r=[whole(pu), sg.res()], w=[h_all.res(hc)], kind="op",
                    tag=tag2 + "_hu")
    for db in range(DB):
        wdv = wd[db % 2]
        src = down_ap[f0 * 128:(f0 + fpass) * 128, db * 128:(db + 1) * 128]\
            .rearrange("(kb p) f -> p kb f", kb=fpass)
        dstap = wdv[:, 0:fpass * 128].rearrange("p (kb f) -> p kb f", kb=fpass)
        mk.p._add("gpsimd", lambda eng, src=src, dstap=dstap: eng.dma_start(
            out=dstap, in_=src),
            r=[src_res_d or whole(down_ap.tensor)], w=[wdv.res()],
            kind="dma", tag=tag2 + "_d")
        for th in range(2):
            pd_ = mk.PC if th == 0 else mk.PD
            for fb in range(fpass):
                rc = slice(fb * 1024 + th * 512, fb * 1024 + (th + 1) * 512)
                mk.mm(pd_[:, :], wdv[:, fb * 128:(fb + 1) * 128], h_all[:, rc],
                      r=[wdv.res(slice(fb * 128, (fb + 1) * 128)),
                         h_all.res(rc)],
                      w=[whole(pd_)], start=(fb == 0), stop=(fb == fpass - 1),
                      tag=tag2 + "_dm")
            dc = slice(db * 1024 + th * 512, db * 1024 + (th + 1) * 512)
            p._add("vector", lambda eng, pd_=pd_, dc=dc: eng.tensor_tensor(
                dst_update[:, dc], dst_update[:, dc], pd_[:, :], ALU.add),
                r=[dst_update.res(dc), whole(pd_)], w=[dst_update.res(dc)],
                kind="op", tag=tag2 + "_dacc")



def build_l0_finish(mk, xnT, attnT, x_rows_ap, x_acc, woutT_ap, bout_c,
                    wrouter_ap, weg_ap, weu_ap, wed_ap, wsg_ap, wsu_ap, wsd_ap,
                    tag="l0f", stages=("init", "outproj", "router", "moe", "shared")):
    p, nc = mk.p, mk.nc
    import os

    xt = mk.f32.alloc(1024, tag + "_xt")
    xt2 = mk.f32.alloc(1024, tag + "_xt2")
    for kb in range(DB):
        buf = xt if kb % 2 == 0 else xt2
        p._add("sync", lambda eng, kb=kb, b=buf: eng.dma_start(
            out=b[:, :], in_=x_rows_ap[kb * 128:(kb + 1) * 128, :]),
            r=[whole(x_rows_ap.tensor)], w=[buf.res()], kind="dma",
            tag=tag + "_ldx")
        dc = slice(kb * 1024, (kb + 1) * 1024)
        p._add("scalar", lambda eng, b=buf, dc=dc, kb=kb: eng.activation(
            x_acc[:, dc], b[:, :], AF.Identity, bias=bout_c[:, kb:kb + 1]),
            r=[buf.res(), bout_c.res(slice(kb, kb + 1))], w=[x_acc.res(dc)],
            kind="op", tag=tag + "_xev")
    xt.free(); xt2.free()

    wo = [mk.bf.alloc(DB * 128, tag + f"_wo{i}") for i in range(2)]
    lvl = int(os.environ.get("OUTPROJ_LVL", "3"))
    if os.environ.get("EXTRA_DMA"):
        spare = mk.bf.alloc(2048, "spare_nofree")
        mode_ = os.environ["EXTRA_DMA"]
        if mode_ == "memset":
            p._add("gpsimd", lambda eng: eng.memset(spare[:, 0:128], 0.0),
                   r=[], w=[spare.res(slice(0, 128))], kind="op",
                   tag=tag + "_extramemset")
        elif mode_ == "sync":
            p._add("sync", lambda eng: eng.dma_start(
                out=spare[:, 0:128], in_=woutT_ap[0:128, 0:128]),
                r=[whole(woutT_ap.tensor)], w=[spare.res(slice(0, 128))],
                kind="dma", tag=tag + "_extrasync")
        elif mode_ == "plain":
            p._add("gpsimd", lambda eng: eng.dma_start(
                out=spare[:, 0:128], in_=woutT_ap[0:128, 0:128]),
                r=[whole(woutT_ap.tensor)], w=[spare.res(slice(0, 128))],
                kind="dma", tag=tag + "_extraplain")
        lvl = 0
    wsplit = int(os.environ.get("WSPLIT", "1"))
    if "outproj" in stages:
        for fb in range(DB):
            if lvl < 1:
                break
            wv = wo[fb % 2]
            load_wcol(mk, woutT_ap, fb * 128, 128, DB, wv, tag=tag + "_wodma", split=wsplit)
            if lvl < 2:
                continue
            for th in range(2):
                ps = mk.PC if th == 0 else mk.PD
                for ab in range(DB):
                    rc = slice(ab * 1024 + th * 512, ab * 1024 + (th + 1) * 512)
                    mk.mm(ps[:, :], wv[:, ab * 128:(ab + 1) * 128], attnT[:, rc],
                          r=[wv.res(slice(ab * 128, (ab + 1) * 128)), attnT.res(rc)],
                          w=[whole(ps)], start=(ab == 0), stop=(ab == DB - 1),
                          tag=tag + "_wo")
                dc = slice(fb * 1024 + th * 512, fb * 1024 + (th + 1) * 512)
                if lvl >= 3:
                    p._add("vector", lambda eng, ps=ps, dc=dc: eng.tensor_tensor(
                        x_acc[:, dc], x_acc[:, dc], ps[:, :], ALU.add),
                        r=[x_acc.res(dc), whole(ps)], w=[x_acc.res(dc)], kind="op",
                        tag=tag + "_woacc")
    for w_ in wo:
        w_.free()
    attnT.free()
    if "router" not in stages:
        return

    wr = mk.bf.alloc(DB * 8, tag + "_wr")
    load_wcol(mk, wrouter_ap, 0, E0, DB, wr, tag=tag + "_wrdma")
    logit = mk.f32.alloc(16, tag + "_lg")
    prob = mk.f32.alloc(8, tag + "_pr")
    rk = mk.f32.alloc(8, tag + "_rk")
    cmb = mk.bf.alloc(8, tag + "_cmb")
    cT_all = mk.bf.alloc(1024, tag + "_cT")
    for i in range(8):
        ps = mk.PE_
        for kb in range(DB):
            mk.mm(ps[:, 0:E0], xnT[:, kb * 1024 + i * 128: kb * 1024 + (i + 1) * 128],
                  wr[:, kb * 8:(kb + 1) * 8],
                  r=[xnT.res(slice(kb * 1024 + i * 128, kb * 1024 + (i + 1) * 128)),
                     wr.res(slice(kb * 8, (kb + 1) * 8))],
                  w=[whole(ps)], start=(kb == 0), stop=(kb == DB - 1),
                  tag=tag + "_rt")
        p._add("vector", lambda eng, ps=ps: eng.reduce_max(
            logit[:, 8:9], ps[:, 0:E0], axis=AX.X, negate=True),
            r=[whole(ps)], w=[logit.res(slice(8, 9))], kind="op", tag=tag)
        p._add("scalar", lambda eng, ps=ps: eng.activation(
            logit[:, 0:E0], ps[:, 0:E0], AF.Exp, bias=logit[:, 8:9],
            accum_out=logit[:, 9:10]),
            r=[whole(ps), logit.res(slice(8, 9))],
            w=[logit.res(slice(0, E0)), logit.res(slice(9, 10))], kind="op",
            tag=tag + "_exp")
        p._add("vector", lambda eng: eng.reciprocal(
            logit[:, 10:11], logit[:, 9:10]),
            r=[logit.res(slice(9, 10))], w=[logit.res(slice(10, 11))],
            kind="op", tag=tag)
        p._add("vector", lambda eng: eng.tensor_scalar(
            prob[:, 0:E0], logit[:, 0:E0], logit[:, 10:11], None, ALU.mult),
            r=[logit.res(slice(0, E0)), logit.res(slice(10, 11))],
            w=[prob.res()], kind="op", tag=tag)
        for e in range(E0):
            p._add("vector", lambda eng, e=e: eng.tensor_scalar(
                rk[:, 0:E0], prob[:, 0:E0], prob[:, e:e + 1], None, ALU.is_gt),
                r=[prob.res()], w=[rk.res(slice(0, E0))], kind="op", tag=tag)
            p._add("vector", lambda eng: eng.reduce_sum(
                logit[:, 11:12], rk[:, 0:E0], axis=AX.X),
                r=[rk.res(slice(0, E0))], w=[logit.res(slice(11, 12))],
                kind="op", tag=tag)
            if e > 0:
                p._add("vector", lambda eng, e=e: eng.tensor_scalar(
                    rk[:, 0:e], prob[:, 0:e], prob[:, e:e + 1], None,
                    ALU.is_equal),
                    r=[prob.res()], w=[rk.res(slice(0, e))], kind="op", tag=tag)
                p._add("vector", lambda eng, e=e: eng.reduce_sum(
                    logit[:, 12:13], rk[:, 0:e], axis=AX.X),
                    r=[rk.res(slice(0, e))], w=[logit.res(slice(12, 13))],
                    kind="op", tag=tag)
                p._add("vector", lambda eng: eng.tensor_tensor(
                    logit[:, 11:12], logit[:, 11:12], logit[:, 12:13], ALU.add),
                    r=[logit.res(slice(11, 13))], w=[logit.res(slice(11, 12))],
                    kind="op", tag=tag)
            p._add("vector", lambda eng: eng.tensor_scalar(
                logit[:, 13:14], logit[:, 11:12], 2.0, None, ALU.is_lt),
                r=[logit.res(slice(11, 12))], w=[logit.res(slice(13, 14))],
                kind="op", tag=tag)
            p._add("vector", lambda eng, e=e: eng.tensor_scalar(
                cmb[:, e:e + 1], prob[:, e:e + 1], logit[:, 13:14], None,
                ALU.mult),
                r=[prob.res(), logit.res(slice(13, 14))],
                w=[cmb.res(slice(e, e + 1))], kind="op", tag=tag)
        pt = mk.PT0 if i % 2 == 0 else mk.PT1
        mk.tr(pt[0:E0, 0:128], cmb[:, 0:E0], r=[cmb.res()], w=[whole(pt)],
              tag=tag + "_ctr")
        p._add("vector", lambda eng, pt=pt, i=i: eng.tensor_copy(
            cT_all.ap(slice(0, E0), slice(i * 128, (i + 1) * 128)),
            pt[0:E0, 0:128]),
            r=[whole(pt)], w=[cT_all.res(slice(i * 128, (i + 1) * 128))],
            kind="op", tag=tag + "_cte")
    wr.free(); logit.free(); prob.free(); rk.free()

    sel = np.zeros((E0, E0 * 128), np.float32)
    for e in range(E0):
        sel[e, e * 128:(e + 1) * 128] = 1.0
    selc = mk.const("l0_sel", sel, BF16)

    h_all = mk.bf.alloc(FB0 * 1024, tag + "_h")
    cb = mk.bf.alloc(1024, tag + "_cb")
    sg = mk.bf.alloc(512, tag + "_sg")
    t1 = mk.bf.alloc(512, tag + "_t1")
    wg = [mk.bf.alloc(DB * 128, tag + f"_wg{i}") for i in range(2)]
    wu = [mk.bf.alloc(DB * 128, tag + f"_wu{i}") for i in range(2)]
    wd = [mk.bf.alloc(FB0 * 128, tag + f"_wd{i}") for i in range(2)]
    ffnbufs = {"xnT": xnT, "h": h_all, "sg": sg, "t1": t1, "wg": wg, "wu": wu,
               "wd": wd}

    for e in (range(E0) if "moe" in stages else []):
        for th in range(2):
            ps = mk.PF
            mk.mm(ps[:, :], selc[0:E0, e * 128:(e + 1) * 128],
                  cT_all[0:E0, th * 512:(th + 1) * 512],
                  r=[selc.res(slice(e * 128, (e + 1) * 128)),
                     cT_all.res(slice(th * 512, (th + 1) * 512))],
                  w=[whole(ps)], tag=tag + "_cbm")
            p._add("vector", lambda eng, ps=ps, th=th: eng.tensor_copy(
                cb[:, th * 512:(th + 1) * 512], ps[:, :]),
                r=[whole(ps)], w=[cb.res(slice(th * 512, (th + 1) * 512))],
                kind="op", tag=tag + "_cbe")
        ffn_expert(mk, ffnbufs, weg_ap[e], weu_ap[e], wed_ap[e], FB0, cb,
                   x_acc, 0, f"{tag}_e{e}",
                   src_res_g=whole(weg_ap[e].tensor),
                   src_res_u=whole(weu_ap[e].tensor),
                   src_res_d=whole(wed_ap[e].tensor))

    if "shared" in stages:
        ffn_expert(mk, ffnbufs, wsg_ap, wsu_ap, wsd_ap, SB0, None, x_acc, 0,
                   tag + "_sh")

    for t in (h_all, cb, sg, t1, cT_all):
        t.free()
    for lst in (wg, wu, wd):
        for w_ in lst:
            w_.free()




def acc_loader(mk, x_acc, tag="lnacc"):
    def loader(i, buf):
        for kb in range(DB):
            ps = mk.PA if kb % 2 == 0 else mk.PB
            sc = slice(kb * 1024 + i * 128, kb * 1024 + (i + 1) * 128)
            trf32(mk, ps[:, 0:128], x_acc[:, sc],
                  r=[x_acc.res(sc)], w=[whole(ps)], tag=tag + "_tr")
            mk.p._add("vector", lambda eng, ps=ps, kb=kb, b=buf: eng.tensor_copy(
                b[:, kb * 128:(kb + 1) * 128], ps[:, 0:128]),
                r=[whole(ps)], w=[buf.res(slice(kb * 128, (kb + 1) * 128))],
                kind="op", tag=tag + "_ev")
    return loader


def build_l1_attn(mk, xn1T, x_acc, wqT_ap, wkT_ap, wvT_ap, woT_ap,
                  vemb0_ap, vemb1_ap, tau_ap, kvoff_ap, qtb_ap, cc_rg,
                  tag="l1a"):
    p, nc = mk.p, mk.nc
    SEQ = N
    NT = SEQ // 128

    maskc = mk.const("l1_mask", np.tril(np.ones((128, 128), np.float32), -1), BF16)
    negbig = mk.const("l1_negbig",
                      np.triu(np.full((128, 128), -1e30, np.float32), 0), BF16)
    kwT_lo = mk.bf.alloc(SEQ, tag + "_kwlo")
    kwT_hi = mk.bf.alloc(SEQ, tag + "_kwhi")
    qwT_lo = mk.bf.alloc(SEQ, tag + "_qwlo")
    qwT_hi = mk.bf.alloc(SEQ, tag + "_qwhi")
    s_row = mk.bf.alloc(SEQ, tag + "_srow")
    onesf = mk.const("l1_onesf", np.ones((1, 128), np.float32))

    taut = mk.f32.alloc(8, tag + "_tau")
    p._add("sync", lambda eng: eng.dma_start(out=taut[0:1, 0:1], in_=tau_ap),
           r=[whole(tau_ap.tensor)], w=[taut.res(slice(0, 1))], kind="dma", tag=tag)
    p._add("vector", lambda eng: eng.reciprocal(taut[0:1, 1:2], taut[0:1, 0:1]),
           r=[taut.res(slice(0, 1))], w=[taut.res(slice(1, 2))], kind="op", tag=tag)
    invt = mk.f32.alloc(4, tag + "_invt")
    p._add("tensor", lambda eng: eng.matmul(
        mk.PF[:, 0:1], onesf[0:1, :], taut[0:1, 1:2], start=True, stop=True),
        [onesf.res(), taut.res(slice(1, 2))], [whole(mk.PF)], "op", tag)
    p._add("vector", lambda eng: eng.tensor_copy(invt[:, 0:1], mk.PF[:, 0:1]),
           r=[whole(mk.PF)], w=[invt.res(slice(0, 1))], kind="op", tag=tag)
    p._add("vector", lambda eng: eng.tensor_scalar(
        invt[:, 1:2], invt[:, 0:1], 1.0 / 16.0, None, ALU.mult),
        r=[invt.res(slice(0, 1))], w=[invt.res(slice(1, 2))], kind="op", tag=tag)
    wq = mk.bf.alloc(DB * 128, tag + "_wq")
    wk = mk.bf.alloc(DB * 32, tag + "_wk")
    wv_ = mk.bf.alloc(DB * 32, tag + "_wv")
    load_wcol(mk, wqT_ap, 0, 128, DB, wq, tag=tag + "_wq")
    load_wcol(mk, wkT_ap, 0, 32, DB, wk, tag=tag + "_wk")
    load_wcol(mk, wvT_ap, 0, 32, DB, wv_, tag=tag + "_wv")
    qs_sb = mk.bf.alloc(1024, tag + "_qs")
    kT_sb = mk.bf.alloc(1024, tag + "_kT")
    vT_sb = mk.bf.alloc(1024, tag + "_vT")
    for th in range(2):
        pq = mk.PA if th == 0 else mk.PB
        pk = mk.PC if th == 0 else mk.PD
        pv = mk.PE_ if th == 0 else mk.PF
        for kb in range(DB):
            rc = slice(kb * 1024 + th * 512, kb * 1024 + (th + 1) * 512)
            mk.mm(pq[:, :], wq[:, kb * 128:(kb + 1) * 128], xn1T[:, rc],
                  r=[wq.res(slice(kb * 128, (kb + 1) * 128)), xn1T.res(rc)],
                  w=[whole(pq)], start=(kb == 0), stop=(kb == DB - 1), tag=tag + "_q")
            mk.mm(pk[0:32, :], wk[:, kb * 32:(kb + 1) * 32], xn1T[:, rc],
                  r=[wk.res(slice(kb * 32, (kb + 1) * 32)), xn1T.res(rc)],
                  w=[whole(pk)], start=(kb == 0), stop=(kb == DB - 1), tag=tag + "_k")
            mk.mm(pv[0:32, :], wv_[:, kb * 32:(kb + 1) * 32], xn1T[:, rc],
                  r=[wv_.res(slice(kb * 32, (kb + 1) * 32)), xn1T.res(rc)],
                  w=[whole(pv)], start=(kb == 0), stop=(kb == DB - 1), tag=tag + "_v")
        tc = slice(th * 512, (th + 1) * 512)
        p._add("scalar", lambda eng, pq=pq, tc=tc: eng.activation(
            qs_sb[:, tc], pq[:, :], AF.Tanh, scale=invt[:, 0:1]),
            r=[whole(pq), invt.res(slice(0, 1))], w=[qs_sb.res(tc)],
            kind="op", tag=tag + "_qevac")
        p._add("scalar", lambda eng, pk=pk, tc=tc: eng.activation(
            kT_sb.ap(slice(0, 32), tc), pk[0:32, :], AF.Tanh,
            scale=invt[0:32, 0:1]),
            r=[whole(pk), invt.res(slice(0, 1))], w=[kT_sb.res(tc)],
            kind="op", tag=tag + "_kevac")
        p._add("scalar", lambda eng, pv=pv, tc=tc: eng.activation(
            vT_sb.ap(slice(0, 32), tc), pv[0:32, :], AF.Sigmoid,
            scale=invt[0:32, 0:1]),
            r=[whole(pv), invt.res(slice(0, 1))], w=[vT_sb.res(tc)],
            kind="op", tag=tag + "_vevac")
    p._add("vector", lambda eng: eng.tensor_scalar(
        qs_sb[:, :], qs_sb[:, :], invt[:, 1:2], None, ALU.mult),
        r=[qs_sb.res(), invt.res(slice(1, 2))], w=[qs_sb.res()], kind="op",
        tag=tag + "_qscale")
    for w_ in (wq, wk, wv_):
        w_.free()
    qTpad = mk.bf.alloc(31 + SEQ, tag + "_qTpad")
    rrow = mk.const("l1_rrow", bf16(np.arange(SEQ, dtype=np.float32)[None, :]))
    invi = mk.const("l1_invi", bf16((1.0 / (np.arange(SEQ, dtype=np.float64)
                                            + 1.0)).astype(np.float32)[None, :]))
    p._add("vector", lambda eng: eng.tensor_scalar(
        invi.ap(slice(0, 1)), invi.ap(slice(0, 1)), taut[0:1, 1:2], None,
        ALU.mult),
        r=[invi.res(), taut.res(slice(1, 2))], w=[invi.res()], kind="op", tag=tag)

    cc1_in = mk.ctx.dram(tag + "_cc1in", [TPC, 192], BF16)
    cc1_out = mk.ctx.dram(tag + "_cc1out", [SEQ, 192], BF16)
    stg = mk.bf.alloc(192, tag + "_stg")
    for i in range(8):
        pt = mk.PT0 if i % 2 == 0 else mk.PT1
        mk.tr(pt[:, 0:128], qs_sb[:, i * 128:(i + 1) * 128],
              r=[qs_sb.res(slice(i * 128, (i + 1) * 128))], w=[whole(pt)],
              tag=tag + "_qtr")
        p._add("vector", lambda eng, pt=pt: eng.tensor_copy(
            stg[:, 0:128], pt[:, 0:128]),
            r=[whole(pt)], w=[stg.res(slice(0, 128))], kind="op", tag=tag)
        pt2 = mk.PT1 if i % 2 == 0 else mk.PT0
        mk.tr(pt2[0:128, 0:32], kT_sb.ap(slice(0, 32), slice(i * 128, (i + 1) * 128)),
              r=[kT_sb.res(slice(i * 128, (i + 1) * 128))], w=[whole(pt2)],
              tag=tag + "_ktr", np_=32)
        p._add("vector", lambda eng, pt2=pt2: eng.tensor_copy(
            stg[:, 128:160], pt2[:, 0:32]),
            r=[whole(pt2)], w=[stg.res(slice(128, 160))], kind="op", tag=tag)
        pt3 = mk.PT0 if i % 2 == 0 else mk.PT1
        mk.tr(pt3[0:128, 0:32], vT_sb.ap(slice(0, 32), slice(i * 128, (i + 1) * 128)),
              r=[vT_sb.res(slice(i * 128, (i + 1) * 128))], w=[whole(pt3)],
              tag=tag + "_vtr", np_=32)
        p._add("vector", lambda eng, pt3=pt3: eng.tensor_copy(
            stg[:, 160:192], pt3[:, 0:32]),
            r=[whole(pt3)], w=[stg.res(slice(160, 192))], kind="op", tag=tag)
        p._add("sync", lambda eng, i=i: eng.dma_start(
            out=cc1_in.tensor.ap()[i * 128:(i + 1) * 128, :], in_=stg[:, :]),
            r=[stg.res()], w=[whole(cc1_in.tensor)], kind="dma", tag=tag + "_stg")
    stg.free()
    for t in (qs_sb, kT_sb, vT_sb):
        t.free()
    p._add("gpsimd", lambda eng: eng.collective_compute(
        "AllGather", ALU.bypass, replica_groups=cc_rg,
        ins=[cc1_in.tensor.ap().opt()], outs=[cc1_out.tensor.ap().opt()]),
        r=[whole(cc1_in.tensor)], w=[whole(cc1_out.tensor)], kind="cc",
        tag=tag + "_cc1")

    kvo_sb = nc.alloc_sbuf_tensor(tag + "_kvo", [1, 2], I32)
    p._add("sync", lambda eng: eng.dma_start(out=kvo_sb[0:1, 0:2], in_=kvoff_ap),
           r=[whole(kvoff_ap.tensor)], w=[whole(kvo_sb)], kind="dma", tag=tag)
    ktile_all = mk.bf.alloc(NT * 8, tag + "_ktall")
    kTpad = mk.bf.alloc(31 + SEQ, tag + "_kTpad")
    p._add("vector", lambda eng: eng.memset(kTpad.ap(slice(0, 8), slice(0, 31)), 0.0),
           r=[], w=[kTpad.res(slice(0, 31))], kind="op", tag=tag)
    regs = {}

    def _kv_reg(eng, which):
        if which not in regs:
            regs[which] = eng.value_load(kvo_sb[0:1, which:which + 1])
        return regs[which]

    def ldk(eng):
        off = _kv_reg(eng, 0)
        srcap = AP(cc1_out.tensor, off, [[192, 128], [128 * 192, NT], [1, 8]])
        dstap = ktile_all[:, :].rearrange("p (j c) -> p j c", j=NT, c=8)
        return eng.dma_start(out=dstap, in_=srcap)
    p._add("sync", ldk, r=[whole(cc1_out.tensor), whole(kvo_sb)],
           w=[ktile_all.res()], kind="dma", tag=tag + "_kld")
    for j in range(NT):
        pt = mk.PT0 if j % 2 == 0 else mk.PT1
        mk.tr(pt[0:8, 0:128], ktile_all[:, j * 8:(j + 1) * 8],
              r=[ktile_all.res(slice(j * 8, (j + 1) * 8))], w=[whole(pt)],
              tag=tag + "_ktr2")
        p._add("vector", lambda eng, pt=pt, j=j: eng.tensor_copy(
            kTpad.ap(slice(0, 8), slice(31 + j * 128, 31 + (j + 1) * 128)),
            pt[0:8, 0:128]),
            r=[whole(pt)], w=[kTpad.res(slice(31 + j * 128, 31 + (j + 1) * 128))],
            kind="op", tag=tag + "_kpe")
    for d in range(16):
        for half, kw in ((0, kwT_lo), (1, kwT_hi)):
            dd = d + 16 * half
            p._add("sync", lambda eng, d=d, dd=dd, kw=kw: eng.dma_start(
                out=kw.ap(slice(d * 8, (d + 1) * 8), slice(0, SEQ)),
                in_=kTpad.ap(slice(0, 8), slice(dd, dd + SEQ))),
                r=[kTpad.res()], w=[kw.res()], kind="dma", tag=tag + "_kw")
    kTpad.free()
    vs_sb = mk.bf.alloc(NT * 8, tag + "_vs")
    p._add("vector", lambda eng: eng.memset(vs_sb[:, :], 0.0),
           r=[], w=[vs_sb.res()], kind="op", tag=tag)

    def ldv(eng):
        off = _kv_reg(eng, 1)
        srcap = AP(cc1_out.tensor, off + 192, [[192, 128], [128 * 192, NT - 1], [1, 8]])
        dstap = vs_sb[:, 0:(NT - 1) * 8].rearrange("p (j c) -> p j c",
                                                   j=NT - 1, c=8)
        return eng.dma_start(out=dstap, in_=srcap)
    p._add("sync", ldv, r=[whole(cc1_out.tensor), whole(kvo_sb), vs_sb.res()],
           w=[vs_sb.res(slice(0, (NT - 1) * 8))], kind="dma", tag=tag + "_vld")

    def ldv2(eng):
        off = _kv_reg(eng, 1)
        srcap = AP(cc1_out.tensor, off + ((NT - 1) * 128 + 1) * 192,
                   [[192, 127], [1, 8]])
        return eng.dma_start(out=vs_sb.ap(slice(0, 127),
                                          slice((NT - 1) * 8, NT * 8)), in_=srcap)
    p._add("sync", ldv2, r=[whole(cc1_out.tensor), whole(kvo_sb), vs_sb.res()],
           w=[vs_sb.res(slice((NT - 1) * 8, NT * 8))], kind="dma",
           tag=tag + "_vld2")

    cc2_in = mk.ctx.dram(tag + "_cc2in", [4 * SEQ, 8], BF16)
    cc2_out = mk.ctx.dram(tag + "_cc2out", [4 * 4 * SEQ, 8], BF16)
    qtile_all = mk.bf.alloc(NT * 8, tag + "_qtall")
    ptsb = mk.bf.alloc(128, tag + "_ptsb")
    ostg = mk.bf.alloc(NT * 8, tag + "_ostg")
    st = mk.f32.alloc(4, tag + "_st")
    for hh in range(4):
        p._add("vector", lambda eng: eng.memset(
            qTpad.ap(slice(0, 8), slice(0, 31)), 0.0),
            r=[], w=[qTpad.res(slice(0, 31))], kind="op", tag=tag)
        def ldq(eng, hh=hh):
            off = _kv_reg(eng, 0)
            qoff = (off - 128) * 4 + hh * 8
            srcap = AP(cc1_out.tensor, qoff,
                       [[192, 128], [128 * 192, NT], [1, 8]])
            dstap = qtile_all[:, :].rearrange("p (j c) -> p j c", j=NT, c=8)
            return eng.dma_start(out=dstap, in_=srcap)
        p._add("sync", ldq, r=[whole(cc1_out.tensor), whole(kvo_sb)],
               w=[qtile_all.res()], kind="dma", tag=tag + "_qld")
        for j in range(NT):
            pt = mk.PT0 if j % 2 == 0 else mk.PT1
            mk.tr(pt[0:8, 0:128], qtile_all[:, j * 8:(j + 1) * 8],
                  r=[qtile_all.res(slice(j * 8, (j + 1) * 8))], w=[whole(pt)],
                  tag=tag + "_qtr2")
            p._add("vector", lambda eng, pt=pt, j=j: eng.tensor_copy(
                qTpad.ap(slice(0, 8), slice(31 + j * 128, 31 + (j + 1) * 128)),
                pt[0:8, 0:128]),
                r=[whole(pt)],
                w=[qTpad.res(slice(31 + j * 128, 31 + (j + 1) * 128))],
                kind="op", tag=tag + "_qpe")
        for d in range(16):
            for half, qw in ((0, qwT_lo), (1, qwT_hi)):
                dd = d + 16 * half
                p._add("sync", lambda eng, d=d, dd=dd, qw=qw: eng.dma_start(
                    out=qw.ap(slice(d * 8, (d + 1) * 8), slice(0, SEQ)),
                    in_=qTpad.ap(slice(0, 8), slice(dd, dd + SEQ))),
                    r=[qTpad.res()], w=[qw.res()], kind="dma", tag=tag + "_qw")
        for qt in range(NT):
            kmax = (qt + 1) * 128
            nch = (kmax + 511) // 512
            qc = slice(qt * 128, (qt + 1) * 128)
            for ch in range(nch):
                ps = mk.PA if ch % 2 == 0 else mk.PB
                nn = min(512, kmax - ch * 512)
                kc = slice(ch * 512, ch * 512 + nn)
                mk.mm(ps[:, 0:nn], qwT_lo[:, qc], kwT_lo[:, kc],
                      r=[qwT_lo.res(qc), kwT_lo.res(kc)], w=[whole(ps)],
                      start=True, stop=False, tag=tag + "_s1")
                mk.mm(ps[:, 0:nn], qwT_hi[:, qc], kwT_hi[:, kc],
                      r=[qwT_hi.res(qc), kwT_hi.res(kc)], w=[whole(ps)],
                      start=False, stop=False, tag=tag + "_s2")
                mk.mm(ps[:, 0:nn], invi.ap(slice(0, 1), qc),
                      rrow.ap(slice(0, 1), kc),
                      r=[invi.res(qc), rrow.res(kc)], w=[whole(ps)],
                      start=False, stop=True, tag=tag + "_sb")
                p._add("scalar", lambda eng, ps=ps, kc=kc, nn=nn: eng.activation(
                    s_row[:, kc], ps[:, 0:nn], AF.Copy),
                    r=[whole(ps)], w=[s_row.res(kc)], kind="op",
                    tag=tag + "_sev")
            dc = slice(qt * 128, kmax)
            p._add("vector", lambda eng, dc=dc: eng.tensor_tensor(
                s_row[:, dc], s_row[:, dc], maskc[:, 0:128], ALU.mult),
                r=[s_row.res(dc), maskc.res()], w=[s_row.res(dc)], kind="op",
                tag=tag + "_dm1")
            p._add("vector", lambda eng, dc=dc: eng.tensor_tensor(
                s_row[:, dc], s_row[:, dc], negbig[:, 0:128], ALU.add),
                r=[s_row.res(dc), negbig.res()], w=[s_row.res(dc)], kind="op",
                tag=tag + "_dm2")
            p._add("vector", lambda eng, kmax=kmax: eng.reduce_max(
                st[:, 0:1], s_row[:, 0:kmax], axis=AX.X, negate=True),
                r=[s_row.res(slice(0, kmax))], w=[st.res(slice(0, 1))],
                kind="op", tag=tag + "_max")
            p._add("scalar", lambda eng, kmax=kmax: eng.activation(
                s_row[:, 0:kmax], s_row[:, 0:kmax], AF.Exp, bias=st[:, 0:1],
                accum_out=st[:, 1:2]),
                r=[s_row.res(slice(0, kmax)), st.res(slice(0, 1))],
                w=[s_row.res(slice(0, kmax)), st.res(slice(1, 2))], kind="op",
                tag=tag + "_exp")
            if qt == 0:
                p._add("vector", lambda eng: eng.memset(
                    s_row.ap(slice(0, 1), slice(0, 128)), 0.0),
                    r=[], w=[s_row.res(slice(0, 128))], kind="op",
                    tag=tag + "_r0")
            p._add("vector", lambda eng: eng.reciprocal(st[:, 2:3], st[:, 1:2]),
                   r=[st.res(slice(1, 2))], w=[st.res(slice(2, 3))], kind="op",
                   tag=tag + "_rec")
            for kb in range(qt + 1):
                pt = mk.PT0 if kb % 2 == 0 else mk.PT1
                mk.tr(pt[:, 0:128], s_row[:, kb * 128:(kb + 1) * 128],
                      r=[s_row.res(slice(kb * 128, (kb + 1) * 128))],
                      w=[whole(pt)], tag=tag + "_ptr")
                p._add("vector", lambda eng, pt=pt: eng.tensor_copy(
                    ptsb[:, :], pt[:, 0:128]),
                    r=[whole(pt)], w=[ptsb.res()], kind="op", tag=tag + "_pte")
                mk.mm(mk.PC[:, 0:8], ptsb[:, :], vs_sb[:, kb * 8:(kb + 1) * 8],
                      r=[ptsb.res(), vs_sb.res(slice(kb * 8, (kb + 1) * 8))],
                      w=[whole(mk.PC)], start=(kb == 0), stop=(kb == qt),
                      tag=tag + "_pv")
            p._add("scalar", lambda eng, qt=qt: eng.activation(
                ostg[:, qt * 8:(qt + 1) * 8], mk.PC[:, 0:8], AF.Identity,
                scale=st[:, 2:3]),
                r=[whole(mk.PC), st.res(slice(2, 3))],
                w=[ostg.res(slice(qt * 8, (qt + 1) * 8))], kind="op",
                tag=tag + "_oev")
        srcap = ostg[:, :].rearrange("p (q v) -> p q v", q=NT, v=8)
        dstap = AP(cc2_in.tensor, hh * SEQ * 8, [[8, 128], [128 * 8, NT], [1, 8]])
        p._add("sync", lambda eng, srcap=srcap, dstap=dstap: eng.dma_start(
            out=dstap, in_=srcap),
            r=[ostg.res()], w=[whole(cc2_in.tensor)], kind="dma",
            tag=tag + "_ogw")
    for t in (qTpad, qwT_lo, qwT_hi, qtile_all, s_row, ptsb, ostg, kwT_lo,
              kwT_hi, vs_sb, ktile_all):
        t.free()
    p._add("gpsimd", lambda eng: eng.collective_compute(
        "AllGather", ALU.bypass, replica_groups=cc_rg,
        ins=[cc2_in.tensor.ap().opt()], outs=[cc2_out.tensor.ap().opt()]),
        r=[whole(cc2_in.tensor)], w=[whole(cc2_out.tensor)], kind="cc",
        tag=tag + "_cc2")

    vemb = mk.f32.alloc(4, tag + "_vemb")
    p._add("sync", lambda eng: eng.dma_start(out=vemb[:, 0:1], in_=vemb0_ap),
           r=[whole(vemb0_ap.tensor)], w=[vemb.res(slice(0, 1))], kind="dma",
           tag=tag)
    p._add("sync", lambda eng: eng.dma_start(out=vemb[:, 1:2], in_=vemb1_ap),
           r=[whole(vemb1_ap.tensor)], w=[vemb.res(slice(1, 2))], kind="dma",
           tag=tag)
    p._add("vector", lambda eng: eng.tensor_tensor(
        vemb[:, 2:3], vemb[:, 1:2], vemb[:, 0:1], ALU.subtract),
        r=[vemb.res(slice(0, 2))], w=[vemb.res(slice(2, 3))], kind="op",
        tag=tag)
    qtb_sb = nc.alloc_sbuf_tensor(tag + "_qtb", [1, 1], I32)
    p._add("sync", lambda eng: eng.dma_start(out=qtb_sb[0:1, 0:1], in_=qtb_ap),
           r=[whole(qtb_ap.tensor)], w=[whole(qtb_sb)], kind="dma", tag=tag)
    amix = mk.bf.alloc(128, tag + "_amix")
    attn1 = mk.bf.alloc(1024, tag + "_attn1")
    for ti in range(8):
        def lda(eng, ti=ti):
            if "qtb" not in regs:
                regs["qtb"] = eng.value_load(qtb_sb[0:1, 0:1])
            qb = regs["qtb"]
            off = (qb + ti * 128) * 8
            srcap = AP(cc2_out.tensor, off,
                       [[8, 128], [4 * SEQ * 8, 4], [SEQ * 8, 4], [1, 8]])
            dstap = amix[:, :].rearrange("p (r hh v) -> p r hh v", r=4, hh=4, v=8)
            return eng.dma_start(out=dstap, in_=srcap)
        p._add("sync", lda, r=[whole(cc2_out.tensor), whole(qtb_sb)],
               w=[amix.res()], kind="dma", tag=tag + "_ald")
        pt = mk.PT0 if ti % 2 == 0 else mk.PT1
        mk.tr(pt[:, 0:128], amix[:, :], r=[amix.res()], w=[whole(pt)],
              tag=tag + "_atr")
        p._add("vector", lambda eng, pt=pt, ti=ti: eng.tensor_copy(
            attn1[:, ti * 128:(ti + 1) * 128], pt[:, 0:128]),
            r=[whole(pt)], w=[attn1.res(slice(ti * 128, (ti + 1) * 128))],
            kind="op", tag=tag + "_ate")
    import os as _os
    if _os.environ.get("DBG"):
        raw_o = mk.ctx.out("raw_o", (128, 1024), F32)
        rawf = mk.f32.alloc(1024, tag + "_rawf")
        p._add("vector", lambda eng: eng.tensor_copy(rawf[:, :], attn1[:, :]),
               r=[attn1.res()], w=[rawf.res()], kind="op", tag=tag + "_rawc")
        p._add("sync", lambda eng: eng.dma_start(out=raw_o, in_=rawf[:, :]),
               r=[rawf.res()], w=[whole(raw_o.tensor)], kind="dma",
               tag=tag + "_rawo")
        rawf.free()
        mk._raw_out = raw_o
    p._add("vector", lambda eng: eng.tensor_scalar(
        attn1[:, :], attn1[:, :], vemb[:, 2:3], vemb[:, 0:1],
        ALU.mult, ALU.add),
        r=[attn1.res(), vemb.res(slice(0, 3))], w=[attn1.res()], kind="op",
        tag=tag + "_vemb")
    woSB = mk.bf.alloc(2048, tag + "_wo")
    load_wcol(mk, woT_ap, 0, 2048, 1, woSB, tag=tag + "_wodma")
    for fb in range(DB):
        for th in range(2):
            ps = mk.PD if th == 0 else mk.PE_
            mk.mm(ps[:, :], woSB[:, fb * 128:(fb + 1) * 128],
                  attn1[:, th * 512:(th + 1) * 512],
                  r=[woSB.res(slice(fb * 128, (fb + 1) * 128)),
                     attn1.res(slice(th * 512, (th + 1) * 512))],
                  w=[whole(ps)], tag=tag + "_wom")
            dc = slice(fb * 1024 + th * 512, fb * 1024 + (th + 1) * 512)
            p._add("vector", lambda eng, ps=ps, dc=dc: eng.tensor_tensor(
                x_acc[:, dc], x_acc[:, dc], ps[:, :], ALU.add),
                r=[x_acc.res(dc), whole(ps)], w=[x_acc.res(dc)], kind="op",
                tag=tag + "_woacc")
    for t in (amix, attn1, woSB, taut, invt, st, vemb):
        t.free()



def build_l1_ffn(mk, xn1T, x_acc, gw_ap, ebias_ap, sw1_ap, sw2_ap,
                 rw1_aps, rw2_aps, cc_all, tag="l1f"):
    p, nc = mk.p, mk.nc
    onesf = mk.const("l1_onesf", np.ones((1, 128), np.float32))
    ones128b = mk.const("l1f_ones128", np.ones((128, 1), np.float32), BF16)

    gwv = mk.bf.alloc(DB * E1, tag + "_gw")
    load_wcol(mk, gw_ap, 0, E1, DB, gwv, tag=tag + "_gw")
    eb = mk.f32.alloc(16, tag + "_eb")
    ebb = mk.bf.alloc(16, tag + "_ebb")
    p._add("sync", lambda eng: eng.dma_start(out=eb.ap(slice(0, 1), slice(0, E1)),
                                             in_=ebias_ap),
           r=[whole(ebias_ap.tensor)], w=[eb.res(slice(0, E1))], kind="dma",
           tag=tag)
    p._add("vector", lambda eng: eng.tensor_copy(
        ebb.ap(slice(0, 1), slice(0, E1)), eb.ap(slice(0, 1), slice(0, E1))),
        r=[eb.res(slice(0, E1))], w=[ebb.res(slice(0, E1))], kind="op", tag=tag)
    p._add("vector", lambda eng: eng.tensor_copy(
        eb.ap(slice(0, 1), slice(E1, 2 * E1)), ebb.ap(slice(0, 1), slice(0, E1))),
        r=[ebb.res(slice(0, E1))], w=[eb.res(slice(E1, 2 * E1))], kind="op",
        tag=tag)
    p._add("vector", lambda eng: eng.tensor_tensor(
        eb.ap(slice(0, 1), slice(0, E1)), eb.ap(slice(0, 1), slice(0, E1)),
        eb.ap(slice(0, 1), slice(E1, 2 * E1)), ALU.subtract),
        r=[eb.res(slice(0, 2 * E1))], w=[eb.res(slice(0, E1))], kind="op",
        tag=tag)
    p._add("vector", lambda eng: eng.tensor_copy(
        ebb.ap(slice(0, 1), slice(8, 8 + E1)), eb.ap(slice(0, 1), slice(0, E1))),
        r=[eb.res(slice(0, E1))], w=[ebb.res(slice(8, 8 + E1))], kind="op",
        tag=tag)
    onesb_row = mk.const("l1f_onesrow", np.ones((1, 128), np.float32), BF16)

    aff_own = mk.f32.alloc(64, tag + "_aff")
    mask_own = mk.f32.alloc(64, tag + "_mask")
    s_own = mk.f32.alloc(64, tag + "_sown")
    rk = mk.f32.alloc(16, tag + "_rk")
    cc3_in = mk.ctx.dram(tag + "_cc3in", [TPC, 8], F32)
    cc3_out = mk.ctx.dram(tag + "_cc3out", [B * N, 8], F32)
    p._add("vector", lambda eng: eng.memset(mask_own[:, :], 0.0),
           r=[], w=[mask_own.res()], kind="op", tag=tag)
    p._add("vector", lambda eng: eng.memset(s_own[:, :], -1.0),
           r=[], w=[s_own.res()], kind="op", tag=tag)
    for i in range(8):
        ps = mk.PE_
        for kb in range(DB):
            mk.mm(ps[:, 0:E1],
                  xn1T[:, kb * 1024 + i * 128: kb * 1024 + (i + 1) * 128],
                  gwv[:, kb * E1:(kb + 1) * E1],
                  r=[xn1T.res(slice(kb * 1024 + i * 128, kb * 1024 + (i + 1) * 128)),
                     gwv.res(slice(kb * E1, (kb + 1) * E1))],
                  w=[whole(ps)], start=(kb == 0), stop=False, tag=tag + "_gl")
        mk.mm(ps[:, 0:E1], onesb_row[0:1, :], ebb.ap(slice(0, 1), slice(0, E1)),
              r=[onesb_row.res(), ebb.res(slice(0, E1))], w=[whole(ps)],
              start=False, stop=False, tag=tag + "_gb1")
        mk.mm(ps[:, 0:E1], onesb_row[0:1, :], ebb.ap(slice(0, 1), slice(8, 8 + E1)),
              r=[onesb_row.res(), ebb.res(slice(8, 8 + E1))], w=[whole(ps)],
              start=False, stop=True, tag=tag + "_gb2")
        ac = slice(i * 8, i * 8 + E1)
        p._add("scalar", lambda eng, ps=ps, ac=ac: eng.activation(
            aff_own[:, ac], ps[:, 0:E1], AF.Sigmoid),
            r=[whole(ps)], w=[aff_own.res(ac)], kind="op", tag=tag + "_sig")
        for e in range(E1):
            p._add("vector", lambda eng, e=e, ac=ac: eng.tensor_scalar(
                rk[:, 0:E1], aff_own[:, ac], aff_own[:, i * 8 + e:i * 8 + e + 1],
                None, ALU.is_gt),
                r=[aff_own.res(ac)], w=[rk.res(slice(0, E1))], kind="op",
                tag=tag)
            p._add("vector", lambda eng: eng.reduce_sum(
                rk[:, 8:9], rk[:, 0:E1], axis=AX.X),
                r=[rk.res(slice(0, E1))], w=[rk.res(slice(8, 9))], kind="op",
                tag=tag)
            if e > 0:
                p._add("vector", lambda eng, e=e, i=i: eng.tensor_scalar(
                    rk[:, 0:e], aff_own[:, i * 8:i * 8 + e],
                    aff_own[:, i * 8 + e:i * 8 + e + 1], None, ALU.is_equal),
                    r=[aff_own.res(ac)], w=[rk.res(slice(0, e))], kind="op",
                    tag=tag)
                p._add("vector", lambda eng, e=e: eng.reduce_sum(
                    rk[:, 9:10], rk[:, 0:e], axis=AX.X),
                    r=[rk.res(slice(0, e))], w=[rk.res(slice(9, 10))],
                    kind="op", tag=tag)
                p._add("vector", lambda eng: eng.tensor_tensor(
                    rk[:, 8:9], rk[:, 8:9], rk[:, 9:10], ALU.add),
                    r=[rk.res(slice(8, 10))], w=[rk.res(slice(8, 9))],
                    kind="op", tag=tag)
            p._add("vector", lambda eng, e=e, i=i: eng.tensor_scalar(
                mask_own[:, i * 8 + e:i * 8 + e + 1], rk[:, 8:9], 2.0, None,
                ALU.is_lt),
                r=[rk.res(slice(8, 9))],
                w=[mask_own.res(slice(i * 8 + e, i * 8 + e + 1))], kind="op",
                tag=tag)
        p._add("vector", lambda eng, ac=ac: eng.scalar_tensor_tensor(
            s_own[:, ac], aff_own[:, ac], 1.0, mask_own[:, ac],
            ALU.add, ALU.mult),
            r=[aff_own.res(ac), mask_own.res(ac)], w=[s_own.res(ac)],
            kind="op", tag=tag)
        p._add("vector", lambda eng, ac=ac: eng.tensor_scalar(
            s_own[:, ac], s_own[:, ac], -1.0, None, ALU.add),
            r=[s_own.res(ac)], w=[s_own.res(ac)], kind="op", tag=tag)
        p._add("sync", lambda eng, i=i: eng.dma_start(
            out=cc3_in.tensor.ap()[i * 128:(i + 1) * 128, :],
            in_=s_own[:, i * 8:(i + 1) * 8]),
            r=[s_own.res(slice(i * 8, (i + 1) * 8))],
            w=[whole(cc3_in.tensor)], kind="dma", tag=tag + "_sw")
    gwv.free()
    p._add("gpsimd", lambda eng: eng.collective_compute(
        "AllGather", ALU.bypass, replica_groups=cc_all,
        ins=[cc3_in.tensor.ap().opt()], outs=[cc3_out.tensor.ap().opt()]),
        r=[whole(cc3_in.tensor)], w=[whole(cc3_out.tensor)], kind="cc",
        tag=tag + "_cc3")

    sall = mk.f32.alloc(512, tag + "_sall")
    cntb = mk.bf.alloc(512, tag + "_cntb")
    bis = mk.f32.alloc(64, tag + "_bis")
    srcap = AP(cc3_out.tensor, 0, [[8, 128], [1024, 64], [1, 8]])
    p._add("sync", lambda eng: eng.dma_start(
        out=sall[:, :].rearrange("p (g e) -> p g e", g=64, e=8), in_=srcap),
        r=[whole(cc3_out.tensor)], w=[sall.res()], kind="dma", tag=tag + "_sld")
    p._add("vector", lambda eng: eng.memset(bis[0:1, 0:8], -2.0),
           r=[], w=[bis.res(slice(0, 8))], kind="op", tag=tag)
    p._add("vector", lambda eng: eng.memset(bis[0:1, 8:16], 1.0),
           r=[], w=[bis.res(slice(8, 16))], kind="op", tag=tag)
    for it in range(BISECT_ITERS):
        p._add("vector", lambda eng: eng.tensor_tensor(
            bis[0:1, 16:24], bis[0:1, 0:8], bis[0:1, 8:16], ALU.add),
            r=[bis.res(slice(0, 16))], w=[bis.res(slice(16, 24))], kind="op",
            tag=tag)
        p._add("vector", lambda eng: eng.tensor_scalar(
            bis[0:1, 16:24], bis[0:1, 16:24], 0.5, None, ALU.mult),
            r=[bis.res(slice(16, 24))], w=[bis.res(slice(16, 24))], kind="op",
            tag=tag)
        p._add("vector", lambda eng: eng.tensor_copy(
            ebb.ap(slice(0, 1), slice(0, 8)), bis[0:1, 16:24]),
            r=[bis.res(slice(16, 24))], w=[ebb.res(slice(0, 8))], kind="op",
            tag=tag)
        mk.mm(mk.PF[:, 0:8], onesf[0:1, :], bis[0:1, 16:24],
              r=[onesf.res(), bis.res(slice(16, 24))], w=[whole(mk.PF)],
              tag=tag + "_thb")
        thb = AP(mk.PF, 0, [[512, 128], [0, 64], [1, 8]])
        p._add("vector", lambda eng, thb=thb: eng.tensor_tensor(
            cntb[:, :].rearrange("p (g e) -> p g e", g=64, e=8),
            sall[:, :].rearrange("p (g e) -> p g e", g=64, e=8),
            thb, ALU.is_gt),
            r=[sall.res(), whole(mk.PF)], w=[cntb.res()], kind="op",
            tag=tag + "_cmp")
        mk.mm(mk.PE_[0:1, :], ones128b[0:128, 0:1], cntb[:, :],
              r=[ones128b.res(), cntb.res()], w=[whole(mk.PE_)],
              tag=tag + "_csum")
        p._add("vector", lambda eng: eng.reduce_sum(
            bis[0:1, 32:40],
            AP(mk.PE_, 0, [[512, 1], [1, 8], [8, 64]]), axis=AX.X),
            r=[whole(mk.PE_)], w=[bis.res(slice(32, 40))], kind="op",
            tag=tag + "_gred")
        p._add("vector", lambda eng: eng.tensor_scalar(
            bis[0:1, 40:48], bis[0:1, 32:40], float(CAP1), None, ALU.is_ge),
            r=[bis.res(slice(32, 40))], w=[bis.res(slice(40, 48))], kind="op",
            tag=tag)
        p._add("vector", lambda eng: eng.tensor_tensor(
            bis[0:1, 48:56], bis[0:1, 16:24], bis[0:1, 0:8], ALU.subtract),
            r=[bis.res(slice(0, 24))], w=[bis.res(slice(48, 56))], kind="op",
            tag=tag)
        p._add("vector", lambda eng: eng.tensor_tensor(
            bis[0:1, 48:56], bis[0:1, 48:56], bis[0:1, 40:48], ALU.mult),
            r=[bis.res(slice(40, 56))], w=[bis.res(slice(48, 56))], kind="op",
            tag=tag)
        p._add("vector", lambda eng: eng.tensor_tensor(
            bis[0:1, 0:8], bis[0:1, 0:8], bis[0:1, 48:56], ALU.add),
            r=[bis.res(slice(0, 8)), bis.res(slice(48, 56))],
            w=[bis.res(slice(0, 8))], kind="op", tag=tag)
        p._add("vector", lambda eng: eng.tensor_tensor(
            bis[0:1, 48:56], bis[0:1, 16:24], bis[0:1, 8:16], ALU.subtract),
            r=[bis.res(slice(8, 24))], w=[bis.res(slice(48, 56))], kind="op",
            tag=tag)
        p._add("vector", lambda eng: eng.scalar_tensor_tensor(
            bis[0:1, 56:64], bis[0:1, 40:48], 1.0, bis[0:1, 48:56],
            ALU.subtract, ALU.mult),
            r=[bis.res(slice(40, 56))], w=[bis.res(slice(56, 64))], kind="op",
            tag=tag)
        p._add("vector", lambda eng: eng.tensor_tensor(
            bis[0:1, 8:16], bis[0:1, 8:16], bis[0:1, 56:64], ALU.subtract),
            r=[bis.res(slice(8, 16)), bis.res(slice(56, 64))],
            w=[bis.res(slice(8, 16))], kind="op", tag=tag)
    sall.free(); cntb.free()

    wT_all = mk.bf.alloc(1024, tag + "_wT")
    wtile = mk.f32.alloc(8, tag + "_wtile")
    wtb = mk.bf.alloc(8, tag + "_wtb")
    mk.mm(mk.PF[:, 0:8], onesf[0:1, :], bis[0:1, 0:8],
          r=[onesf.res(), bis.res(slice(0, 8))], w=[whole(mk.PF)],
          tag=tag + "_lob")
    lob = mk.f32.alloc(8, tag + "_lob")
    p._add("vector", lambda eng: eng.tensor_copy(lob[:, :], mk.PF[:, 0:8]),
           r=[whole(mk.PF)], w=[lob.res()], kind="op", tag=tag)
    for i in range(8):
        ac = slice(i * 8, (i + 1) * 8)
        p._add("vector", lambda eng, ac=ac: eng.tensor_tensor(
            wtile[:, :], s_own[:, ac], lob[:, :], ALU.is_gt),
            r=[s_own.res(ac), lob.res()], w=[wtile.res()], kind="op", tag=tag)
        p._add("vector", lambda eng, ac=ac: eng.tensor_tensor(
            wtile[:, :], wtile[:, :], s_own[:, ac], ALU.mult),
            r=[wtile.res(), s_own.res(ac)], w=[wtile.res()], kind="op", tag=tag)
        p._add("vector", lambda eng, ac=ac: eng.tensor_tensor(
            wtile[:, :], wtile[:, :], mask_own[:, ac], ALU.mult),
            r=[wtile.res(), mask_own.res(ac)], w=[wtile.res()], kind="op",
            tag=tag)
        p._add("vector", lambda eng: eng.tensor_copy(wtb[:, :], wtile[:, :]),
               r=[wtile.res()], w=[wtb.res()], kind="op", tag=tag)
        pt = mk.PT0 if i % 2 == 0 else mk.PT1
        mk.tr(pt[0:8, 0:128], wtb[:, 0:8], r=[wtb.res()], w=[whole(pt)],
              tag=tag + "_wtr")
        p._add("vector", lambda eng, pt=pt, i=i: eng.tensor_copy(
            wT_all.ap(slice(0, 8), slice(i * 128, (i + 1) * 128)),
            pt[0:8, 0:128]),
            r=[whole(pt)], w=[wT_all.res(slice(i * 128, (i + 1) * 128))],
            kind="op", tag=tag + "_wte")
    wtile.free(); wtb.free(); lob.free()
    for t in (aff_own, mask_own, s_own, rk, eb, ebb, bis):
        t.free()

    sel = np.zeros((8, 8 * 128), np.float32)
    for e in range(8):
        sel[e, e * 128:(e + 1) * 128] = 1.0
    selc = mk.const("l1_sel", sel, BF16)

    h_all = mk.bf.alloc(FB0 * 1024, tag + "_h")
    cb = mk.bf.alloc(1024, tag + "_cb")
    sg = mk.bf.alloc(512, tag + "_sg")
    t1 = mk.bf.alloc(512, tag + "_t1")
    wg = [mk.bf.alloc(DB * 128, tag + f"_wg{i}") for i in range(2)]
    wu = [mk.bf.alloc(DB * 128, tag + f"_wu{i}") for i in range(2)]
    wd = [mk.bf.alloc(FB0 * 128, tag + f"_wd{i}") for i in range(2)]
    ffnbufs = {"xnT": xn1T, "h": h_all, "sg": sg, "t1": t1, "wg": wg,
               "wu": wu, "wd": wd}
    FB1 = F1 // 128
    ffn_expert(mk, ffnbufs, sw1_ap, sw1_ap, sw2_ap, FB1, None, x_acc,
               F1, tag + "_sh", ucol0=0)
    for e in range(E1):
        for th in range(2):
            ps = mk.PF
            mk.mm(ps[:, :], selc[0:8, e * 128:(e + 1) * 128],
                  wT_all.ap(slice(0, 8), slice(th * 512, (th + 1) * 512)),
                  r=[selc.res(slice(e * 128, (e + 1) * 128)),
                     wT_all.res(slice(th * 512, (th + 1) * 512))],
                  w=[whole(ps)], tag=tag + "_cbm")
            p._add("vector", lambda eng, ps=ps, th=th: eng.tensor_copy(
                cb[:, th * 512:(th + 1) * 512], ps[:, :]),
                r=[whole(ps)], w=[cb.res(slice(th * 512, (th + 1) * 512))],
                kind="op", tag=tag + "_cbe")
        ffn_expert(mk, ffnbufs, rw1_aps[e], rw1_aps[e], rw2_aps[e], FB1, cb,
                   x_acc, F1, f"{tag}_e{e}", ucol0=0,
                   src_res_g=whole(rw1_aps[e].tensor),
                   src_res_u=whole(rw1_aps[e].tensor),
                   src_res_d=whole(rw2_aps[e].tensor))
    for t in (h_all, cb, sg, t1, wT_all):
        t.free()
    for lst in (wg, wu, wd):
        for w_ in lst:
            w_.free()



def build_full(mk, inputs):
    ctx = mk.ctx
    x = inputs["L0_attn_x"]
    xr = x.reshape(B * N // TPC, TPC, D)
    x_ap = ctx.inp("x_rows", (TPC, D), F32, lambda c: xr[c])
    xT_ap = ctx.inp("xT_rows", (D, TPC), F32,
                    lambda c: np.ascontiguousarray(xr[c].T))
    wqkvT = np.ascontiguousarray(inputs["L0_attn_w_qkv"].T)
    wq_ap = ctx.inp("wqkvT", (D, 3 * D), F32, lambda c: wqkvT)
    bq_c = mk.const("bqkv", np.ascontiguousarray(
        inputs["L0_attn_b_qkv"].reshape(48, 128).T))
    woutT = np.ascontiguousarray(inputs["L0_attn_w_out"].T)
    woutT_ap = ctx.inp("woutT", (D, D), F32, lambda c: woutT)
    bout_c = mk.const("bout", np.ascontiguousarray(
        inputs["L0_attn_b_out"].reshape(16, 128).T))
    wrouter_ap = ctx.inp("wrouter", (D, E0), F32,
                         lambda c: inputs["L0_ffn_W_router"])
    weg, weu, wed = [], [], []
    for e in range(E0):
        weg.append(ctx.inp(f"weg{e}", (D, DE0), F32,
                           lambda c, e=e: inputs["L0_ffn_We_gate"][e]))
        weu.append(ctx.inp(f"weu{e}", (D, DE0), F32,
                           lambda c, e=e: inputs["L0_ffn_We_up"][e]))
        wed.append(ctx.inp(f"wed{e}", (DE0, D), F32,
                           lambda c, e=e: inputs["L0_ffn_We_down"][e]))
    wsg_ap = ctx.inp("wsg", (D, DS0), F32, lambda c: inputs["L0_ffn_Ws_gate"])
    wsu_ap = ctx.inp("wsu", (D, DS0), F32, lambda c: inputs["L0_ffn_Ws_up"])
    wsd_ap = ctx.inp("wsd", (DS0, D), F32, lambda c: inputs["L0_ffn_Ws_down"])
    wqT1 = ctx.inp("wqT1", (D, 128), F32,
                   lambda c: np.ascontiguousarray(inputs["L1_attn_Wq"].T))
    wkT1 = ctx.inp("wkT1", (D, 32), F32,
                   lambda c: np.ascontiguousarray(inputs["L1_attn_Wk"].T))
    wvT1 = ctx.inp("wvT1", (D, 32), F32,
                   lambda c: np.ascontiguousarray(inputs["L1_attn_Wv"].T))
    woT1 = ctx.inp("woT1", (128, D), F32,
                   lambda c: np.ascontiguousarray(inputs["L1_attn_Wo"].T))
    ve0 = ctx.inp("ve0", (128, 1), F32,
                  lambda c: np.asarray(inputs["L1_attn_v_emb0"]).reshape(128, 1))
    ve1 = ctx.inp("ve1", (128, 1), F32,
                  lambda c: np.asarray(inputs["L1_attn_v_emb1"]).reshape(128, 1))
    tau_ap = ctx.inp("tau", (1, 1), F32,
                     lambda c: np.asarray(inputs["L1_attn_tau"],
                                          np.float32).reshape(1, 1))
    kvoff = ctx.inp("kvoff", (1, 2), I32,
                    lambda c: np.array([[128 + (c % 4) * 8, 160 + (c % 4) * 8]],
                                       np.int32))
    qtb = ctx.inp("qtb", (1, 1), I32,
                  lambda c: np.array([[(c % 4) * 1024]], np.int32))
    gw_ap = ctx.inp("gw1", (D, E1), F32, lambda c: inputs["L1_ffn_gate_w"])
    ebias_ap = ctx.inp("ebias1", (1, E1), F32,
                       lambda c: np.asarray(inputs["L1_ffn_expert_bias"])
                       .reshape(1, E1))
    sw1_ap = ctx.inp("sw1", (D, 2 * F1), F32,
                     lambda c: inputs["L1_ffn_shared_w1"][0])
    sw2_ap = ctx.inp("sw2", (F1, D), F32,
                     lambda c: inputs["L1_ffn_shared_w2"][0])
    rw1, rw2 = [], []
    for e in range(E1):
        rw1.append(ctx.inp(f"rw1_{e}", (D, 2 * F1), F32,
                           lambda c, e=e: inputs["L1_ffn_routed_w1"][e]))
        rw2.append(ctx.inp(f"rw2_{e}", (F1, D), F32,
                           lambda c, e=e: inputs["L1_ffn_routed_w2"][e]))

    xnT = mk.bf.alloc(DB * 1024, "xnT")
    attnT = mk.bf.alloc(H0 * 1024, "attnT")
    x_acc = mk.f32.alloc(DB * 1024, "x_acc")
    build_ln_xnT(mk, x_ap, xnT)
    build_lightning(mk, xnT, wq_ap, bq_c, attnT, [list(range(NCORE))])
    build_l0_finish(mk, xnT, attnT, xT_ap, x_acc, woutT_ap, bout_c, wrouter_ap,
                    weg, weu, wed, wsg_ap, wsu_ap, wsd_ap)
    xnT.free()
    mk.free_const("lg_dq")
    mk.free_const("lg_dmT")
    mk.free_const("l0_sel")
    xn1T = mk.bf.alloc(DB * 1024, "xn1T")
    build_ln_xnT(mk, None, xn1T, tag="ln1", loader=acc_loader(mk, x_acc))
    bgrp = [[0, 1, 2, 3], [4, 5, 6, 7]]
    build_l1_attn(mk, xn1T, x_acc, wqT1, wkT1, wvT1, woT1, ve0, ve1, tau_ap,
                  kvoff, qtb, bgrp)
    build_l1_ffn(mk, xn1T, x_acc, gw_ap, ebias_ap, sw1_ap, sw2_ap, rw1, rw2,
                 [list(range(NCORE))])
    y_o = ctx.out("yT_o", (D, TPC), F32)
    p = mk.p
    for kb in range(DB):
        p._add("sync", lambda eng, kb=kb: eng.dma_start(
            out=y_o[kb * 128:(kb + 1) * 128, :],
            in_=x_acc[:, kb * 1024:(kb + 1) * 1024]),
            r=[x_acc.res(slice(kb * 1024, (kb + 1) * 1024))],
            w=[whole(y_o.tensor)], kind="dma", tag="yout")
    p._add("sync", lambda eng: eng.nop(), r=[whole(y_o.tensor)], w=[],
           kind="op", tag="final")
    return y_o




def kernel(**inputs):
    nc = bass.Bass("TRN2", target_bir_lowering=False)
    mk = MK(nc)
    build_full(mk, inputs)
    mk.p.finalize()
    res = run_bass_kernel_spmd(nc, mk.ctx.in_maps(), core_ids=list(range(NCORE)))
    out = np.concatenate([res.results[c]["yT_o"].T for c in range(NCORE)],
                         axis=0).reshape(B, N, D)
    return np.ascontiguousarray(out, dtype=np.float32)


# version 02
# speedup vs baseline: 1.2087x; 1.2087x over previous
import math
import numpy as np
import ml_dtypes
import concourse.bass as bass
import concourse.mybir as mybir
from concourse.ap import AP
from concourse.bass_utils import run_bass_kernel_spmd

F32 = mybir.dt.float32
BF16 = mybir.dt.bfloat16
I32 = mybir.dt.int32
ALU = mybir.AluOpType
AF = mybir.ActivationFunctionType
AX = mybir.AxisListType

B, N, D = 2, 4096, 2048
H0, HD, CH = 16, 128, 128
NCORE = 8
TPC = 1024
NCH = TPC // CH
DB = D // 128
E0, DE0, DS0 = 8, 1408, 2816
FB0 = DE0 // 128
SB0 = DS0 // 128
H1, KV1, NB1, NV1, W1 = 16, 4, 8, 8, 32
QKD = W1 * NB1
E1, F1 = 7, 1024
CAP1 = (B * N) // E1
BISECT_ITERS = 38

ENGS = ("tensor", "vector", "scalar", "gpsimd", "sync")


class _Op:
    __slots__ = ("eng", "fn", "reads", "writes", "seq", "kind", "inc_needed",
                 "deps", "_semkey", "_semval", "_issue_wait", "tag")

    def __init__(self, eng, fn, reads, writes, kind, tag=""):
        self.eng, self.fn, self.reads, self.writes = eng, fn, reads, writes
        self.kind = kind
        self.inc_needed = kind in ("dma", "cc")
        self.deps = []
        self.tag = tag


class Prog:
    def __init__(self, nc):
        self.nc = nc
        self.ops = []
        self.eng_ops = {e: [] for e in ENGS}
        self.state = {}
        self._fin = False

    def _touch(self, op):
        deps = op.deps
        for res, lo, hi, is_w in op.reads + op.writes:
            bid = id(res)
            segs = self.state.setdefault(bid, [])
            if is_w:
                keep = []
                for s in segs:
                    if s[0] < hi and lo < s[1]:
                        if s[2] is not None:
                            deps.append(s[2])
                        deps.extend(s[3])
                        if s[0] >= lo and s[1] <= hi:
                            continue
                    keep.append(s)
                keep.append([lo, hi, op, []])
                self.state[bid] = keep
            else:
                for s in segs:
                    if s[0] < hi and lo < s[1]:
                        if s[2] is not None:
                            deps.append(s[2])
                        s[3].append(op)
                segs.append([lo, hi, None, [op]])

    def _add(self, eng, fn, r, w, kind, tag=""):
        assert not self._fin
        op = _Op(eng, fn, [x + (False,) for x in r], [x + (True,) for x in w], kind, tag)
        op.seq = len(self.eng_ops[eng])
        self.eng_ops[eng].append(op)
        self.ops.append(op)
        self._touch(op)
        seen, deps = set(), []
        for d in op.deps:
            if d is op or id(d) in seen:
                continue
            seen.add(id(d))
            if d.eng == eng and d.kind == "op" and eng == "tensor" \
                    and kind == "op":
                continue
            deps.append(d)
        op.deps = deps
        return op

    def finalize(self):
        assert not self._fin
        self._fin = True
        nc = self.nc
        for op in self.ops:
            for d in op.deps:
                d.inc_needed = True
        NDS = 8
        sems = {}
        for e in ENGS:
            if any(o.kind == "op" and o.inc_needed for o in self.eng_ops[e]):
                sems[("op", e)] = nc.semaphore(f"sem_{e}").__enter__()
            n_dma = sum(1 for o in self.eng_ops[e] if o.kind == "dma")
            for j in range(min(NDS, n_dma)):
                sems[("dma", e, j)] = nc.semaphore(f"dmas_{e}_{j}").__enter__()
            if any(o.kind == "cc" for o in self.eng_ops[e]):
                sems[("cc", e)] = nc.semaphore(f"cc_{e}").__enter__()
        self.sems = sems
        for e in ENGS:
            c_op = c_dma = c_cc = 0
            for op in self.eng_ops[e]:
                if op.kind == "dma":
                    j = c_dma % NDS
                    g = c_dma // NDS
                    c_dma += 1
                    op._semkey, op._semval = ("dma", e, j), 16 * (g + 1)
                    op._issue_wait = 16 * g if g > 0 else None
                elif op.kind == "cc":
                    op._semkey, op._semval = ("cc", e), c_cc + 1
                    op._issue_wait = c_cc if c_cc > 0 else None
                    c_cc += 1
                elif op.inc_needed:
                    c_op += 1
                    op._semkey, op._semval = ("op", e), c_op
                    op._issue_wait = None
                else:
                    op._semkey = op._semval = None
                    op._issue_wait = None
        nwaits = [0]
        with nc.Block() as block:
            for e in ENGS:
                ops = self.eng_ops[e]
                if not ops:
                    continue

                def emit(eng, ops=ops, e=e):
                    waited = {}
                    for op in ops:
                        for d in op.deps:
                            sk, v = d._semkey, d._semval
                            if waited.get(sk, -1) < v:
                                eng.wait_ge(self.sems[sk], v)
                                waited[sk] = v
                                nwaits[0] += 1
                        if op._issue_wait is not None:
                            sk = op._semkey
                            if waited.get(sk, -1) < op._issue_wait:
                                eng.wait_ge(self.sems[sk], op._issue_wait)
                                waited[sk] = op._issue_wait
                                nwaits[0] += 1
                        try:
                            inst = op.fn(eng)
                        except Exception:
                            import traceback
                            print(f"EMIT FAIL tag={op.tag} eng={e} seq={op.seq}")
                            traceback.print_exc()
                            raise
                        if op._semkey is not None:
                            if op.kind == "op" and e in ("vector", "scalar",
                                                         "gpsimd"):
                                eng.drain(fusable=False).then_inc(
                                    self.sems[op._semkey], 1)
                            else:
                                inst.then_inc(self.sems[op._semkey],
                                              16 if op.kind == "dma" else 1)

                getattr(block, e)(emit)
        self.n_waits = nwaits[0]
        return self


class Pool:

    def __init__(self, nc, name, cols, dtype):
        self.t = nc.alloc_sbuf_tensor(name, [128, cols], dtype)
        self.cols = cols
        self.free_list = [(0, cols)]
        self.hw = 0

    def alloc(self, n, name=""):
        import os
        nr = (n + 31) & ~31
        for i, (a, b) in enumerate(self.free_list):
            if b - a >= nr:
                self.free_list[i] = (a + nr, b)
                if self.free_list[i][0] == self.free_list[i][1]:
                    del self.free_list[i]
                self.hw = max(self.hw, a + nr)
                if os.environ.get("PRINTALLOC"):
                    print(f"ALLOC {self.t.name} {name}: [{a},{a + nr})")
                v = View(self, a, n, name)
                v.nres = nr
                return v
        raise MemoryError(f"pool {self.t.name}: no {nr} cols (free={self.free_list})")

    def free(self, view):
        import os
        a, b = view.c0, view.c0 + getattr(view, "nres", view.n)
        if os.environ.get("PRINTALLOC"):
            print(f"FREE  {self.t.name} {view.name}: [{a},{b})")
        self.free_list.append((a, b))
        self.free_list.sort()
        merged = []
        for seg in self.free_list:
            if merged and merged[-1][1] == seg[0]:
                merged[-1] = (merged[-1][0], seg[1])
            else:
                merged.append(list(seg))
        self.free_list = [tuple(s) for s in merged]


class View:

    def __init__(self, pool, c0, n, name=""):
        self.pool, self.c0, self.n, self.name = pool, c0, n, name

    def ap(self, psl=slice(None), csl=slice(None)):
        a, b, _ = csl.indices(self.n)
        return self.pool.t[psl, self.c0 + a:self.c0 + b]

    def __getitem__(self, key):
        psl, csl = key
        return self.ap(psl, csl)

    def res(self, csl=slice(None)):
        a, b, _ = csl.indices(self.n)
        return (self.pool.t, self.c0 + a, self.c0 + b)

    def free(self):
        self.pool.free(self)


def whole(t):
    return (t, 0, 1 << 30)


class Ctx:

    def __init__(self, nc):
        self.nc = nc
        self.p = Prog(nc)
        self.inputs = {}
        self.outputs = {}
        self.aps = {}
        self.psum = {}
        self.out_dma_res = []

    def inp(self, name, shape, dtype, percore):
        t = self.nc.dram_tensor(name, list(shape), dtype, kind="ExternalInput")
        self.inputs[name] = percore
        ap = t.ap()
        self.aps[name] = ap
        return ap

    def out(self, name, shape, dtype):
        t = self.nc.dram_tensor(name, list(shape), dtype, kind="ExternalOutput")
        ap = t.ap()
        self.aps[name] = ap
        return ap

    def dram(self, name, shape, dtype):
        t = self.nc.dram_tensor(name, list(shape), dtype, kind="Internal")
        ap = t.ap()
        self.aps[name] = ap
        return ap

    def alloc_psum(self, name, shape, dtype):
        t = self.nc.alloc_psum_tensor(name, list(shape), dtype)
        self.psum[name] = t
        return t

    def in_maps(self, ncore=NCORE):
        return [{k: np.ascontiguousarray(fn(c)) for k, fn in self.inputs.items()}
                for c in range(ncore)]


def bf16(x):
    return np.asarray(x).astype(ml_dtypes.bfloat16)


def lightning_tables():
    start = 2.0 ** (-2.0 ** (-(math.log2(H0) - 3)))
    slopes = np.array([start ** (i + 1) for i in range(H0)], dtype=np.float64)
    lam = np.exp(-slopes)
    pos = np.arange(CH, dtype=np.float64)
    dq = lam[:, None] ** (pos + 1.0)
    dk = lam[:, None] ** (CH - 1.0 - pos)
    diff = pos[:, None] - pos[None, :]
    dmask = np.where(diff >= 0, lam[:, None, None] ** np.maximum(diff, 0.0), 0.0)
    lam_c = lam ** CH
    return lam.astype(np.float32), dq.astype(np.float32), dk.astype(np.float32), \
        dmask.astype(np.float32), lam_c.astype(np.float32)


def lightning_coef(core):
    lam, _, _, _, lam_c = lightning_tables()
    rho = core % 4
    base = (core // 4) * 4
    coef = np.zeros((NCORE, H0), np.float32)
    lam8 = lam_c.astype(np.float64) ** NCH
    for rp in range(rho):
        coef[base + rp] = (lam8 ** (rho - 1 - rp)).astype(np.float32)
    return np.broadcast_to(coef.reshape(1, NCORE * H0), (128, NCORE * H0)).copy()


class MK:

    def __init__(self, nc, fused=True):
        self.nc = nc
        self.ctx = Ctx(nc)
        self.p = self.ctx.p
        self.fused = fused
        self.bf = Pool(nc, "poolbf", 57 * 1024 + 512, BF16)
        self.f32 = Pool(nc, "poolf32", 22 * 1024 + 512, F32)
        self.PA = nc.alloc_psum_tensor("PA", [128, 512], F32)
        self.PB = nc.alloc_psum_tensor("PB", [128, 512], F32)
        self.PC = nc.alloc_psum_tensor("PC", [128, 512], F32)
        self.PD = nc.alloc_psum_tensor("PD", [128, 512], F32)
        self.PE_ = nc.alloc_psum_tensor("PE_", [128, 512], F32)
        self.PF = nc.alloc_psum_tensor("PF", [128, 512], F32)
        self.PT0 = nc.alloc_psum_tensor("PT0", [128, 1024], BF16)
        self.PT1 = nc.alloc_psum_tensor("PT1", [128, 1024], BF16)
        self.consts = {}

    def const(self, name, arr, dtype=None):
        if name in self.consts:
            return self.consts[name]
        arr = np.asarray(arr)
        if dtype == BF16:
            arr = bf16(arr)
        npdt = arr.dtype
        dt = BF16 if npdt == ml_dtypes.bfloat16 else (F32 if npdt == np.float32 else I32)
        assert arr.ndim == 2 and arr.shape[0] <= 128
        ap = self.ctx.inp("c_" + name, arr.shape, dt, lambda c, a=arr: a)
        pool = self.bf if dt == BF16 else self.f32
        v = pool.alloc(arr.shape[1], "c_" + name)
        rows, cols = arr.shape
        self.p._add("gpsimd",
                    lambda eng, ap=ap, v=v, rows=rows, cols=cols: eng.dma_start(
                        out=v.ap(slice(0, rows), slice(0, cols)), in_=ap),
                    r=[], w=[v.res()], kind="dma", tag="const_" + name)
        self.consts[name] = v
        return v

    def ident(self):
        return self.const("ident", np.eye(128, dtype=np.float32), BF16)

    def free_const(self, name):
        if name in self.consts:
            self.consts.pop(name).free()

    def mm(self, out_ap, lhsT_ap, rhs_ap, r, w, start=True, stop=True, tag="mm"):
        return self.p._add("tensor",
                           lambda eng: eng.matmul(out_ap, lhsT_ap, rhs_ap,
                                                  start=start, stop=stop),
                           r, w, "op", tag)

    def tr(self, out_ap, in_ap, r, w, tag="tr", np_=128):
        idv = self.ident()
        return self.p._add("tensor",
                           lambda eng, i=idv: eng.transpose(
                               out_ap, in_ap, i[0:np_, 0:np_]),
                           list(r) + [idv.res()], w, "op", tag)


def build_ln_xnT(mk, x_rows_ap, xnT, eps=1e-5, tag="ln0", x_dram_res=None,
                 loader=None):
    p, nc = mk.p, mk.nc
    xt = mk.f32.alloc(2048, tag + "_xt")
    xt2 = mk.f32.alloc(2048, tag + "_xt2")
    stats = mk.f32.alloc(16, tag + "_stats")
    sq = mk.f32.alloc(512, tag + "_sq")
    xn_tm = mk.bf.alloc(2048, tag + "_xntm")
    xr = x_dram_res or (whole(x_rows_ap.tensor) if x_rows_ap is not None else None)
    for i in range(8):
        buf = xt if i % 2 == 0 else xt2
        if loader is not None:
            loader(i, buf)
        else:
            p._add("sync", lambda eng, a=x_rows_ap, i=i, b=buf: eng.dma_start(
                out=b[:, :], in_=a[i * 128:(i + 1) * 128, :]),
                r=[xr], w=[buf.res()], kind="dma", tag=tag + "_ldx")
        p._add("vector", lambda eng, b=buf, s=stats: eng.reduce_sum(
            s[:, 0:1], b[:, :], axis=AX.X, negate=True),
            r=[buf.res()], w=[stats.res(slice(0, 1))], kind="op", tag=tag)
        p._add("vector", lambda eng, s=stats: eng.tensor_scalar_mul(
            s[:, 0:1], s[:, 0:1], 1.0 / D),
            r=[stats.res(slice(0, 1))], w=[stats.res(slice(0, 1))], kind="op", tag=tag)
        for ch in range(4):
            p._add("scalar", lambda eng, b=buf, s=stats, q=sq, ch=ch: eng.activation(
                q[:, :], b[:, ch * 512:(ch + 1) * 512], AF.Square,
                accum_out=s[:, 1 + ch:2 + ch]),
                r=[buf.res(slice(ch * 512, (ch + 1) * 512))],
                w=[sq.res(), stats.res(slice(1 + ch, 2 + ch))], kind="op", tag=tag)
        p._add("vector", lambda eng, s=stats: eng.tensor_tensor(
            s[:, 1:3], s[:, 1:3], s[:, 3:5], ALU.add),
            r=[stats.res(slice(1, 5))], w=[stats.res(slice(1, 3))], kind="op", tag=tag)
        p._add("vector", lambda eng, s=stats: eng.tensor_tensor(
            s[:, 1:2], s[:, 1:2], s[:, 2:3], ALU.add),
            r=[stats.res(slice(1, 3))], w=[stats.res(slice(1, 2))], kind="op", tag=tag)
        p._add("vector", lambda eng, s=stats: eng.tensor_tensor(
            s[:, 5:6], s[:, 0:1], s[:, 0:1], ALU.mult),
            r=[stats.res(slice(0, 1))], w=[stats.res(slice(5, 6))], kind="op", tag=tag)
        p._add("vector", lambda eng, s=stats: eng.tensor_scalar(
            s[:, 6:7], s[:, 1:2], 1.0 / D, float(eps), ALU.mult, ALU.add),
            r=[stats.res(slice(1, 2))], w=[stats.res(slice(6, 7))], kind="op", tag=tag)
        p._add("vector", lambda eng, s=stats: eng.tensor_tensor(
            s[:, 6:7], s[:, 6:7], s[:, 5:6], ALU.subtract),
            r=[stats.res(slice(5, 7))], w=[stats.res(slice(6, 7))], kind="op", tag=tag)
        p._add("scalar", lambda eng, s=stats: eng.activation(
            s[:, 7:8], s[:, 6:7], AF.Sqrt),
            r=[stats.res(slice(6, 7))], w=[stats.res(slice(7, 8))], kind="op", tag=tag)
        p._add("vector", lambda eng, s=stats: eng.reciprocal(
            s[:, 8:9], s[:, 7:8]),
            r=[stats.res(slice(7, 8))], w=[stats.res(slice(8, 9))], kind="op", tag=tag)
        p._add("vector", lambda eng, s=stats: eng.tensor_tensor(
            s[:, 9:10], s[:, 0:1], s[:, 8:9], ALU.mult),
            r=[stats.res(slice(0, 1)), stats.res(slice(8, 9))],
            w=[stats.res(slice(9, 10))], kind="op", tag=tag)
        p._add("scalar", lambda eng, b=buf, s=stats, o=xn_tm: eng.activation(
            o[:, :], b[:, :], AF.Identity, scale=s[:, 8:9], bias=s[:, 9:10]),
            r=[buf.res(), stats.res(slice(8, 10))], w=[xn_tm.res()], kind="op", tag=tag)
        for k in range(DB):
            pt = mk.PT0 if (k % 2 == 0) else mk.PT1
            psl = slice((k % 4) * 256, (k % 4) * 256 + 128)
            mk.tr(pt[:, 0:128], xn_tm[:, k * 128:(k + 1) * 128],
                  r=[xn_tm.res(slice(k * 128, (k + 1) * 128))],
                  w=[whole(pt)], tag=tag + "_tr")
            p._add("vector", lambda eng, pt=pt, k=k, i=i, o=xnT: eng.tensor_copy(
                o[:, k * 1024 + i * 128: k * 1024 + (i + 1) * 128], pt[:, 0:128]),
                r=[whole(pt)], w=[xnT.res(slice(k * 1024 + i * 128, k * 1024 + (i + 1) * 128))],
                kind="op", tag=tag + "_trevac")
    for t in (xt, xt2, stats, sq, xn_tm):
        t.free()


def build_lightning(mk, xnT, wqkvT_ap, bqkv_c, attnT, cc_rg, tag="lg"):
    p, nc = mk.p, mk.nc
    lam, dq, dk, dmask, lam_c = lightning_tables()
    dq_arr = np.zeros((128, H0 * CH), np.float32)
    for h in range(H0):
        dq_arr[:, h * CH:(h + 1) * CH] = dq[h][None, :]
    dq_t = mk.const("lg_dq", dq_arr, BF16)
    dk_t = mk.const("lg_dk", dk.T, F32)
    dmT = mk.const("lg_dmT", np.ascontiguousarray(
        dmask.transpose(2, 0, 1).reshape(CH, H0 * CH)), BF16)
    coef_ap = mk.ctx.inp("lg_coef", (128, NCORE * H0), F32, lightning_coef)
    coefv = mk.f32.alloc(NCORE * H0, "lg_coef")
    p._add("gpsimd", lambda eng: eng.dma_start(out=coefv[:, :], in_=coef_ap),
           r=[], w=[coefv.res()], kind="dma", tag=tag)

    import os
    HPP = int(os.environ.get('HPP', '4'))
    cc_in = mk.ctx.dram("lg_ccin", [HPP * 128, 128], F32)
    cc_out = mk.ctx.dram("lg_ccout", [NCORE * HPP * 128, 128], F32)

    kT = mk.bf.alloc(HPP * 1024, "lg_kT")
    vb = mk.bf.alloc(HPP * 1024, "lg_vb")
    S = mk.f32.alloc(128, "lg_S")
    Sbf = mk.bf.alloc(128, "lg_Sbf")
    A_sb = mk.f32.alloc(128, "lg_A")
    wbuf = [mk.bf.alloc(128, f"lg_w{i}") for i in range(4)]
    scT2 = [mk.bf.alloc(128, f"lg_scT{i}") for i in range(2)]
    qT = mk.bf.alloc(1024, "lg_qT")
    qs2 = [mk.bf.alloc(128, f"lg_qs{i}") for i in range(2)]
    ktmp2 = [mk.bf.alloc(128, f"lg_ktmp{i}") for i in range(2)]
    Ae = mk.f32.alloc(128, "lg_Ae")

    def project(cols, bias_idx, dst_view, dst_c0, tag2):
        for kb in range(DB):
            wb = wbuf[kb % 4]
            p._add("gpsimd", lambda eng, kb=kb, wb=wb: eng.dma_start(
                out=wb[:, :], in_=wqkvT_ap[kb * 128:(kb + 1) * 128, cols:cols + 128]),
                r=[], w=[wb.res()], kind="dma", tag=tag2 + "_wdma")
            for th in range(2):
                ps = mk.PA if th == 0 else mk.PB
                mk.mm(ps[:, :], wb[:, :],
                      xnT[:, kb * 1024 + th * 512: kb * 1024 + (th + 1) * 512],
                      r=[wb.res(), xnT.res(slice(kb * 1024 + th * 512,
                                                 kb * 1024 + (th + 1) * 512))],
                      w=[whole(ps)], start=(kb == 0), stop=(kb == DB - 1), tag=tag2)
        for th in range(2):
            ps = mk.PA if th == 0 else mk.PB
            p._add("scalar", lambda eng, ps=ps, th=th: eng.activation(
                dst_view[:, dst_c0 + th * 512: dst_c0 + (th + 1) * 512], ps[:, :],
                AF.Identity, bias=bqkv_c[:, bias_idx:bias_idx + 1]),
                r=[whole(ps), bqkv_c.res(slice(bias_idx, bias_idx + 1))],
                w=[dst_view.res(slice(dst_c0 + th * 512, dst_c0 + (th + 1) * 512))],
                kind="op", tag=tag2 + "_evac")

    def kv_chunk(hl, h, j, dst_psum):
        jc = slice(hl * 1024 + j * 128, hl * 1024 + (j + 1) * 128)
        pt = mk.PT0 if j % 2 == 0 else mk.PT1
        ktmp = ktmp2[j % 2]
        mk.tr(pt[:, 0:128], kT[:, jc], r=[kT.res(jc)], w=[whole(pt)],
              tag=tag + "_ktr")
        p._add("scalar", lambda eng, pt=pt, h=h, ktmp=ktmp: eng.activation(
            ktmp[:, :], pt[:, 0:128], AF.Copy, scale=dk_t[:, h:h + 1]),
            r=[whole(pt), dk_t.res(slice(h, h + 1))], w=[ktmp.res()],
            kind="op", tag=tag + "_ks")
        mk.mm(dst_psum[:, 0:128], ktmp[:, :], vb[:, jc],
              r=[ktmp.res(), vb.res(jc)], w=[whole(dst_psum)], tag=tag + "_kv")

    for pas in range(H0 // HPP):
        h0 = pas * HPP
        for hl in range(HPP):
            h = h0 + hl
            project(D + h * 128, DB + h, kT, hl * 1024, f"{tag}_k{h}")
            project(2 * D + h * 128, 2 * DB + h, vb, hl * 1024, f"{tag}_v{h}")
            for j in range(NCH):
                jc = slice(hl * 1024 + j * 128, hl * 1024 + (j + 1) * 128)
                pt = mk.PT0 if j % 2 == 0 else mk.PT1
                mk.tr(pt[:, 0:128], vb[:, jc], r=[vb.res(jc)], w=[whole(pt)],
                      tag=tag + "_vtr")
                p._add("vector", lambda eng, pt=pt, jc=jc: eng.tensor_copy(
                    vb[:, jc], pt[:, 0:128]),
                    r=[whole(pt)], w=[vb.res(jc)], kind="op", tag=tag + "_vte")
            p._add("vector", lambda eng: eng.memset(S[:, :], 0.0),
                   r=[], w=[S.res()], kind="op", tag=tag)
            for j in range(NCH):
                pkv = mk.PC if j % 2 == 0 else mk.PF
                kv_chunk(hl, h, j, pkv)
                p._add("vector", lambda eng, h=h, pkv=pkv: eng.scalar_tensor_tensor(
                    S[:, :], S[:, :], float(lam_c[h]), pkv[:, 0:128],
                    ALU.mult, ALU.add),
                    r=[S.res(), whole(pkv)], w=[S.res()], kind="op",
                    tag=tag + "_scan")
            p._add("vector", lambda eng: eng.tensor_copy(A_sb[:, :], S[:, :]),
                   r=[S.res()], w=[A_sb.res()], kind="op", tag=tag)
            p._add("sync", lambda eng, hl=hl: eng.dma_start(
                out=cc_in.tensor.ap()[hl * 128:(hl + 1) * 128, :], in_=A_sb[:, :]),
                r=[A_sb.res()], w=[whole(cc_in.tensor)], kind="dma",
                tag=tag + "_Aout")

        p._add("gpsimd", lambda eng: eng.collective_compute(
            "AllGather", ALU.bypass, replica_groups=cc_rg,
            ins=[cc_in.tensor.ap().opt()], outs=[cc_out.tensor.ap().opt()]),
            r=[whole(cc_in.tensor)], w=[whole(cc_out.tensor)], kind="cc",
            tag=tag + "_cc")

        for hl in range(HPP):
            h = h0 + hl
            project(h * 128, h, qT, 0, f"{tag}_q{h}")
            p._add("vector", lambda eng: eng.memset(S[:, :], 0.0),
                   r=[], w=[S.res()], kind="op", tag=tag)
            for r_ in range(NCORE):
                p._add("sync", lambda eng, r_=r_, hl=hl: eng.dma_start(
                    out=Ae[:, :],
                    in_=cc_out.tensor.ap()[(r_ * HPP + hl) * 128:
                                           (r_ * HPP + hl + 1) * 128, :]),
                    r=[whole(cc_out.tensor)], w=[Ae.res()], kind="dma",
                    tag=tag + "_Ald")
                p._add("vector", lambda eng, r_=r_, h=h: eng.scalar_tensor_tensor(
                    S[:, :], Ae[:, :], coefv[:, r_ * H0 + h:r_ * H0 + h + 1],
                    S[:, :], ALU.mult, ALU.add),
                    r=[Ae.res(), coefv.res(slice(r_ * H0 + h, r_ * H0 + h + 1)),
                       S.res()],
                    w=[S.res()], kind="op", tag=tag + "_sent")
            for j in range(NCH):
                jc = slice(hl * 1024 + j * 128, hl * 1024 + (j + 1) * 128)
                qc = slice(j * 128, (j + 1) * 128)
                oc = slice(h * 1024 + j * 128, h * 1024 + (j + 1) * 128)
                p._add("scalar", lambda eng: eng.activation(
                    Sbf[:, :], S[:, :], AF.Copy),
                    r=[S.res()], w=[Sbf.res()], kind="op", tag=tag + "_sbf")
                qs = qs2[j % 2]
                p._add("vector", lambda eng, h=h, qc=qc, qs=qs: eng.tensor_tensor(
                    qs[:, :], qT[:, qc], dq_t[:, h * 128:(h + 1) * 128], ALU.mult),
                    r=[qT.res(qc), dq_t.res(slice(h * 128, (h + 1) * 128))],
                    w=[qs.res()], kind="op", tag=tag + "_qs")
                mk.mm(mk.PD[:, 0:128], kT[:, jc], qT[:, qc],
                      r=[kT.res(jc), qT.res(qc)], w=[whole(mk.PD)],
                      tag=tag + "_sc")
                scT = scT2[j % 2]
                p._add("vector", lambda eng, h=h, scT=scT: eng.tensor_tensor(
                    scT[:, :], mk.PD[:, 0:128], dmT[:, h * 128:(h + 1) * 128],
                    ALU.mult),
                    r=[whole(mk.PD), dmT.res(slice(h * 128, (h + 1) * 128))],
                    w=[scT.res()], kind="op", tag=tag + "_msk")
                mk.mm(mk.PE_[:, 0:128], vb[:, jc], scT[:, :],
                      r=[vb.res(jc), scT.res()], w=[whole(mk.PE_)],
                      start=True, stop=False, tag=tag + "_intra")
                mk.mm(mk.PE_[:, 0:128], Sbf[:, :], qs[:, :],
                      r=[Sbf.res(), qs.res()], w=[whole(mk.PE_)],
                      start=False, stop=True, tag=tag + "_inter")
                p._add("vector", lambda eng, oc=oc: eng.tensor_copy(
                    attnT[:, oc], mk.PE_[:, 0:128]),
                    r=[whole(mk.PE_)], w=[attnT.res(oc)], kind="op",
                    tag=tag + "_oevac")
                pkv = mk.PC if j % 2 == 0 else mk.PF
                kv_chunk(hl, h, j, pkv)
                p._add("vector", lambda eng, h=h, pkv=pkv: eng.scalar_tensor_tensor(
                    S[:, :], S[:, :], float(lam_c[h]), pkv[:, 0:128],
                    ALU.mult, ALU.add),
                    r=[S.res(), whole(pkv)], w=[S.res()], kind="op",
                    tag=tag + "_scan2")
    if not os.environ.get("KEEPBUFS"):
        for t in (kT, vb, S, Sbf, A_sb, scT2[0], scT2[1], qT, qs2[0], qs2[1],
              ktmp2[0], ktmp2[1], Ae, coefv):
            t.free()
        for wbv in wbuf:
            wbv.free()



def load_wcol(mk, dram_ap, col0, width, kblocks, dst, queue="gpsimd", tag="wld",
              src_res=None, split=1):
    per = (kblocks + split - 1) // split
    for s0 in range(0, kblocks, per):
        s1 = min(kblocks, s0 + per)
        src = dram_ap[s0 * 128:s1 * 128, col0:col0 + width].rearrange(
            "(kb p) f -> p kb f", kb=s1 - s0)
        dstap = dst[:, s0 * width:s1 * width].rearrange(
            "p (kb f) -> p kb f", kb=s1 - s0)
        mk.p._add(queue, lambda eng, src=src, dstap=dstap: eng.dma_start(
            out=dstap, in_=src),
            r=[src_res or whole(dram_ap.tensor)],
            w=[dst.res(slice(s0 * width, s1 * width))], kind="dma", tag=tag)


def trf32(mk, out_ap, in_ap, r, w, tag="trf"):
    idv = mk.const("identf32", np.eye(128, dtype=np.float32))
    return mk.p._add("tensor",
                     lambda eng, i=idv: eng.transpose(out_ap, in_ap, i[0:128, :]),
                     list(r) + [idv.res()], w, "op", tag)


def ffn_expert(mk, bufs, gate_ap, up_ap, down_ap, fblocks, cb_view, dst_update,
               gcol0, tag2, src_res_g=None, src_res_u=None, src_res_d=None,
               ucol0=None):
    if ucol0 is None:
        ucol0 = gcol0
    for f0 in range(0, fblocks, FB0):
        fpass = min(FB0, fblocks - f0)
        _ffn_pass(mk, bufs, gate_ap, up_ap, down_ap, f0, fpass, fblocks,
                  cb_view, dst_update, gcol0, ucol0, tag2, src_res_g,
                  src_res_u, src_res_d)

def _ffn_pass(mk, bufs, gate_ap, up_ap, down_ap, f0, fpass, fblocks, cb_view,
              dst_update, gcol0, ucol0, tag2, src_res_g, src_res_u,
              src_res_d):
    p = mk.p
    xnT, h_all, sg, t1, wg, wu, wd = (bufs["xnT"], bufs["h"], bufs["sg"],
                                      bufs["t1"], bufs["wg"], bufs["wu"],
                                      bufs["wd"])
    for fb in range(fpass):
        wgv, wuv = wg[fb % 2], wu[fb % 2]
        load_wcol(mk, gate_ap, gcol0 + (f0 + fb) * 128, 128, DB, wgv,
                  tag=tag2 + "_g", src_res=src_res_g)
        load_wcol(mk, up_ap, ucol0 + (f0 + fb) * 128, 128, DB, wuv,
                  tag=tag2 + "_u", src_res=src_res_u)
        for th in range(2):
            pg = mk.PC if th == 0 else mk.PD
            pu = mk.PE_ if th == 0 else mk.PF
            for kb in range(DB):
                rc = slice(kb * 1024 + th * 512, kb * 1024 + (th + 1) * 512)
                mk.mm(pg[:, :], wgv[:, kb * 128:(kb + 1) * 128], xnT[:, rc],
                      r=[wgv.res(slice(kb * 128, (kb + 1) * 128)), xnT.res(rc)],
                      w=[whole(pg)], start=(kb == 0), stop=(kb == DB - 1),
                      tag=tag2 + "_gm")
                mk.mm(pu[:, :], wuv[:, kb * 128:(kb + 1) * 128], xnT[:, rc],
                      r=[wuv.res(slice(kb * 128, (kb + 1) * 128)), xnT.res(rc)],
                      w=[whole(pu)], start=(kb == 0), stop=(kb == DB - 1),
                      tag=tag2 + "_um")
            p._add("scalar", lambda eng, pg=pg: eng.activation(
                sg[:, :], pg[:, :], AF.Silu),
                r=[whole(pg)], w=[sg.res()], kind="op", tag=tag2 + "_silu")
            hc = slice(fb * 1024 + th * 512, fb * 1024 + (th + 1) * 512)
            if cb_view is not None:
                p._add("vector", lambda eng, pu=pu: eng.tensor_tensor(
                    t1[:, :], pu[:, :], sg[:, :], ALU.mult),
                    r=[whole(pu), sg.res()], w=[t1.res()], kind="op",
                    tag=tag2 + "_hu")
                p._add("vector", lambda eng, hc=hc, th=th: eng.tensor_tensor(
                    h_all[:, hc], t1[:, :],
                    cb_view[:, th * 512:(th + 1) * 512], ALU.mult),
                    r=[t1.res(), cb_view.res(slice(th * 512, (th + 1) * 512))],
                    w=[h_all.res(hc)], kind="op", tag=tag2 + "_hc")
            else:
                p._add("vector", lambda eng, pu=pu, hc=hc: eng.tensor_tensor(
                    h_all[:, hc], pu[:, :], sg[:, :], ALU.mult),
                    r=[whole(pu), sg.res()], w=[h_all.res(hc)], kind="op",
                    tag=tag2 + "_hu")
    for db in range(DB):
        wdv = wd[db % 2]
        src = down_ap[f0 * 128:(f0 + fpass) * 128, db * 128:(db + 1) * 128]\
            .rearrange("(kb p) f -> p kb f", kb=fpass)
        dstap = wdv[:, 0:fpass * 128].rearrange("p (kb f) -> p kb f", kb=fpass)
        mk.p._add("gpsimd", lambda eng, src=src, dstap=dstap: eng.dma_start(
            out=dstap, in_=src),
            r=[src_res_d or whole(down_ap.tensor)], w=[wdv.res()],
            kind="dma", tag=tag2 + "_d")
        for th in range(2):
            pd_ = mk.PC if th == 0 else mk.PD
            for fb in range(fpass):
                rc = slice(fb * 1024 + th * 512, fb * 1024 + (th + 1) * 512)
                mk.mm(pd_[:, :], wdv[:, fb * 128:(fb + 1) * 128], h_all[:, rc],
                      r=[wdv.res(slice(fb * 128, (fb + 1) * 128)),
                         h_all.res(rc)],
                      w=[whole(pd_)], start=(fb == 0), stop=(fb == fpass - 1),
                      tag=tag2 + "_dm")
            dc = slice(db * 1024 + th * 512, db * 1024 + (th + 1) * 512)
            p._add("vector", lambda eng, pd_=pd_, dc=dc: eng.tensor_tensor(
                dst_update[:, dc], dst_update[:, dc], pd_[:, :], ALU.add),
                r=[dst_update.res(dc), whole(pd_)], w=[dst_update.res(dc)],
                kind="op", tag=tag2 + "_dacc")



def build_l0_finish(mk, xnT, attnT, x_rows_ap, x_acc, woutT_ap, bout_c,
                    wrouter_ap, weg_ap, weu_ap, wed_ap, wsg_ap, wsu_ap, wsd_ap,
                    tag="l0f", stages=("init", "outproj", "router", "moe", "shared")):
    p, nc = mk.p, mk.nc
    import os

    xt = mk.f32.alloc(1024, tag + "_xt")
    xt2 = mk.f32.alloc(1024, tag + "_xt2")
    for kb in range(DB):
        buf = xt if kb % 2 == 0 else xt2
        p._add("sync", lambda eng, kb=kb, b=buf: eng.dma_start(
            out=b[:, :], in_=x_rows_ap[kb * 128:(kb + 1) * 128, :]),
            r=[whole(x_rows_ap.tensor)], w=[buf.res()], kind="dma",
            tag=tag + "_ldx")
        dc = slice(kb * 1024, (kb + 1) * 1024)
        p._add("scalar", lambda eng, b=buf, dc=dc, kb=kb: eng.activation(
            x_acc[:, dc], b[:, :], AF.Identity, bias=bout_c[:, kb:kb + 1]),
            r=[buf.res(), bout_c.res(slice(kb, kb + 1))], w=[x_acc.res(dc)],
            kind="op", tag=tag + "_xev")
    xt.free(); xt2.free()

    wo = [mk.bf.alloc(DB * 128, tag + f"_wo{i}") for i in range(2)]
    lvl = int(os.environ.get("OUTPROJ_LVL", "3"))
    if os.environ.get("EXTRA_DMA"):
        spare = mk.bf.alloc(2048, "spare_nofree")
        mode_ = os.environ["EXTRA_DMA"]
        if mode_ == "memset":
            p._add("gpsimd", lambda eng: eng.memset(spare[:, 0:128], 0.0),
                   r=[], w=[spare.res(slice(0, 128))], kind="op",
                   tag=tag + "_extramemset")
        elif mode_ == "sync":
            p._add("sync", lambda eng: eng.dma_start(
                out=spare[:, 0:128], in_=woutT_ap[0:128, 0:128]),
                r=[whole(woutT_ap.tensor)], w=[spare.res(slice(0, 128))],
                kind="dma", tag=tag + "_extrasync")
        elif mode_ == "plain":
            p._add("gpsimd", lambda eng: eng.dma_start(
                out=spare[:, 0:128], in_=woutT_ap[0:128, 0:128]),
                r=[whole(woutT_ap.tensor)], w=[spare.res(slice(0, 128))],
                kind="dma", tag=tag + "_extraplain")
        lvl = 0
    wsplit = int(os.environ.get("WSPLIT", "1"))
    if "outproj" in stages:
        for fb in range(DB):
            if lvl < 1:
                break
            wv = wo[fb % 2]
            load_wcol(mk, woutT_ap, fb * 128, 128, DB, wv, tag=tag + "_wodma", split=wsplit)
            if lvl < 2:
                continue
            for th in range(2):
                ps = mk.PC if th == 0 else mk.PD
                for ab in range(DB):
                    rc = slice(ab * 1024 + th * 512, ab * 1024 + (th + 1) * 512)
                    mk.mm(ps[:, :], wv[:, ab * 128:(ab + 1) * 128], attnT[:, rc],
                          r=[wv.res(slice(ab * 128, (ab + 1) * 128)), attnT.res(rc)],
                          w=[whole(ps)], start=(ab == 0), stop=(ab == DB - 1),
                          tag=tag + "_wo")
                dc = slice(fb * 1024 + th * 512, fb * 1024 + (th + 1) * 512)
                if lvl >= 3:
                    p._add("vector", lambda eng, ps=ps, dc=dc: eng.tensor_tensor(
                        x_acc[:, dc], x_acc[:, dc], ps[:, :], ALU.add),
                        r=[x_acc.res(dc), whole(ps)], w=[x_acc.res(dc)], kind="op",
                        tag=tag + "_woacc")
    for w_ in wo:
        w_.free()
    attnT.free()
    if "router" not in stages:
        return

    wr = mk.bf.alloc(DB * 8, tag + "_wr")
    load_wcol(mk, wrouter_ap, 0, E0, DB, wr, tag=tag + "_wrdma")
    logit = mk.f32.alloc(16, tag + "_lg")
    prob = mk.f32.alloc(8, tag + "_pr")
    rk = mk.f32.alloc(8, tag + "_rk")
    cmb = mk.bf.alloc(8, tag + "_cmb")
    cT_all = mk.bf.alloc(1024, tag + "_cT")
    for i in range(8):
        ps = mk.PE_
        for kb in range(DB):
            mk.mm(ps[:, 0:E0], xnT[:, kb * 1024 + i * 128: kb * 1024 + (i + 1) * 128],
                  wr[:, kb * 8:(kb + 1) * 8],
                  r=[xnT.res(slice(kb * 1024 + i * 128, kb * 1024 + (i + 1) * 128)),
                     wr.res(slice(kb * 8, (kb + 1) * 8))],
                  w=[whole(ps)], start=(kb == 0), stop=(kb == DB - 1),
                  tag=tag + "_rt")
        p._add("vector", lambda eng, ps=ps: eng.reduce_max(
            logit[:, 8:9], ps[:, 0:E0], axis=AX.X, negate=True),
            r=[whole(ps)], w=[logit.res(slice(8, 9))], kind="op", tag=tag)
        p._add("scalar", lambda eng, ps=ps: eng.activation(
            logit[:, 0:E0], ps[:, 0:E0], AF.Exp, bias=logit[:, 8:9],
            accum_out=logit[:, 9:10]),
            r=[whole(ps), logit.res(slice(8, 9))],
            w=[logit.res(slice(0, E0)), logit.res(slice(9, 10))], kind="op",
            tag=tag + "_exp")
        p._add("vector", lambda eng: eng.reciprocal(
            logit[:, 10:11], logit[:, 9:10]),
            r=[logit.res(slice(9, 10))], w=[logit.res(slice(10, 11))],
            kind="op", tag=tag)
        p._add("vector", lambda eng: eng.tensor_scalar(
            prob[:, 0:E0], logit[:, 0:E0], logit[:, 10:11], None, ALU.mult),
            r=[logit.res(slice(0, E0)), logit.res(slice(10, 11))],
            w=[prob.res()], kind="op", tag=tag)
        for e in range(E0):
            p._add("vector", lambda eng, e=e: eng.tensor_scalar(
                rk[:, 0:E0], prob[:, 0:E0], prob[:, e:e + 1], None, ALU.is_gt),
                r=[prob.res()], w=[rk.res(slice(0, E0))], kind="op", tag=tag)
            p._add("vector", lambda eng: eng.reduce_sum(
                logit[:, 11:12], rk[:, 0:E0], axis=AX.X),
                r=[rk.res(slice(0, E0))], w=[logit.res(slice(11, 12))],
                kind="op", tag=tag)
            if e > 0:
                p._add("vector", lambda eng, e=e: eng.tensor_scalar(
                    rk[:, 0:e], prob[:, 0:e], prob[:, e:e + 1], None,
                    ALU.is_equal),
                    r=[prob.res()], w=[rk.res(slice(0, e))], kind="op", tag=tag)
                p._add("vector", lambda eng, e=e: eng.reduce_sum(
                    logit[:, 12:13], rk[:, 0:e], axis=AX.X),
                    r=[rk.res(slice(0, e))], w=[logit.res(slice(12, 13))],
                    kind="op", tag=tag)
                p._add("vector", lambda eng: eng.tensor_tensor(
                    logit[:, 11:12], logit[:, 11:12], logit[:, 12:13], ALU.add),
                    r=[logit.res(slice(11, 13))], w=[logit.res(slice(11, 12))],
                    kind="op", tag=tag)
            p._add("vector", lambda eng: eng.tensor_scalar(
                logit[:, 13:14], logit[:, 11:12], 2.0, None, ALU.is_lt),
                r=[logit.res(slice(11, 12))], w=[logit.res(slice(13, 14))],
                kind="op", tag=tag)
            p._add("vector", lambda eng, e=e: eng.tensor_scalar(
                cmb[:, e:e + 1], prob[:, e:e + 1], logit[:, 13:14], None,
                ALU.mult),
                r=[prob.res(), logit.res(slice(13, 14))],
                w=[cmb.res(slice(e, e + 1))], kind="op", tag=tag)
        pt = mk.PT0 if i % 2 == 0 else mk.PT1
        mk.tr(pt[0:E0, 0:128], cmb[:, 0:E0], r=[cmb.res()], w=[whole(pt)],
              tag=tag + "_ctr")
        p._add("vector", lambda eng, pt=pt, i=i: eng.tensor_copy(
            cT_all.ap(slice(0, E0), slice(i * 128, (i + 1) * 128)),
            pt[0:E0, 0:128]),
            r=[whole(pt)], w=[cT_all.res(slice(i * 128, (i + 1) * 128))],
            kind="op", tag=tag + "_cte")
    wr.free(); logit.free(); prob.free(); rk.free()

    sel = np.zeros((E0, E0 * 128), np.float32)
    for e in range(E0):
        sel[e, e * 128:(e + 1) * 128] = 1.0
    selc = mk.const("l0_sel", sel, BF16)

    h_all = mk.bf.alloc(FB0 * 1024, tag + "_h")
    cb = mk.bf.alloc(1024, tag + "_cb")
    sg = mk.bf.alloc(512, tag + "_sg")
    t1 = mk.bf.alloc(512, tag + "_t1")
    wg = [mk.bf.alloc(DB * 128, tag + f"_wg{i}") for i in range(2)]
    wu = [mk.bf.alloc(DB * 128, tag + f"_wu{i}") for i in range(2)]
    wd = [mk.bf.alloc(FB0 * 128, tag + f"_wd{i}") for i in range(2)]
    ffnbufs = {"xnT": xnT, "h": h_all, "sg": sg, "t1": t1, "wg": wg, "wu": wu,
               "wd": wd}

    for e in (range(E0) if "moe" in stages else []):
        for th in range(2):
            ps = mk.PF
            mk.mm(ps[:, :], selc[0:E0, e * 128:(e + 1) * 128],
                  cT_all[0:E0, th * 512:(th + 1) * 512],
                  r=[selc.res(slice(e * 128, (e + 1) * 128)),
                     cT_all.res(slice(th * 512, (th + 1) * 512))],
                  w=[whole(ps)], tag=tag + "_cbm")
            p._add("vector", lambda eng, ps=ps, th=th: eng.tensor_copy(
                cb[:, th * 512:(th + 1) * 512], ps[:, :]),
                r=[whole(ps)], w=[cb.res(slice(th * 512, (th + 1) * 512))],
                kind="op", tag=tag + "_cbe")
        ffn_expert(mk, ffnbufs, weg_ap[e], weu_ap[e], wed_ap[e], FB0, cb,
                   x_acc, 0, f"{tag}_e{e}",
                   src_res_g=whole(weg_ap[e].tensor),
                   src_res_u=whole(weu_ap[e].tensor),
                   src_res_d=whole(wed_ap[e].tensor))

    if "shared" in stages:
        ffn_expert(mk, ffnbufs, wsg_ap, wsu_ap, wsd_ap, SB0, None, x_acc, 0,
                   tag + "_sh")

    for t in (h_all, cb, sg, t1, cT_all):
        t.free()
    for lst in (wg, wu, wd):
        for w_ in lst:
            w_.free()




def acc_loader(mk, x_acc, tag="lnacc"):
    def loader(i, buf):
        for kb in range(DB):
            ps = mk.PA if kb % 2 == 0 else mk.PB
            sc = slice(kb * 1024 + i * 128, kb * 1024 + (i + 1) * 128)
            trf32(mk, ps[:, 0:128], x_acc[:, sc],
                  r=[x_acc.res(sc)], w=[whole(ps)], tag=tag + "_tr")
            mk.p._add("vector", lambda eng, ps=ps, kb=kb, b=buf: eng.tensor_copy(
                b[:, kb * 128:(kb + 1) * 128], ps[:, 0:128]),
                r=[whole(ps)], w=[buf.res(slice(kb * 128, (kb + 1) * 128))],
                kind="op", tag=tag + "_ev")
    return loader


def build_l1_attn(mk, xn1T, x_acc, wqT_ap, wkT_ap, wvT_ap, woT_ap,
                  vemb0_ap, vemb1_ap, tau_ap, kvoff_ap, qtb_ap, cc_rg,
                  tag="l1a"):
    p, nc = mk.p, mk.nc
    SEQ = N
    NT = SEQ // 128

    maskc = mk.const("l1_mask", np.tril(np.ones((128, 128), np.float32), -1), BF16)
    negbig = mk.const("l1_negbig",
                      np.triu(np.full((128, 128), -1e30, np.float32), 0), BF16)
    kwT_lo = mk.bf.alloc(SEQ, tag + "_kwlo")
    kwT_hi = mk.bf.alloc(SEQ, tag + "_kwhi")
    qwT_lo = mk.bf.alloc(SEQ, tag + "_qwlo")
    qwT_hi = mk.bf.alloc(SEQ, tag + "_qwhi")
    s_row2 = [mk.bf.alloc(SEQ, tag + f"_srow{i}") for i in range(2)]
    ptb2 = [mk.bf.alloc(1024, tag + f"_ptb{i}") for i in range(2)]
    onesf = mk.const("l1_onesf", np.ones((1, 128), np.float32))

    taut = mk.f32.alloc(8, tag + "_tau")
    p._add("sync", lambda eng: eng.dma_start(out=taut[0:1, 0:1], in_=tau_ap),
           r=[whole(tau_ap.tensor)], w=[taut.res(slice(0, 1))], kind="dma", tag=tag)
    p._add("vector", lambda eng: eng.reciprocal(taut[0:1, 1:2], taut[0:1, 0:1]),
           r=[taut.res(slice(0, 1))], w=[taut.res(slice(1, 2))], kind="op", tag=tag)
    invt = mk.f32.alloc(4, tag + "_invt")
    p._add("tensor", lambda eng: eng.matmul(
        mk.PF[:, 0:1], onesf[0:1, :], taut[0:1, 1:2], start=True, stop=True),
        [onesf.res(), taut.res(slice(1, 2))], [whole(mk.PF)], "op", tag)
    p._add("vector", lambda eng: eng.tensor_copy(invt[:, 0:1], mk.PF[:, 0:1]),
           r=[whole(mk.PF)], w=[invt.res(slice(0, 1))], kind="op", tag=tag)
    p._add("vector", lambda eng: eng.tensor_scalar(
        invt[:, 1:2], invt[:, 0:1], 1.0 / 16.0, None, ALU.mult),
        r=[invt.res(slice(0, 1))], w=[invt.res(slice(1, 2))], kind="op", tag=tag)
    wq = mk.bf.alloc(DB * 128, tag + "_wq")
    wk = mk.bf.alloc(DB * 32, tag + "_wk")
    wv_ = mk.bf.alloc(DB * 32, tag + "_wv")
    load_wcol(mk, wqT_ap, 0, 128, DB, wq, tag=tag + "_wq")
    load_wcol(mk, wkT_ap, 0, 32, DB, wk, tag=tag + "_wk")
    load_wcol(mk, wvT_ap, 0, 32, DB, wv_, tag=tag + "_wv")
    qs_sb = mk.bf.alloc(1024, tag + "_qs")
    kT_sb = mk.bf.alloc(1024, tag + "_kT")
    vT_sb = mk.bf.alloc(1024, tag + "_vT")
    for th in range(2):
        pq = mk.PA if th == 0 else mk.PB
        pk = mk.PC if th == 0 else mk.PD
        pv = mk.PE_ if th == 0 else mk.PF
        for kb in range(DB):
            rc = slice(kb * 1024 + th * 512, kb * 1024 + (th + 1) * 512)
            mk.mm(pq[:, :], wq[:, kb * 128:(kb + 1) * 128], xn1T[:, rc],
                  r=[wq.res(slice(kb * 128, (kb + 1) * 128)), xn1T.res(rc)],
                  w=[whole(pq)], start=(kb == 0), stop=(kb == DB - 1), tag=tag + "_q")
            mk.mm(pk[0:32, :], wk[:, kb * 32:(kb + 1) * 32], xn1T[:, rc],
                  r=[wk.res(slice(kb * 32, (kb + 1) * 32)), xn1T.res(rc)],
                  w=[whole(pk)], start=(kb == 0), stop=(kb == DB - 1), tag=tag + "_k")
            mk.mm(pv[0:32, :], wv_[:, kb * 32:(kb + 1) * 32], xn1T[:, rc],
                  r=[wv_.res(slice(kb * 32, (kb + 1) * 32)), xn1T.res(rc)],
                  w=[whole(pv)], start=(kb == 0), stop=(kb == DB - 1), tag=tag + "_v")
        tc = slice(th * 512, (th + 1) * 512)
        p._add("scalar", lambda eng, pq=pq, tc=tc: eng.activation(
            qs_sb[:, tc], pq[:, :], AF.Tanh, scale=invt[:, 0:1]),
            r=[whole(pq), invt.res(slice(0, 1))], w=[qs_sb.res(tc)],
            kind="op", tag=tag + "_qevac")
        p._add("scalar", lambda eng, pk=pk, tc=tc: eng.activation(
            kT_sb.ap(slice(0, 32), tc), pk[0:32, :], AF.Tanh,
            scale=invt[0:32, 0:1]),
            r=[whole(pk), invt.res(slice(0, 1))], w=[kT_sb.res(tc)],
            kind="op", tag=tag + "_kevac")
        p._add("scalar", lambda eng, pv=pv, tc=tc: eng.activation(
            vT_sb.ap(slice(0, 32), tc), pv[0:32, :], AF.Sigmoid,
            scale=invt[0:32, 0:1]),
            r=[whole(pv), invt.res(slice(0, 1))], w=[vT_sb.res(tc)],
            kind="op", tag=tag + "_vevac")
    p._add("vector", lambda eng: eng.tensor_scalar(
        qs_sb[:, :], qs_sb[:, :], invt[:, 1:2], None, ALU.mult),
        r=[qs_sb.res(), invt.res(slice(1, 2))], w=[qs_sb.res()], kind="op",
        tag=tag + "_qscale")
    for w_ in (wq, wk, wv_):
        w_.free()
    qTpad = mk.bf.alloc(31 + SEQ, tag + "_qTpad")
    kTpad = mk.bf.alloc(31 + SEQ, tag + "_kTpad")

    class _RowAt:

        def __init__(self, host):
            self.host = host

        def ap(self, psl=None, csl=slice(None)):
            a, b_, _ = csl.indices(SEQ)
            return self.host.pool.t[32:33, self.host.c0 + a:self.host.c0 + b_]

        def res(self, csl=slice(None)):
            a, b_, _ = csl.indices(SEQ)
            return (self.host.pool.t, self.host.c0 + a, self.host.c0 + b_)
    rrow_ap_in = mk.ctx.inp("c_l1_rrow", (1, SEQ), BF16,
                            lambda c: bf16(np.arange(SEQ, dtype=np.float32)[None, :]))
    invi_ap_in = mk.ctx.inp("c_l1_invi", (1, SEQ), BF16,
                            lambda c: bf16((1.0 / (np.arange(SEQ, dtype=np.float64)
                                                   + 1.0)).astype(np.float32)[None, :]))
    rrow = _RowAt(qTpad)
    invi = _RowAt(kTpad)
    p._add("gpsimd", lambda eng: eng.dma_start(out=rrow.ap(), in_=rrow_ap_in),
           r=[whole(rrow_ap_in.tensor)], w=[rrow.res()], kind="dma", tag=tag)
    p._add("gpsimd", lambda eng: eng.dma_start(out=invi.ap(), in_=invi_ap_in),
           r=[whole(invi_ap_in.tensor)], w=[invi.res()], kind="dma", tag=tag)
    p._add("vector", lambda eng: eng.tensor_scalar(
        invi.ap(), invi.ap(), invt.pool.t[32:33, invt.c0:invt.c0 + 1], None,
        ALU.mult),
        r=[invi.res(), invt.res(slice(0, 1))], w=[invi.res()], kind="op", tag=tag)

    cc1_in = mk.ctx.dram(tag + "_cc1in", [TPC, 192], BF16)
    cc1_out = mk.ctx.dram(tag + "_cc1out", [SEQ, 192], BF16)
    stg = mk.bf.alloc(192, tag + "_stg")
    for i in range(8):
        pt = mk.PT0 if i % 2 == 0 else mk.PT1
        mk.tr(pt[:, 0:128], qs_sb[:, i * 128:(i + 1) * 128],
              r=[qs_sb.res(slice(i * 128, (i + 1) * 128))], w=[whole(pt)],
              tag=tag + "_qtr")
        p._add("vector", lambda eng, pt=pt: eng.tensor_copy(
            stg[:, 0:128], pt[:, 0:128]),
            r=[whole(pt)], w=[stg.res(slice(0, 128))], kind="op", tag=tag)
        pt2 = mk.PT1 if i % 2 == 0 else mk.PT0
        mk.tr(pt2[0:128, 0:32], kT_sb.ap(slice(0, 32), slice(i * 128, (i + 1) * 128)),
              r=[kT_sb.res(slice(i * 128, (i + 1) * 128))], w=[whole(pt2)],
              tag=tag + "_ktr", np_=32)
        p._add("vector", lambda eng, pt2=pt2: eng.tensor_copy(
            stg[:, 128:160], pt2[:, 0:32]),
            r=[whole(pt2)], w=[stg.res(slice(128, 160))], kind="op", tag=tag)
        pt3 = mk.PT0 if i % 2 == 0 else mk.PT1
        mk.tr(pt3[0:128, 0:32], vT_sb.ap(slice(0, 32), slice(i * 128, (i + 1) * 128)),
              r=[vT_sb.res(slice(i * 128, (i + 1) * 128))], w=[whole(pt3)],
              tag=tag + "_vtr", np_=32)
        p._add("vector", lambda eng, pt3=pt3: eng.tensor_copy(
            stg[:, 160:192], pt3[:, 0:32]),
            r=[whole(pt3)], w=[stg.res(slice(160, 192))], kind="op", tag=tag)
        p._add("sync", lambda eng, i=i: eng.dma_start(
            out=cc1_in.tensor.ap()[i * 128:(i + 1) * 128, :], in_=stg[:, :]),
            r=[stg.res()], w=[whole(cc1_in.tensor)], kind="dma", tag=tag + "_stg")
    stg.free()
    for t in (qs_sb, kT_sb, vT_sb):
        t.free()
    p._add("gpsimd", lambda eng: eng.collective_compute(
        "AllGather", ALU.bypass, replica_groups=cc_rg,
        ins=[cc1_in.tensor.ap().opt()], outs=[cc1_out.tensor.ap().opt()]),
        r=[whole(cc1_in.tensor)], w=[whole(cc1_out.tensor)], kind="cc",
        tag=tag + "_cc1")

    kvo_sb = nc.alloc_sbuf_tensor(tag + "_kvo", [1, 2], I32)
    p._add("sync", lambda eng: eng.dma_start(out=kvo_sb[0:1, 0:2], in_=kvoff_ap),
           r=[whole(kvoff_ap.tensor)], w=[whole(kvo_sb)], kind="dma", tag=tag)
    ktile_all = mk.bf.alloc(NT * 8, tag + "_ktall")
    p._add("vector", lambda eng: eng.memset(kTpad.ap(slice(0, 8), slice(0, 31)), 0.0),
           r=[], w=[kTpad.res(slice(0, 31))], kind="op", tag=tag)
    regs = {}

    def _kv_reg(eng, which):
        if which not in regs:
            regs[which] = eng.value_load(kvo_sb[0:1, which:which + 1])
        return regs[which]

    def ldk(eng):
        off = _kv_reg(eng, 0)
        srcap = AP(cc1_out.tensor, off, [[192, 128], [128 * 192, NT], [1, 8]])
        dstap = ktile_all[:, :].rearrange("p (j c) -> p j c", j=NT, c=8)
        return eng.dma_start(out=dstap, in_=srcap)
    p._add("sync", ldk, r=[whole(cc1_out.tensor), whole(kvo_sb)],
           w=[ktile_all.res()], kind="dma", tag=tag + "_kld")
    for j in range(NT):
        pt = mk.PT0 if j % 2 == 0 else mk.PT1
        mk.tr(pt[0:8, 0:128], ktile_all[:, j * 8:(j + 1) * 8],
              r=[ktile_all.res(slice(j * 8, (j + 1) * 8))], w=[whole(pt)],
              tag=tag + "_ktr2")
        p._add("vector", lambda eng, pt=pt, j=j: eng.tensor_copy(
            kTpad.ap(slice(0, 8), slice(31 + j * 128, 31 + (j + 1) * 128)),
            pt[0:8, 0:128]),
            r=[whole(pt)], w=[kTpad.res(slice(31 + j * 128, 31 + (j + 1) * 128))],
            kind="op", tag=tag + "_kpe")
    for d in range(16):
        for half, kw in ((0, kwT_lo), (1, kwT_hi)):
            dd = d + 16 * half
            p._add("sync", lambda eng, d=d, dd=dd, kw=kw: eng.dma_start(
                out=kw.ap(slice(d * 8, (d + 1) * 8), slice(0, SEQ)),
                in_=kTpad.ap(slice(0, 8), slice(dd, dd + SEQ))),
                r=[kTpad.res(slice(0, 31 + SEQ))], w=[kw.res()], kind="dma",
                tag=tag + "_kw")
    vs_sb = mk.bf.alloc(NT * 8, tag + "_vs")
    p._add("vector", lambda eng: eng.memset(vs_sb[:, :], 0.0),
           r=[], w=[vs_sb.res()], kind="op", tag=tag)

    def ldv(eng):
        off = _kv_reg(eng, 1)
        srcap = AP(cc1_out.tensor, off + 192, [[192, 128], [128 * 192, NT - 1], [1, 8]])
        dstap = vs_sb[:, 0:(NT - 1) * 8].rearrange("p (j c) -> p j c",
                                                   j=NT - 1, c=8)
        return eng.dma_start(out=dstap, in_=srcap)
    p._add("sync", ldv, r=[whole(cc1_out.tensor), whole(kvo_sb), vs_sb.res()],
           w=[vs_sb.res(slice(0, (NT - 1) * 8))], kind="dma", tag=tag + "_vld")

    def ldv2(eng):
        off = _kv_reg(eng, 1)
        srcap = AP(cc1_out.tensor, off + ((NT - 1) * 128 + 1) * 192,
                   [[192, 127], [1, 8]])
        return eng.dma_start(out=vs_sb.ap(slice(0, 127),
                                          slice((NT - 1) * 8, NT * 8)), in_=srcap)
    p._add("sync", ldv2, r=[whole(cc1_out.tensor), whole(kvo_sb), vs_sb.res()],
           w=[vs_sb.res(slice((NT - 1) * 8, NT * 8))], kind="dma",
           tag=tag + "_vld2")

    cc2_in = mk.ctx.dram(tag + "_cc2in", [4 * SEQ, 8], BF16)
    cc2_out = mk.ctx.dram(tag + "_cc2out", [4 * 4 * SEQ, 8], BF16)
    qtile_all = mk.bf.alloc(NT * 8, tag + "_qtall")
    ostg = mk.bf.alloc(NT * 8, tag + "_ostg")
    st2 = [mk.f32.alloc(4, tag + f"_st{i}") for i in range(2)]
    for hh in range(4):
        p._add("vector", lambda eng: eng.memset(
            qTpad.ap(slice(0, 8), slice(0, 31)), 0.0),
            r=[], w=[qTpad.res(slice(0, 31))], kind="op", tag=tag)
        def ldq(eng, hh=hh):
            off = _kv_reg(eng, 0)
            qoff = (off - 128) * 4 + hh * 8
            srcap = AP(cc1_out.tensor, qoff,
                       [[192, 128], [128 * 192, NT], [1, 8]])
            dstap = qtile_all[:, :].rearrange("p (j c) -> p j c", j=NT, c=8)
            return eng.dma_start(out=dstap, in_=srcap)
        p._add("sync", ldq, r=[whole(cc1_out.tensor), whole(kvo_sb)],
               w=[qtile_all.res()], kind="dma", tag=tag + "_qld")
        for j in range(NT):
            pt = mk.PT0 if j % 2 == 0 else mk.PT1
            mk.tr(pt[0:8, 0:128], qtile_all[:, j * 8:(j + 1) * 8],
                  r=[qtile_all.res(slice(j * 8, (j + 1) * 8))], w=[whole(pt)],
                  tag=tag + "_qtr2")
            p._add("vector", lambda eng, pt=pt, j=j: eng.tensor_copy(
                qTpad.ap(slice(0, 8), slice(31 + j * 128, 31 + (j + 1) * 128)),
                pt[0:8, 0:128]),
                r=[whole(pt)],
                w=[qTpad.res(slice(31 + j * 128, 31 + (j + 1) * 128))],
                kind="op", tag=tag + "_qpe")
        for d in range(16):
            for half, qw in ((0, qwT_lo), (1, qwT_hi)):
                dd = d + 16 * half
                p._add("sync", lambda eng, d=d, dd=dd, qw=qw: eng.dma_start(
                    out=qw.ap(slice(d * 8, (d + 1) * 8), slice(0, SEQ)),
                    in_=qTpad.ap(slice(0, 8), slice(dd, dd + SEQ))),
                    r=[qTpad.res()], w=[qw.res()], kind="dma", tag=tag + "_qw")
        for qt in range(NT):
            kmax = (qt + 1) * 128
            nch = (kmax + 511) // 512
            qc = slice(qt * 128, (qt + 1) * 128)
            s_row = s_row2[qt % 2]
            st = st2[qt % 2]
            pvb = mk.PC if qt % 2 == 0 else mk.PF
            for ch in range(nch):
                ps = mk.PA if ch % 2 == 0 else mk.PB
                nn = min(512, kmax - ch * 512)
                kc = slice(ch * 512, ch * 512 + nn)
                mk.mm(ps[:, 0:nn], qwT_lo[:, qc], kwT_lo[:, kc],
                      r=[qwT_lo.res(qc), kwT_lo.res(kc)], w=[whole(ps)],
                      start=True, stop=False, tag=tag + "_s1")
                mk.mm(ps[:, 0:nn], qwT_hi[:, qc], kwT_hi[:, kc],
                      r=[qwT_hi.res(qc), kwT_hi.res(kc)], w=[whole(ps)],
                      start=False, stop=False, tag=tag + "_s2")
                mk.mm(ps[:, 0:nn], invi.ap(None, qc),
                      rrow.ap(None, kc),
                      r=[invi.res(qc), rrow.res(kc)], w=[whole(ps)],
                      start=False, stop=True, tag=tag + "_sb")
                p._add("scalar", lambda eng, ps=ps, kc=kc, nn=nn: eng.activation(
                    s_row[:, kc], ps[:, 0:nn], AF.Copy),
                    r=[whole(ps)], w=[s_row.res(kc)], kind="op",
                    tag=tag + "_sev")
            dc = slice(qt * 128, kmax)
            p._add("vector", lambda eng, dc=dc: eng.tensor_tensor(
                s_row[:, dc], s_row[:, dc], maskc[:, 0:128], ALU.mult),
                r=[s_row.res(dc), maskc.res()], w=[s_row.res(dc)], kind="op",
                tag=tag + "_dm1")
            p._add("vector", lambda eng, dc=dc: eng.tensor_tensor(
                s_row[:, dc], s_row[:, dc], negbig[:, 0:128], ALU.add),
                r=[s_row.res(dc), negbig.res()], w=[s_row.res(dc)], kind="op",
                tag=tag + "_dm2")
            p._add("vector", lambda eng, kmax=kmax: eng.reduce_max(
                st[:, 0:1], s_row[:, 0:kmax], axis=AX.X, negate=True),
                r=[s_row.res(slice(0, kmax))], w=[st.res(slice(0, 1))],
                kind="op", tag=tag + "_max")
            p._add("scalar", lambda eng, kmax=kmax: eng.activation(
                s_row[:, 0:kmax], s_row[:, 0:kmax], AF.Exp, bias=st[:, 0:1],
                accum_out=st[:, 1:2]),
                r=[s_row.res(slice(0, kmax)), st.res(slice(0, 1))],
                w=[s_row.res(slice(0, kmax)), st.res(slice(1, 2))], kind="op",
                tag=tag + "_exp")
            if qt == 0:
                p._add("vector", lambda eng: eng.memset(
                    s_row.ap(slice(0, 1), slice(0, 128)), 0.0),
                    r=[], w=[s_row.res(slice(0, 128))], kind="op",
                    tag=tag + "_r0")
            p._add("vector", lambda eng: eng.reciprocal(st[:, 2:3], st[:, 1:2]),
                   r=[st.res(slice(1, 2))], w=[st.res(slice(2, 3))], kind="op",
                   tag=tag + "_rec")
            for g0 in range(0, qt + 1, 8):
                g1 = min(qt + 1, g0 + 8)
                gi = (g0 // 8) % 2
                pt = mk.PT0 if gi == 0 else mk.PT1
                pb = ptb2[gi]
                for kb in range(g0, g1):
                    mk.tr(pt[:, (kb - g0) * 128:(kb - g0 + 1) * 128],
                          s_row[:, kb * 128:(kb + 1) * 128],
                          r=[s_row.res(slice(kb * 128, (kb + 1) * 128))],
                          w=[(pt, (kb - g0) * 128, (kb - g0 + 1) * 128)],
                          tag=tag + "_ptr")
                nn = (g1 - g0) * 128
                p._add("vector", lambda eng, pt=pt, pb=pb, nn=nn: eng.tensor_copy(
                    pb[:, 0:nn], pt[:, 0:nn]),
                    r=[whole(pt)], w=[pb.res(slice(0, nn))], kind="op",
                    tag=tag + "_pte")
                for kb in range(g0, g1):
                    mk.mm(pvb[:, 0:8], pb[:, (kb - g0) * 128:(kb - g0 + 1) * 128],
                          vs_sb[:, kb * 8:(kb + 1) * 8],
                          r=[pb.res(slice((kb - g0) * 128, (kb - g0 + 1) * 128)),
                             vs_sb.res(slice(kb * 8, (kb + 1) * 8))],
                          w=[whole(pvb)], start=(kb == 0), stop=(kb == qt),
                          tag=tag + "_pv")
            p._add("scalar", lambda eng, qt=qt, pvb=pvb, st=st: eng.activation(
                ostg[:, qt * 8:(qt + 1) * 8], pvb[:, 0:8], AF.Identity,
                scale=st[:, 2:3]),
                r=[whole(pvb), st.res(slice(2, 3))],
                w=[ostg.res(slice(qt * 8, (qt + 1) * 8))], kind="op",
                tag=tag + "_oev")
        srcap = ostg[:, :].rearrange("p (q v) -> p q v", q=NT, v=8)
        dstap = AP(cc2_in.tensor, hh * SEQ * 8, [[8, 128], [128 * 8, NT], [1, 8]])
        p._add("sync", lambda eng, srcap=srcap, dstap=dstap: eng.dma_start(
            out=dstap, in_=srcap),
            r=[ostg.res()], w=[whole(cc2_in.tensor)], kind="dma",
            tag=tag + "_ogw")
    for t in (qTpad, qwT_lo, qwT_hi, qtile_all, s_row2[0], s_row2[1],
              ptb2[0], ptb2[1], ostg, kwT_lo, kwT_hi, vs_sb, ktile_all):
        t.free()
    p._add("gpsimd", lambda eng: eng.collective_compute(
        "AllGather", ALU.bypass, replica_groups=cc_rg,
        ins=[cc2_in.tensor.ap().opt()], outs=[cc2_out.tensor.ap().opt()]),
        r=[whole(cc2_in.tensor)], w=[whole(cc2_out.tensor)], kind="cc",
        tag=tag + "_cc2")

    vemb = mk.f32.alloc(4, tag + "_vemb")
    p._add("sync", lambda eng: eng.dma_start(out=vemb[:, 0:1], in_=vemb0_ap),
           r=[whole(vemb0_ap.tensor)], w=[vemb.res(slice(0, 1))], kind="dma",
           tag=tag)
    p._add("sync", lambda eng: eng.dma_start(out=vemb[:, 1:2], in_=vemb1_ap),
           r=[whole(vemb1_ap.tensor)], w=[vemb.res(slice(1, 2))], kind="dma",
           tag=tag)
    p._add("vector", lambda eng: eng.tensor_tensor(
        vemb[:, 2:3], vemb[:, 1:2], vemb[:, 0:1], ALU.subtract),
        r=[vemb.res(slice(0, 2))], w=[vemb.res(slice(2, 3))], kind="op",
        tag=tag)
    qtb_sb = nc.alloc_sbuf_tensor(tag + "_qtb", [1, 1], I32)
    p._add("sync", lambda eng: eng.dma_start(out=qtb_sb[0:1, 0:1], in_=qtb_ap),
           r=[whole(qtb_ap.tensor)], w=[whole(qtb_sb)], kind="dma", tag=tag)
    amix = mk.bf.alloc(128, tag + "_amix")
    attn1 = mk.bf.alloc(1024, tag + "_attn1")
    for ti in range(8):
        def lda(eng, ti=ti):
            if "qtb" not in regs:
                regs["qtb"] = eng.value_load(qtb_sb[0:1, 0:1])
            qb = regs["qtb"]
            off = (qb + ti * 128) * 8
            srcap = AP(cc2_out.tensor, off,
                       [[8, 128], [4 * SEQ * 8, 4], [SEQ * 8, 4], [1, 8]])
            dstap = amix[:, :].rearrange("p (r hh v) -> p r hh v", r=4, hh=4, v=8)
            return eng.dma_start(out=dstap, in_=srcap)
        p._add("sync", lda, r=[whole(cc2_out.tensor), whole(qtb_sb)],
               w=[amix.res()], kind="dma", tag=tag + "_ald")
        pt = mk.PT0 if ti % 2 == 0 else mk.PT1
        mk.tr(pt[:, 0:128], amix[:, :], r=[amix.res()], w=[whole(pt)],
              tag=tag + "_atr")
        p._add("vector", lambda eng, pt=pt, ti=ti: eng.tensor_copy(
            attn1[:, ti * 128:(ti + 1) * 128], pt[:, 0:128]),
            r=[whole(pt)], w=[attn1.res(slice(ti * 128, (ti + 1) * 128))],
            kind="op", tag=tag + "_ate")
    import os as _os
    if _os.environ.get("DBG"):
        raw_o = mk.ctx.out("raw_o", (128, 1024), F32)
        rawf = mk.f32.alloc(1024, tag + "_rawf")
        p._add("vector", lambda eng: eng.tensor_copy(rawf[:, :], attn1[:, :]),
               r=[attn1.res()], w=[rawf.res()], kind="op", tag=tag + "_rawc")
        p._add("sync", lambda eng: eng.dma_start(out=raw_o, in_=rawf[:, :]),
               r=[rawf.res()], w=[whole(raw_o.tensor)], kind="dma",
               tag=tag + "_rawo")
        rawf.free()
        mk._raw_out = raw_o
    p._add("vector", lambda eng: eng.tensor_scalar(
        attn1[:, :], attn1[:, :], vemb[:, 2:3], vemb[:, 0:1],
        ALU.mult, ALU.add),
        r=[attn1.res(), vemb.res(slice(0, 3))], w=[attn1.res()], kind="op",
        tag=tag + "_vemb")
    woSB = mk.bf.alloc(2048, tag + "_wo")
    load_wcol(mk, woT_ap, 0, 2048, 1, woSB, tag=tag + "_wodma")
    for fb in range(DB):
        for th in range(2):
            ps = mk.PD if th == 0 else mk.PE_
            mk.mm(ps[:, :], woSB[:, fb * 128:(fb + 1) * 128],
                  attn1[:, th * 512:(th + 1) * 512],
                  r=[woSB.res(slice(fb * 128, (fb + 1) * 128)),
                     attn1.res(slice(th * 512, (th + 1) * 512))],
                  w=[whole(ps)], tag=tag + "_wom")
            dc = slice(fb * 1024 + th * 512, fb * 1024 + (th + 1) * 512)
            p._add("vector", lambda eng, ps=ps, dc=dc: eng.tensor_tensor(
                x_acc[:, dc], x_acc[:, dc], ps[:, :], ALU.add),
                r=[x_acc.res(dc), whole(ps)], w=[x_acc.res(dc)], kind="op",
                tag=tag + "_woacc")
    for t in (amix, attn1, woSB, taut, invt, st2[0], st2[1], vemb):
        t.free()



def build_l1_ffn(mk, xn1T, x_acc, gw_ap, ebias_ap, sw1_ap, sw2_ap,
                 rw1_aps, rw2_aps, cc_all, tag="l1f"):
    p, nc = mk.p, mk.nc
    onesf = mk.const("l1_onesf", np.ones((1, 128), np.float32))
    ones128b = mk.const("l1f_ones128", np.ones((128, 1), np.float32), BF16)

    gwv = mk.bf.alloc(DB * E1, tag + "_gw")
    load_wcol(mk, gw_ap, 0, E1, DB, gwv, tag=tag + "_gw")
    eb = mk.f32.alloc(16, tag + "_eb")
    ebb = mk.bf.alloc(16, tag + "_ebb")
    p._add("sync", lambda eng: eng.dma_start(out=eb.ap(slice(0, 1), slice(0, E1)),
                                             in_=ebias_ap),
           r=[whole(ebias_ap.tensor)], w=[eb.res(slice(0, E1))], kind="dma",
           tag=tag)
    p._add("vector", lambda eng: eng.tensor_copy(
        ebb.ap(slice(0, 1), slice(0, E1)), eb.ap(slice(0, 1), slice(0, E1))),
        r=[eb.res(slice(0, E1))], w=[ebb.res(slice(0, E1))], kind="op", tag=tag)
    p._add("vector", lambda eng: eng.tensor_copy(
        eb.ap(slice(0, 1), slice(E1, 2 * E1)), ebb.ap(slice(0, 1), slice(0, E1))),
        r=[ebb.res(slice(0, E1))], w=[eb.res(slice(E1, 2 * E1))], kind="op",
        tag=tag)
    p._add("vector", lambda eng: eng.tensor_tensor(
        eb.ap(slice(0, 1), slice(0, E1)), eb.ap(slice(0, 1), slice(0, E1)),
        eb.ap(slice(0, 1), slice(E1, 2 * E1)), ALU.subtract),
        r=[eb.res(slice(0, 2 * E1))], w=[eb.res(slice(0, E1))], kind="op",
        tag=tag)
    p._add("vector", lambda eng: eng.tensor_copy(
        ebb.ap(slice(0, 1), slice(8, 8 + E1)), eb.ap(slice(0, 1), slice(0, E1))),
        r=[eb.res(slice(0, E1))], w=[ebb.res(slice(8, 8 + E1))], kind="op",
        tag=tag)
    onesb_row = mk.const("l1f_onesrow", np.ones((1, 128), np.float32), BF16)

    aff_own = mk.f32.alloc(64, tag + "_aff")
    mask_own = mk.f32.alloc(64, tag + "_mask")
    s_own = mk.f32.alloc(64, tag + "_sown")
    rk = mk.f32.alloc(16, tag + "_rk")
    cc3_in = mk.ctx.dram(tag + "_cc3in", [TPC, 8], F32)
    cc3_out = mk.ctx.dram(tag + "_cc3out", [B * N, 8], F32)
    p._add("vector", lambda eng: eng.memset(mask_own[:, :], 0.0),
           r=[], w=[mask_own.res()], kind="op", tag=tag)
    p._add("vector", lambda eng: eng.memset(s_own[:, :], -1.0),
           r=[], w=[s_own.res()], kind="op", tag=tag)
    for i in range(8):
        ps = mk.PE_
        for kb in range(DB):
            mk.mm(ps[:, 0:E1],
                  xn1T[:, kb * 1024 + i * 128: kb * 1024 + (i + 1) * 128],
                  gwv[:, kb * E1:(kb + 1) * E1],
                  r=[xn1T.res(slice(kb * 1024 + i * 128, kb * 1024 + (i + 1) * 128)),
                     gwv.res(slice(kb * E1, (kb + 1) * E1))],
                  w=[whole(ps)], start=(kb == 0), stop=False, tag=tag + "_gl")
        mk.mm(ps[:, 0:E1], onesb_row[0:1, :], ebb.ap(slice(0, 1), slice(0, E1)),
              r=[onesb_row.res(), ebb.res(slice(0, E1))], w=[whole(ps)],
              start=False, stop=False, tag=tag + "_gb1")
        mk.mm(ps[:, 0:E1], onesb_row[0:1, :], ebb.ap(slice(0, 1), slice(8, 8 + E1)),
              r=[onesb_row.res(), ebb.res(slice(8, 8 + E1))], w=[whole(ps)],
              start=False, stop=True, tag=tag + "_gb2")
        ac = slice(i * 8, i * 8 + E1)
        p._add("scalar", lambda eng, ps=ps, ac=ac: eng.activation(
            aff_own[:, ac], ps[:, 0:E1], AF.Sigmoid),
            r=[whole(ps)], w=[aff_own.res(ac)], kind="op", tag=tag + "_sig")
        for e in range(E1):
            p._add("vector", lambda eng, e=e, ac=ac: eng.tensor_scalar(
                rk[:, 0:E1], aff_own[:, ac], aff_own[:, i * 8 + e:i * 8 + e + 1],
                None, ALU.is_gt),
                r=[aff_own.res(ac)], w=[rk.res(slice(0, E1))], kind="op",
                tag=tag)
            p._add("vector", lambda eng: eng.reduce_sum(
                rk[:, 8:9], rk[:, 0:E1], axis=AX.X),
                r=[rk.res(slice(0, E1))], w=[rk.res(slice(8, 9))], kind="op",
                tag=tag)
            if e > 0:
                p._add("vector", lambda eng, e=e, i=i: eng.tensor_scalar(
                    rk[:, 0:e], aff_own[:, i * 8:i * 8 + e],
                    aff_own[:, i * 8 + e:i * 8 + e + 1], None, ALU.is_equal),
                    r=[aff_own.res(ac)], w=[rk.res(slice(0, e))], kind="op",
                    tag=tag)
                p._add("vector", lambda eng, e=e: eng.reduce_sum(
                    rk[:, 9:10], rk[:, 0:e], axis=AX.X),
                    r=[rk.res(slice(0, e))], w=[rk.res(slice(9, 10))],
                    kind="op", tag=tag)
                p._add("vector", lambda eng: eng.tensor_tensor(
                    rk[:, 8:9], rk[:, 8:9], rk[:, 9:10], ALU.add),
                    r=[rk.res(slice(8, 10))], w=[rk.res(slice(8, 9))],
                    kind="op", tag=tag)
            p._add("vector", lambda eng, e=e, i=i: eng.tensor_scalar(
                mask_own[:, i * 8 + e:i * 8 + e + 1], rk[:, 8:9], 2.0, None,
                ALU.is_lt),
                r=[rk.res(slice(8, 9))],
                w=[mask_own.res(slice(i * 8 + e, i * 8 + e + 1))], kind="op",
                tag=tag)
        p._add("vector", lambda eng, ac=ac: eng.scalar_tensor_tensor(
            s_own[:, ac], aff_own[:, ac], 1.0, mask_own[:, ac],
            ALU.add, ALU.mult),
            r=[aff_own.res(ac), mask_own.res(ac)], w=[s_own.res(ac)],
            kind="op", tag=tag)
        p._add("vector", lambda eng, ac=ac: eng.tensor_scalar(
            s_own[:, ac], s_own[:, ac], -1.0, None, ALU.add),
            r=[s_own.res(ac)], w=[s_own.res(ac)], kind="op", tag=tag)
        p._add("sync", lambda eng, i=i: eng.dma_start(
            out=cc3_in.tensor.ap()[i * 128:(i + 1) * 128, :],
            in_=s_own[:, i * 8:(i + 1) * 8]),
            r=[s_own.res(slice(i * 8, (i + 1) * 8))],
            w=[whole(cc3_in.tensor)], kind="dma", tag=tag + "_sw")
    gwv.free()
    p._add("gpsimd", lambda eng: eng.collective_compute(
        "AllGather", ALU.bypass, replica_groups=cc_all,
        ins=[cc3_in.tensor.ap().opt()], outs=[cc3_out.tensor.ap().opt()]),
        r=[whole(cc3_in.tensor)], w=[whole(cc3_out.tensor)], kind="cc",
        tag=tag + "_cc3")

    sall = mk.f32.alloc(512, tag + "_sall")
    cntb = mk.bf.alloc(512, tag + "_cntb")
    bis = mk.f32.alloc(64, tag + "_bis")
    srcap = AP(cc3_out.tensor, 0, [[8, 128], [1024, 64], [1, 8]])
    p._add("sync", lambda eng: eng.dma_start(
        out=sall[:, :].rearrange("p (g e) -> p g e", g=64, e=8), in_=srcap),
        r=[whole(cc3_out.tensor)], w=[sall.res()], kind="dma", tag=tag + "_sld")
    p._add("vector", lambda eng: eng.memset(bis[0:1, 0:8], -2.0),
           r=[], w=[bis.res(slice(0, 8))], kind="op", tag=tag)
    p._add("vector", lambda eng: eng.memset(bis[0:1, 8:16], 1.0),
           r=[], w=[bis.res(slice(8, 16))], kind="op", tag=tag)
    for it in range(BISECT_ITERS):
        p._add("vector", lambda eng: eng.tensor_tensor(
            bis[0:1, 16:24], bis[0:1, 0:8], bis[0:1, 8:16], ALU.add),
            r=[bis.res(slice(0, 16))], w=[bis.res(slice(16, 24))], kind="op",
            tag=tag)
        p._add("vector", lambda eng: eng.tensor_scalar(
            bis[0:1, 16:24], bis[0:1, 16:24], 0.5, None, ALU.mult),
            r=[bis.res(slice(16, 24))], w=[bis.res(slice(16, 24))], kind="op",
            tag=tag)
        p._add("vector", lambda eng: eng.tensor_copy(
            ebb.ap(slice(0, 1), slice(0, 8)), bis[0:1, 16:24]),
            r=[bis.res(slice(16, 24))], w=[ebb.res(slice(0, 8))], kind="op",
            tag=tag)
        mk.mm(mk.PF[:, 0:8], onesf[0:1, :], bis[0:1, 16:24],
              r=[onesf.res(), bis.res(slice(16, 24))], w=[whole(mk.PF)],
              tag=tag + "_thb")
        thb = AP(mk.PF, 0, [[512, 128], [0, 64], [1, 8]])
        p._add("vector", lambda eng, thb=thb: eng.tensor_tensor(
            cntb[:, :].rearrange("p (g e) -> p g e", g=64, e=8),
            sall[:, :].rearrange("p (g e) -> p g e", g=64, e=8),
            thb, ALU.is_gt),
            r=[sall.res(), whole(mk.PF)], w=[cntb.res()], kind="op",
            tag=tag + "_cmp")
        mk.mm(mk.PE_[0:1, :], ones128b[0:128, 0:1], cntb[:, :],
              r=[ones128b.res(), cntb.res()], w=[whole(mk.PE_)],
              tag=tag + "_csum")
        p._add("vector", lambda eng: eng.reduce_sum(
            bis[0:1, 32:40],
            AP(mk.PE_, 0, [[512, 1], [1, 8], [8, 64]]), axis=AX.X),
            r=[whole(mk.PE_)], w=[bis.res(slice(32, 40))], kind="op",
            tag=tag + "_gred")
        p._add("vector", lambda eng: eng.tensor_scalar(
            bis[0:1, 40:48], bis[0:1, 32:40], float(CAP1), None, ALU.is_ge),
            r=[bis.res(slice(32, 40))], w=[bis.res(slice(40, 48))], kind="op",
            tag=tag)
        p._add("vector", lambda eng: eng.tensor_tensor(
            bis[0:1, 48:56], bis[0:1, 16:24], bis[0:1, 0:8], ALU.subtract),
            r=[bis.res(slice(0, 24))], w=[bis.res(slice(48, 56))], kind="op",
            tag=tag)
        p._add("vector", lambda eng: eng.tensor_tensor(
            bis[0:1, 48:56], bis[0:1, 48:56], bis[0:1, 40:48], ALU.mult),
            r=[bis.res(slice(40, 56))], w=[bis.res(slice(48, 56))], kind="op",
            tag=tag)
        p._add("vector", lambda eng: eng.tensor_tensor(
            bis[0:1, 0:8], bis[0:1, 0:8], bis[0:1, 48:56], ALU.add),
            r=[bis.res(slice(0, 8)), bis.res(slice(48, 56))],
            w=[bis.res(slice(0, 8))], kind="op", tag=tag)
        p._add("vector", lambda eng: eng.tensor_tensor(
            bis[0:1, 48:56], bis[0:1, 16:24], bis[0:1, 8:16], ALU.subtract),
            r=[bis.res(slice(8, 24))], w=[bis.res(slice(48, 56))], kind="op",
            tag=tag)
        p._add("vector", lambda eng: eng.scalar_tensor_tensor(
            bis[0:1, 56:64], bis[0:1, 40:48], 1.0, bis[0:1, 48:56],
            ALU.subtract, ALU.mult),
            r=[bis.res(slice(40, 56))], w=[bis.res(slice(56, 64))], kind="op",
            tag=tag)
        p._add("vector", lambda eng: eng.tensor_tensor(
            bis[0:1, 8:16], bis[0:1, 8:16], bis[0:1, 56:64], ALU.subtract),
            r=[bis.res(slice(8, 16)), bis.res(slice(56, 64))],
            w=[bis.res(slice(8, 16))], kind="op", tag=tag)
    sall.free(); cntb.free()

    wT_all = mk.bf.alloc(1024, tag + "_wT")
    wtile = mk.f32.alloc(8, tag + "_wtile")
    wtb = mk.bf.alloc(8, tag + "_wtb")
    mk.mm(mk.PF[:, 0:8], onesf[0:1, :], bis[0:1, 0:8],
          r=[onesf.res(), bis.res(slice(0, 8))], w=[whole(mk.PF)],
          tag=tag + "_lob")
    lob = mk.f32.alloc(8, tag + "_lob")
    p._add("vector", lambda eng: eng.tensor_copy(lob[:, :], mk.PF[:, 0:8]),
           r=[whole(mk.PF)], w=[lob.res()], kind="op", tag=tag)
    for i in range(8):
        ac = slice(i * 8, (i + 1) * 8)
        p._add("vector", lambda eng, ac=ac: eng.tensor_tensor(
            wtile[:, :], s_own[:, ac], lob[:, :], ALU.is_gt),
            r=[s_own.res(ac), lob.res()], w=[wtile.res()], kind="op", tag=tag)
        p._add("vector", lambda eng, ac=ac: eng.tensor_tensor(
            wtile[:, :], wtile[:, :], s_own[:, ac], ALU.mult),
            r=[wtile.res(), s_own.res(ac)], w=[wtile.res()], kind="op", tag=tag)
        p._add("vector", lambda eng, ac=ac: eng.tensor_tensor(
            wtile[:, :], wtile[:, :], mask_own[:, ac], ALU.mult),
            r=[wtile.res(), mask_own.res(ac)], w=[wtile.res()], kind="op",
            tag=tag)
        p._add("vector", lambda eng: eng.tensor_copy(wtb[:, :], wtile[:, :]),
               r=[wtile.res()], w=[wtb.res()], kind="op", tag=tag)
        pt = mk.PT0 if i % 2 == 0 else mk.PT1
        mk.tr(pt[0:8, 0:128], wtb[:, 0:8], r=[wtb.res()], w=[whole(pt)],
              tag=tag + "_wtr")
        p._add("vector", lambda eng, pt=pt, i=i: eng.tensor_copy(
            wT_all.ap(slice(0, 8), slice(i * 128, (i + 1) * 128)),
            pt[0:8, 0:128]),
            r=[whole(pt)], w=[wT_all.res(slice(i * 128, (i + 1) * 128))],
            kind="op", tag=tag + "_wte")
    wtile.free(); wtb.free(); lob.free()
    for t in (aff_own, mask_own, s_own, rk, eb, ebb, bis):
        t.free()

    sel = np.zeros((8, 8 * 128), np.float32)
    for e in range(8):
        sel[e, e * 128:(e + 1) * 128] = 1.0
    selc = mk.const("l1_sel", sel, BF16)

    h_all = mk.bf.alloc(FB0 * 1024, tag + "_h")
    cb = mk.bf.alloc(1024, tag + "_cb")
    sg = mk.bf.alloc(512, tag + "_sg")
    t1 = mk.bf.alloc(512, tag + "_t1")
    wg = [mk.bf.alloc(DB * 128, tag + f"_wg{i}") for i in range(2)]
    wu = [mk.bf.alloc(DB * 128, tag + f"_wu{i}") for i in range(2)]
    wd = [mk.bf.alloc(FB0 * 128, tag + f"_wd{i}") for i in range(2)]
    ffnbufs = {"xnT": xn1T, "h": h_all, "sg": sg, "t1": t1, "wg": wg,
               "wu": wu, "wd": wd}
    FB1 = F1 // 128
    ffn_expert(mk, ffnbufs, sw1_ap, sw1_ap, sw2_ap, FB1, None, x_acc,
               F1, tag + "_sh", ucol0=0)
    for e in range(E1):
        for th in range(2):
            ps = mk.PF
            mk.mm(ps[:, :], selc[0:8, e * 128:(e + 1) * 128],
                  wT_all.ap(slice(0, 8), slice(th * 512, (th + 1) * 512)),
                  r=[selc.res(slice(e * 128, (e + 1) * 128)),
                     wT_all.res(slice(th * 512, (th + 1) * 512))],
                  w=[whole(ps)], tag=tag + "_cbm")
            p._add("vector", lambda eng, ps=ps, th=th: eng.tensor_copy(
                cb[:, th * 512:(th + 1) * 512], ps[:, :]),
                r=[whole(ps)], w=[cb.res(slice(th * 512, (th + 1) * 512))],
                kind="op", tag=tag + "_cbe")
        ffn_expert(mk, ffnbufs, rw1_aps[e], rw1_aps[e], rw2_aps[e], FB1, cb,
                   x_acc, F1, f"{tag}_e{e}", ucol0=0,
                   src_res_g=whole(rw1_aps[e].tensor),
                   src_res_u=whole(rw1_aps[e].tensor),
                   src_res_d=whole(rw2_aps[e].tensor))
    for t in (h_all, cb, sg, t1, wT_all):
        t.free()
    for lst in (wg, wu, wd):
        for w_ in lst:
            w_.free()



def build_full(mk, inputs):
    ctx = mk.ctx
    x = inputs["L0_attn_x"]
    xr = x.reshape(B * N // TPC, TPC, D)
    x_ap = ctx.inp("x_rows", (TPC, D), F32, lambda c: xr[c])
    xT_ap = ctx.inp("xT_rows", (D, TPC), F32,
                    lambda c: np.ascontiguousarray(xr[c].T))
    wqkvT = np.ascontiguousarray(inputs["L0_attn_w_qkv"].T)
    wq_ap = ctx.inp("wqkvT", (D, 3 * D), F32, lambda c: wqkvT)
    bq_c = mk.const("bqkv", np.ascontiguousarray(
        inputs["L0_attn_b_qkv"].reshape(48, 128).T))
    woutT = np.ascontiguousarray(inputs["L0_attn_w_out"].T)
    woutT_ap = ctx.inp("woutT", (D, D), F32, lambda c: woutT)
    bout_c = mk.const("bout", np.ascontiguousarray(
        inputs["L0_attn_b_out"].reshape(16, 128).T))
    wrouter_ap = ctx.inp("wrouter", (D, E0), F32,
                         lambda c: inputs["L0_ffn_W_router"])
    weg, weu, wed = [], [], []
    for e in range(E0):
        weg.append(ctx.inp(f"weg{e}", (D, DE0), F32,
                           lambda c, e=e: inputs["L0_ffn_We_gate"][e]))
        weu.append(ctx.inp(f"weu{e}", (D, DE0), F32,
                           lambda c, e=e: inputs["L0_ffn_We_up"][e]))
        wed.append(ctx.inp(f"wed{e}", (DE0, D), F32,
                           lambda c, e=e: inputs["L0_ffn_We_down"][e]))
    wsg_ap = ctx.inp("wsg", (D, DS0), F32, lambda c: inputs["L0_ffn_Ws_gate"])
    wsu_ap = ctx.inp("wsu", (D, DS0), F32, lambda c: inputs["L0_ffn_Ws_up"])
    wsd_ap = ctx.inp("wsd", (DS0, D), F32, lambda c: inputs["L0_ffn_Ws_down"])
    wqT1 = ctx.inp("wqT1", (D, 128), F32,
                   lambda c: np.ascontiguousarray(inputs["L1_attn_Wq"].T))
    wkT1 = ctx.inp("wkT1", (D, 32), F32,
                   lambda c: np.ascontiguousarray(inputs["L1_attn_Wk"].T))
    wvT1 = ctx.inp("wvT1", (D, 32), F32,
                   lambda c: np.ascontiguousarray(inputs["L1_attn_Wv"].T))
    woT1 = ctx.inp("woT1", (128, D), F32,
                   lambda c: np.ascontiguousarray(inputs["L1_attn_Wo"].T))
    ve0 = ctx.inp("ve0", (128, 1), F32,
                  lambda c: np.asarray(inputs["L1_attn_v_emb0"]).reshape(128, 1))
    ve1 = ctx.inp("ve1", (128, 1), F32,
                  lambda c: np.asarray(inputs["L1_attn_v_emb1"]).reshape(128, 1))
    tau_ap = ctx.inp("tau", (1, 1), F32,
                     lambda c: np.asarray(inputs["L1_attn_tau"],
                                          np.float32).reshape(1, 1))
    kvoff = ctx.inp("kvoff", (1, 2), I32,
                    lambda c: np.array([[128 + (c % 4) * 8, 160 + (c % 4) * 8]],
                                       np.int32))
    qtb = ctx.inp("qtb", (1, 1), I32,
                  lambda c: np.array([[(c % 4) * 1024]], np.int32))
    gw_ap = ctx.inp("gw1", (D, E1), F32, lambda c: inputs["L1_ffn_gate_w"])
    ebias_ap = ctx.inp("ebias1", (1, E1), F32,
                       lambda c: np.asarray(inputs["L1_ffn_expert_bias"])
                       .reshape(1, E1))
    sw1_ap = ctx.inp("sw1", (D, 2 * F1), F32,
                     lambda c: inputs["L1_ffn_shared_w1"][0])
    sw2_ap = ctx.inp("sw2", (F1, D), F32,
                     lambda c: inputs["L1_ffn_shared_w2"][0])
    rw1, rw2 = [], []
    for e in range(E1):
        rw1.append(ctx.inp(f"rw1_{e}", (D, 2 * F1), F32,
                           lambda c, e=e: inputs["L1_ffn_routed_w1"][e]))
        rw2.append(ctx.inp(f"rw2_{e}", (F1, D), F32,
                           lambda c, e=e: inputs["L1_ffn_routed_w2"][e]))

    xnT = mk.bf.alloc(DB * 1024, "xnT")
    attnT = mk.bf.alloc(H0 * 1024, "attnT")
    x_acc = mk.f32.alloc(DB * 1024, "x_acc")
    build_ln_xnT(mk, x_ap, xnT)
    build_lightning(mk, xnT, wq_ap, bq_c, attnT, [list(range(NCORE))])
    build_l0_finish(mk, xnT, attnT, xT_ap, x_acc, woutT_ap, bout_c, wrouter_ap,
                    weg, weu, wed, wsg_ap, wsu_ap, wsd_ap)
    xnT.free()
    mk.free_const("lg_dq")
    mk.free_const("lg_dmT")
    mk.free_const("l0_sel")
    xn1T = mk.bf.alloc(DB * 1024, "xn1T")
    build_ln_xnT(mk, None, xn1T, tag="ln1", loader=acc_loader(mk, x_acc))
    bgrp = [[0, 1, 2, 3], [4, 5, 6, 7]]
    build_l1_attn(mk, xn1T, x_acc, wqT1, wkT1, wvT1, woT1, ve0, ve1, tau_ap,
                  kvoff, qtb, bgrp)
    build_l1_ffn(mk, xn1T, x_acc, gw_ap, ebias_ap, sw1_ap, sw2_ap, rw1, rw2,
                 [list(range(NCORE))])
    y_o = ctx.out("yT_o", (D, TPC), F32)
    p = mk.p
    for kb in range(DB):
        p._add("sync", lambda eng, kb=kb: eng.dma_start(
            out=y_o[kb * 128:(kb + 1) * 128, :],
            in_=x_acc[:, kb * 1024:(kb + 1) * 1024]),
            r=[x_acc.res(slice(kb * 1024, (kb + 1) * 1024))],
            w=[whole(y_o.tensor)], kind="dma", tag="yout")
    p._add("sync", lambda eng: eng.nop(), r=[whole(y_o.tensor)], w=[],
           kind="op", tag="final")
    return y_o




def kernel(**inputs):
    nc = bass.Bass("TRN2", target_bir_lowering=False)
    mk = MK(nc)
    build_full(mk, inputs)
    mk.p.finalize()
    res = run_bass_kernel_spmd(nc, mk.ctx.in_maps(), core_ids=list(range(NCORE)))
    out = np.concatenate([res.results[c]["yT_o"].T for c in range(NCORE)],
                         axis=0).reshape(B, N, D)
    return np.ascontiguousarray(out, dtype=np.float32)


# version 03 (final)
# speedup vs baseline: 1.2458x; 1.0307x over previous
import math
import numpy as np
import ml_dtypes
import concourse.bass as bass
import concourse.mybir as mybir
from concourse.ap import AP
from concourse.bass_utils import run_bass_kernel_spmd

F32 = mybir.dt.float32
BF16 = mybir.dt.bfloat16
I32 = mybir.dt.int32
ALU = mybir.AluOpType
AF = mybir.ActivationFunctionType
AX = mybir.AxisListType

B, N, D = 2, 4096, 2048
H0, HD, CH = 16, 128, 128
NCORE = 8
TPC = 1024
NCH = TPC // CH
DB = D // 128
E0, DE0, DS0 = 8, 1408, 2816
FB0 = DE0 // 128
SB0 = DS0 // 128
H1, KV1, NB1, NV1, W1 = 16, 4, 8, 8, 32
QKD = W1 * NB1
E1, F1 = 7, 1024
CAP1 = (B * N) // E1
BISECT_ITERS = 38

ENGS = ("tensor", "vector", "scalar", "gpsimd", "sync")


class _Op:
    __slots__ = ("eng", "fn", "reads", "writes", "seq", "kind", "inc_needed",
                 "deps", "_semkey", "_semval", "_issue_wait", "tag")

    def __init__(self, eng, fn, reads, writes, kind, tag=""):
        self.eng, self.fn, self.reads, self.writes = eng, fn, reads, writes
        self.kind = kind
        self.inc_needed = kind in ("dma", "cc")
        self.deps = []
        self.tag = tag


class Prog:
    def __init__(self, nc):
        self.nc = nc
        self.ops = []
        self.eng_ops = {e: [] for e in ENGS}
        self.state = {}
        self._fin = False

    def _touch(self, op):
        deps = op.deps
        for res, lo, hi, is_w in op.reads + op.writes:
            bid = id(res)
            segs = self.state.setdefault(bid, [])
            if is_w:
                keep = []
                for s in segs:
                    if s[0] < hi and lo < s[1]:
                        if s[2] is not None:
                            deps.append(s[2])
                        deps.extend(s[3])
                        if s[0] >= lo and s[1] <= hi:
                            continue
                    keep.append(s)
                keep.append([lo, hi, op, []])
                self.state[bid] = keep
            else:
                for s in segs:
                    if s[0] < hi and lo < s[1]:
                        if s[2] is not None:
                            deps.append(s[2])
                        s[3].append(op)
                segs.append([lo, hi, None, [op]])

    def _add(self, eng, fn, r, w, kind, tag=""):
        assert not self._fin
        op = _Op(eng, fn, [x + (False,) for x in r], [x + (True,) for x in w], kind, tag)
        op.seq = len(self.eng_ops[eng])
        self.eng_ops[eng].append(op)
        self.ops.append(op)
        self._touch(op)
        seen, deps = set(), []
        for d in op.deps:
            if d is op or id(d) in seen:
                continue
            seen.add(id(d))
            if d.eng == eng and d.kind == "op" and eng == "tensor" \
                    and kind == "op":
                continue
            deps.append(d)
        op.deps = deps
        return op

    def finalize(self):
        assert not self._fin
        self._fin = True
        nc = self.nc
        for op in self.ops:
            for d in op.deps:
                d.inc_needed = True
        NDS = 8
        sems = {}
        for e in ENGS:
            if any(o.kind == "op" and o.inc_needed for o in self.eng_ops[e]):
                sems[("op", e)] = nc.semaphore(f"sem_{e}").__enter__()
            n_dma = sum(1 for o in self.eng_ops[e] if o.kind == "dma")
            for j in range(min(NDS, n_dma)):
                sems[("dma", e, j)] = nc.semaphore(f"dmas_{e}_{j}").__enter__()
            if any(o.kind == "cc" for o in self.eng_ops[e]):
                sems[("cc", e)] = nc.semaphore(f"cc_{e}").__enter__()
        self.sems = sems
        for e in ENGS:
            c_op = c_dma = c_cc = 0
            for op in self.eng_ops[e]:
                if op.kind == "dma":
                    j = c_dma % NDS
                    g = c_dma // NDS
                    c_dma += 1
                    op._semkey, op._semval = ("dma", e, j), 16 * (g + 1)
                    op._issue_wait = 16 * g if g > 0 else None
                elif op.kind == "cc":
                    op._semkey, op._semval = ("cc", e), c_cc + 1
                    op._issue_wait = c_cc if c_cc > 0 else None
                    c_cc += 1
                elif op.inc_needed:
                    c_op += 1
                    op._semkey, op._semval = ("op", e), c_op
                    op._issue_wait = None
                else:
                    op._semkey = op._semval = None
                    op._issue_wait = None
        nwaits = [0]
        with nc.Block() as block:
            for e in ENGS:
                ops = self.eng_ops[e]
                if not ops:
                    continue

                def emit(eng, ops=ops, e=e):
                    waited = {}
                    for op in ops:
                        for d in op.deps:
                            sk, v = d._semkey, d._semval
                            if waited.get(sk, -1) < v:
                                eng.wait_ge(self.sems[sk], v)
                                waited[sk] = v
                                nwaits[0] += 1
                        if op._issue_wait is not None:
                            sk = op._semkey
                            if waited.get(sk, -1) < op._issue_wait:
                                eng.wait_ge(self.sems[sk], op._issue_wait)
                                waited[sk] = op._issue_wait
                                nwaits[0] += 1
                        try:
                            inst = op.fn(eng)
                        except Exception:
                            import traceback
                            print(f"EMIT FAIL tag={op.tag} eng={e} seq={op.seq}")
                            traceback.print_exc()
                            raise
                        if op._semkey is not None:
                            if op.kind == "op" and e in ("vector", "scalar",
                                                         "gpsimd"):
                                eng.drain(fusable=False).then_inc(
                                    self.sems[op._semkey], 1)
                            else:
                                inst.then_inc(self.sems[op._semkey],
                                              16 if op.kind == "dma" else 1)

                getattr(block, e)(emit)
        self.n_waits = nwaits[0]
        return self


class Pool:

    def __init__(self, nc, name, cols, dtype):
        self.t = nc.alloc_sbuf_tensor(name, [128, cols], dtype)
        self.cols = cols
        self.free_list = [(0, cols)]
        self.hw = 0

    def alloc(self, n, name=""):
        import os
        nr = (n + 31) & ~31
        for i, (a, b) in enumerate(self.free_list):
            if b - a >= nr:
                self.free_list[i] = (a + nr, b)
                if self.free_list[i][0] == self.free_list[i][1]:
                    del self.free_list[i]
                self.hw = max(self.hw, a + nr)
                if os.environ.get("PRINTALLOC"):
                    print(f"ALLOC {self.t.name} {name}: [{a},{a + nr})")
                v = View(self, a, n, name)
                v.nres = nr
                return v
        raise MemoryError(f"pool {self.t.name}: no {nr} cols (free={self.free_list})")

    def free(self, view):
        import os
        a, b = view.c0, view.c0 + getattr(view, "nres", view.n)
        if os.environ.get("PRINTALLOC"):
            print(f"FREE  {self.t.name} {view.name}: [{a},{b})")
        self.free_list.append((a, b))
        self.free_list.sort()
        merged = []
        for seg in self.free_list:
            if merged and merged[-1][1] == seg[0]:
                merged[-1] = (merged[-1][0], seg[1])
            else:
                merged.append(list(seg))
        self.free_list = [tuple(s) for s in merged]


class View:

    def __init__(self, pool, c0, n, name=""):
        self.pool, self.c0, self.n, self.name = pool, c0, n, name

    def ap(self, psl=slice(None), csl=slice(None)):
        a, b, _ = csl.indices(self.n)
        return self.pool.t[psl, self.c0 + a:self.c0 + b]

    def __getitem__(self, key):
        psl, csl = key
        return self.ap(psl, csl)

    def res(self, csl=slice(None)):
        a, b, _ = csl.indices(self.n)
        return (self.pool.t, self.c0 + a, self.c0 + b)

    def free(self):
        self.pool.free(self)


def whole(t):
    return (t, 0, 1 << 30)


class Ctx:

    def __init__(self, nc):
        self.nc = nc
        self.p = Prog(nc)
        self.inputs = {}
        self.outputs = {}
        self.aps = {}
        self.psum = {}
        self.out_dma_res = []

    def inp(self, name, shape, dtype, percore):
        t = self.nc.dram_tensor(name, list(shape), dtype, kind="ExternalInput")
        self.inputs[name] = percore
        ap = t.ap()
        self.aps[name] = ap
        return ap

    def out(self, name, shape, dtype):
        t = self.nc.dram_tensor(name, list(shape), dtype, kind="ExternalOutput")
        ap = t.ap()
        self.aps[name] = ap
        return ap

    def dram(self, name, shape, dtype):
        t = self.nc.dram_tensor(name, list(shape), dtype, kind="Internal")
        ap = t.ap()
        self.aps[name] = ap
        return ap

    def alloc_psum(self, name, shape, dtype):
        t = self.nc.alloc_psum_tensor(name, list(shape), dtype)
        self.psum[name] = t
        return t

    def in_maps(self, ncore=NCORE):
        return [{k: np.ascontiguousarray(fn(c)) for k, fn in self.inputs.items()}
                for c in range(ncore)]


def bf16(x):
    return np.asarray(x).astype(ml_dtypes.bfloat16)


def lightning_tables():
    start = 2.0 ** (-2.0 ** (-(math.log2(H0) - 3)))
    slopes = np.array([start ** (i + 1) for i in range(H0)], dtype=np.float64)
    lam = np.exp(-slopes)
    pos = np.arange(CH, dtype=np.float64)
    dq = lam[:, None] ** (pos + 1.0)
    dk = lam[:, None] ** (CH - 1.0 - pos)
    diff = pos[:, None] - pos[None, :]
    dmask = np.where(diff >= 0, lam[:, None, None] ** np.maximum(diff, 0.0), 0.0)
    lam_c = lam ** CH
    return lam.astype(np.float32), dq.astype(np.float32), dk.astype(np.float32), \
        dmask.astype(np.float32), lam_c.astype(np.float32)


def lightning_coef(core):
    lam, _, _, _, lam_c = lightning_tables()
    rho = core % 4
    base = (core // 4) * 4
    coef = np.zeros((NCORE, H0), np.float32)
    lam8 = lam_c.astype(np.float64) ** NCH
    for rp in range(rho):
        coef[base + rp] = (lam8 ** (rho - 1 - rp)).astype(np.float32)
    return np.broadcast_to(coef.reshape(1, NCORE * H0), (128, NCORE * H0)).copy()


class MK:

    def __init__(self, nc, fused=True):
        self.nc = nc
        self.ctx = Ctx(nc)
        self.p = self.ctx.p
        self.fused = fused
        self.bf = Pool(nc, "poolbf", 57 * 1024 + 512, BF16)
        self.f32 = Pool(nc, "poolf32", 22 * 1024 + 512, F32)
        self.PA = nc.alloc_psum_tensor("PA", [128, 512], F32)
        self.PB = nc.alloc_psum_tensor("PB", [128, 512], F32)
        self.PC = nc.alloc_psum_tensor("PC", [128, 512], F32)
        self.PD = nc.alloc_psum_tensor("PD", [128, 512], F32)
        self.PE_ = nc.alloc_psum_tensor("PE_", [128, 512], F32)
        self.PF = nc.alloc_psum_tensor("PF", [128, 512], F32)
        self.PT0 = nc.alloc_psum_tensor("PT0", [128, 1024], BF16)
        self.PT1 = nc.alloc_psum_tensor("PT1", [128, 1024], BF16)
        self.consts = {}

    def const(self, name, arr, dtype=None):
        if name in self.consts:
            return self.consts[name]
        arr = np.asarray(arr)
        if dtype == BF16:
            arr = bf16(arr)
        npdt = arr.dtype
        dt = BF16 if npdt == ml_dtypes.bfloat16 else (F32 if npdt == np.float32 else I32)
        assert arr.ndim == 2 and arr.shape[0] <= 128
        ap = self.ctx.inp("c_" + name, arr.shape, dt, lambda c, a=arr: a)
        pool = self.bf if dt == BF16 else self.f32
        v = pool.alloc(arr.shape[1], "c_" + name)
        rows, cols = arr.shape
        self.p._add("gpsimd",
                    lambda eng, ap=ap, v=v, rows=rows, cols=cols: eng.dma_start(
                        out=v.ap(slice(0, rows), slice(0, cols)), in_=ap),
                    r=[], w=[v.res()], kind="dma", tag="const_" + name)
        self.consts[name] = v
        return v

    def ident(self):
        return self.const("ident", np.eye(128, dtype=np.float32), BF16)

    def free_const(self, name):
        if name in self.consts:
            self.consts.pop(name).free()

    def mm(self, out_ap, lhsT_ap, rhs_ap, r, w, start=True, stop=True, tag="mm"):
        return self.p._add("tensor",
                           lambda eng: eng.matmul(out_ap, lhsT_ap, rhs_ap,
                                                  start=start, stop=stop),
                           r, w, "op", tag)

    def tr(self, out_ap, in_ap, r, w, tag="tr", np_=128):
        idv = self.ident()
        return self.p._add("tensor",
                           lambda eng, i=idv: eng.transpose(
                               out_ap, in_ap, i[0:np_, 0:np_]),
                           list(r) + [idv.res()], w, "op", tag)


def build_ln_xnT(mk, x_rows_ap, xnT, eps=1e-5, tag="ln0", x_dram_res=None,
                 loader=None):
    p, nc = mk.p, mk.nc
    xt = mk.f32.alloc(2048, tag + "_xt")
    xt2 = mk.f32.alloc(2048, tag + "_xt2")
    stats = mk.f32.alloc(16, tag + "_stats")
    sq = mk.f32.alloc(512, tag + "_sq")
    xn_tm = mk.bf.alloc(2048, tag + "_xntm")
    xr = x_dram_res or (whole(x_rows_ap.tensor) if x_rows_ap is not None else None)
    for i in range(8):
        buf = xt if i % 2 == 0 else xt2
        if loader is not None:
            loader(i, buf)
        else:
            p._add("sync", lambda eng, a=x_rows_ap, i=i, b=buf: eng.dma_start(
                out=b[:, :], in_=a[i * 128:(i + 1) * 128, :]),
                r=[xr], w=[buf.res()], kind="dma", tag=tag + "_ldx")
        p._add("vector", lambda eng, b=buf, s=stats: eng.reduce_sum(
            s[:, 0:1], b[:, :], axis=AX.X, negate=True),
            r=[buf.res()], w=[stats.res(slice(0, 1))], kind="op", tag=tag)
        p._add("vector", lambda eng, s=stats: eng.tensor_scalar_mul(
            s[:, 0:1], s[:, 0:1], 1.0 / D),
            r=[stats.res(slice(0, 1))], w=[stats.res(slice(0, 1))], kind="op", tag=tag)
        for ch in range(4):
            p._add("scalar", lambda eng, b=buf, s=stats, q=sq, ch=ch: eng.activation(
                q[:, :], b[:, ch * 512:(ch + 1) * 512], AF.Square,
                accum_out=s[:, 1 + ch:2 + ch]),
                r=[buf.res(slice(ch * 512, (ch + 1) * 512))],
                w=[sq.res(), stats.res(slice(1 + ch, 2 + ch))], kind="op", tag=tag)
        p._add("vector", lambda eng, s=stats: eng.tensor_tensor(
            s[:, 1:3], s[:, 1:3], s[:, 3:5], ALU.add),
            r=[stats.res(slice(1, 5))], w=[stats.res(slice(1, 3))], kind="op", tag=tag)
        p._add("vector", lambda eng, s=stats: eng.tensor_tensor(
            s[:, 1:2], s[:, 1:2], s[:, 2:3], ALU.add),
            r=[stats.res(slice(1, 3))], w=[stats.res(slice(1, 2))], kind="op", tag=tag)
        p._add("vector", lambda eng, s=stats: eng.tensor_tensor(
            s[:, 5:6], s[:, 0:1], s[:, 0:1], ALU.mult),
            r=[stats.res(slice(0, 1))], w=[stats.res(slice(5, 6))], kind="op", tag=tag)
        p._add("vector", lambda eng, s=stats: eng.tensor_scalar(
            s[:, 6:7], s[:, 1:2], 1.0 / D, float(eps), ALU.mult, ALU.add),
            r=[stats.res(slice(1, 2))], w=[stats.res(slice(6, 7))], kind="op", tag=tag)
        p._add("vector", lambda eng, s=stats: eng.tensor_tensor(
            s[:, 6:7], s[:, 6:7], s[:, 5:6], ALU.subtract),
            r=[stats.res(slice(5, 7))], w=[stats.res(slice(6, 7))], kind="op", tag=tag)
        p._add("scalar", lambda eng, s=stats: eng.activation(
            s[:, 7:8], s[:, 6:7], AF.Sqrt),
            r=[stats.res(slice(6, 7))], w=[stats.res(slice(7, 8))], kind="op", tag=tag)
        p._add("vector", lambda eng, s=stats: eng.reciprocal(
            s[:, 8:9], s[:, 7:8]),
            r=[stats.res(slice(7, 8))], w=[stats.res(slice(8, 9))], kind="op", tag=tag)
        p._add("vector", lambda eng, s=stats: eng.tensor_tensor(
            s[:, 9:10], s[:, 0:1], s[:, 8:9], ALU.mult),
            r=[stats.res(slice(0, 1)), stats.res(slice(8, 9))],
            w=[stats.res(slice(9, 10))], kind="op", tag=tag)
        p._add("scalar", lambda eng, b=buf, s=stats, o=xn_tm: eng.activation(
            o[:, :], b[:, :], AF.Identity, scale=s[:, 8:9], bias=s[:, 9:10]),
            r=[buf.res(), stats.res(slice(8, 10))], w=[xn_tm.res()], kind="op", tag=tag)
        for k in range(DB):
            pt = mk.PT0 if (k % 2 == 0) else mk.PT1
            psl = slice((k % 4) * 256, (k % 4) * 256 + 128)
            mk.tr(pt[:, 0:128], xn_tm[:, k * 128:(k + 1) * 128],
                  r=[xn_tm.res(slice(k * 128, (k + 1) * 128))],
                  w=[whole(pt)], tag=tag + "_tr")
            p._add("vector", lambda eng, pt=pt, k=k, i=i, o=xnT: eng.tensor_copy(
                o[:, k * 1024 + i * 128: k * 1024 + (i + 1) * 128], pt[:, 0:128]),
                r=[whole(pt)], w=[xnT.res(slice(k * 1024 + i * 128, k * 1024 + (i + 1) * 128))],
                kind="op", tag=tag + "_trevac")
    for t in (xt, xt2, stats, sq, xn_tm):
        t.free()


def build_lightning(mk, xnT, wqkvT_ap, bqkv_c, attnT, cc_rg, tag="lg"):
    p, nc = mk.p, mk.nc
    lam, dq, dk, dmask, lam_c = lightning_tables()
    dq_arr = np.zeros((128, H0 * CH), np.float32)
    for h in range(H0):
        dq_arr[:, h * CH:(h + 1) * CH] = dq[h][None, :]
    dq_t = mk.const("lg_dq", dq_arr, BF16)
    dk_t = mk.const("lg_dk", dk.T, F32)
    dmT = mk.const("lg_dmT", np.ascontiguousarray(
        dmask.transpose(2, 0, 1).reshape(CH, H0 * CH)), BF16)
    coef_ap = mk.ctx.inp("lg_coef", (128, NCORE * H0), F32, lightning_coef)
    coefv = mk.f32.alloc(NCORE * H0, "lg_coef")
    p._add("gpsimd", lambda eng: eng.dma_start(out=coefv[:, :], in_=coef_ap),
           r=[], w=[coefv.res()], kind="dma", tag=tag)

    import os
    HPP = int(os.environ.get('HPP', '4'))
    cc_in = mk.ctx.dram("lg_ccin", [HPP * 128, 128], F32)
    cc_out = mk.ctx.dram("lg_ccout", [NCORE * HPP * 128, 128], F32)

    kT = mk.bf.alloc(HPP * 1024, "lg_kT")
    vb = mk.bf.alloc(HPP * 1024, "lg_vb")
    S = mk.f32.alloc(128, "lg_S")
    Sbf = mk.bf.alloc(128, "lg_Sbf")
    A_sb = mk.f32.alloc(128, "lg_A")
    wbuf = [mk.bf.alloc(128, f"lg_w{i}") for i in range(4)]
    scT2 = [mk.bf.alloc(128, f"lg_scT{i}") for i in range(2)]
    qT = mk.bf.alloc(1024, "lg_qT")
    qs2 = [mk.bf.alloc(128, f"lg_qs{i}") for i in range(2)]
    ktmp2 = [mk.bf.alloc(128, f"lg_ktmp{i}") for i in range(2)]
    Ae = mk.f32.alloc(128, "lg_Ae")

    def project(cols, bias_idx, dst_view, dst_c0, tag2):
        for kb in range(DB):
            wb = wbuf[kb % 4]
            p._add("gpsimd", lambda eng, kb=kb, wb=wb: eng.dma_start(
                out=wb[:, :], in_=wqkvT_ap[kb * 128:(kb + 1) * 128, cols:cols + 128]),
                r=[], w=[wb.res()], kind="dma", tag=tag2 + "_wdma")
            for th in range(2):
                ps = mk.PA if th == 0 else mk.PB
                mk.mm(ps[:, :], wb[:, :],
                      xnT[:, kb * 1024 + th * 512: kb * 1024 + (th + 1) * 512],
                      r=[wb.res(), xnT.res(slice(kb * 1024 + th * 512,
                                                 kb * 1024 + (th + 1) * 512))],
                      w=[whole(ps)], start=(kb == 0), stop=(kb == DB - 1), tag=tag2)
        for th in range(2):
            ps = mk.PA if th == 0 else mk.PB
            p._add("scalar", lambda eng, ps=ps, th=th: eng.activation(
                dst_view[:, dst_c0 + th * 512: dst_c0 + (th + 1) * 512], ps[:, :],
                AF.Identity, bias=bqkv_c[:, bias_idx:bias_idx + 1]),
                r=[whole(ps), bqkv_c.res(slice(bias_idx, bias_idx + 1))],
                w=[dst_view.res(slice(dst_c0 + th * 512, dst_c0 + (th + 1) * 512))],
                kind="op", tag=tag2 + "_evac")

    def kv_chunk(hl, h, j, dst_psum):
        jc = slice(hl * 1024 + j * 128, hl * 1024 + (j + 1) * 128)
        pt = mk.PT0 if j % 2 == 0 else mk.PT1
        ktmp = ktmp2[j % 2]
        mk.tr(pt[:, 0:128], kT[:, jc], r=[kT.res(jc)], w=[whole(pt)],
              tag=tag + "_ktr")
        p._add("scalar", lambda eng, pt=pt, h=h, ktmp=ktmp: eng.activation(
            ktmp[:, :], pt[:, 0:128], AF.Copy, scale=dk_t[:, h:h + 1]),
            r=[whole(pt), dk_t.res(slice(h, h + 1))], w=[ktmp.res()],
            kind="op", tag=tag + "_ks")
        mk.mm(dst_psum[:, 0:128], ktmp[:, :], vb[:, jc],
              r=[ktmp.res(), vb.res(jc)], w=[whole(dst_psum)], tag=tag + "_kv")

    for pas in range(H0 // HPP):
        h0 = pas * HPP
        for hl in range(HPP):
            h = h0 + hl
            project(D + h * 128, DB + h, kT, hl * 1024, f"{tag}_k{h}")
            project(2 * D + h * 128, 2 * DB + h, vb, hl * 1024, f"{tag}_v{h}")
            for j in range(NCH):
                jc = slice(hl * 1024 + j * 128, hl * 1024 + (j + 1) * 128)
                pt = mk.PT0 if j % 2 == 0 else mk.PT1
                mk.tr(pt[:, 0:128], vb[:, jc], r=[vb.res(jc)], w=[whole(pt)],
                      tag=tag + "_vtr")
                p._add("vector", lambda eng, pt=pt, jc=jc: eng.tensor_copy(
                    vb[:, jc], pt[:, 0:128]),
                    r=[whole(pt)], w=[vb.res(jc)], kind="op", tag=tag + "_vte")
            p._add("vector", lambda eng: eng.memset(S[:, :], 0.0),
                   r=[], w=[S.res()], kind="op", tag=tag)
            for j in range(NCH):
                pkv = mk.PC if j % 2 == 0 else mk.PF
                kv_chunk(hl, h, j, pkv)
                p._add("vector", lambda eng, h=h, pkv=pkv: eng.scalar_tensor_tensor(
                    S[:, :], S[:, :], float(lam_c[h]), pkv[:, 0:128],
                    ALU.mult, ALU.add),
                    r=[S.res(), whole(pkv)], w=[S.res()], kind="op",
                    tag=tag + "_scan")
            p._add("vector", lambda eng: eng.tensor_copy(A_sb[:, :], S[:, :]),
                   r=[S.res()], w=[A_sb.res()], kind="op", tag=tag)
            p._add("sync", lambda eng, hl=hl: eng.dma_start(
                out=cc_in.tensor.ap()[hl * 128:(hl + 1) * 128, :], in_=A_sb[:, :]),
                r=[A_sb.res()], w=[whole(cc_in.tensor)], kind="dma",
                tag=tag + "_Aout")

        p._add("gpsimd", lambda eng: eng.collective_compute(
            "AllGather", ALU.bypass, replica_groups=cc_rg,
            ins=[cc_in.tensor.ap().opt()], outs=[cc_out.tensor.ap().opt()]),
            r=[whole(cc_in.tensor)], w=[whole(cc_out.tensor)], kind="cc",
            tag=tag + "_cc")

        for hl in range(HPP):
            h = h0 + hl
            project(h * 128, h, qT, 0, f"{tag}_q{h}")
            p._add("vector", lambda eng: eng.memset(S[:, :], 0.0),
                   r=[], w=[S.res()], kind="op", tag=tag)
            for r_ in range(NCORE):
                p._add("sync", lambda eng, r_=r_, hl=hl: eng.dma_start(
                    out=Ae[:, :],
                    in_=cc_out.tensor.ap()[(r_ * HPP + hl) * 128:
                                           (r_ * HPP + hl + 1) * 128, :]),
                    r=[whole(cc_out.tensor)], w=[Ae.res()], kind="dma",
                    tag=tag + "_Ald")
                p._add("vector", lambda eng, r_=r_, h=h: eng.scalar_tensor_tensor(
                    S[:, :], Ae[:, :], coefv[:, r_ * H0 + h:r_ * H0 + h + 1],
                    S[:, :], ALU.mult, ALU.add),
                    r=[Ae.res(), coefv.res(slice(r_ * H0 + h, r_ * H0 + h + 1)),
                       S.res()],
                    w=[S.res()], kind="op", tag=tag + "_sent")
            for j in range(NCH):
                jc = slice(hl * 1024 + j * 128, hl * 1024 + (j + 1) * 128)
                qc = slice(j * 128, (j + 1) * 128)
                oc = slice(h * 1024 + j * 128, h * 1024 + (j + 1) * 128)
                p._add("scalar", lambda eng: eng.activation(
                    Sbf[:, :], S[:, :], AF.Copy),
                    r=[S.res()], w=[Sbf.res()], kind="op", tag=tag + "_sbf")
                qs = qs2[j % 2]
                p._add("vector", lambda eng, h=h, qc=qc, qs=qs: eng.tensor_tensor(
                    qs[:, :], qT[:, qc], dq_t[:, h * 128:(h + 1) * 128], ALU.mult),
                    r=[qT.res(qc), dq_t.res(slice(h * 128, (h + 1) * 128))],
                    w=[qs.res()], kind="op", tag=tag + "_qs")
                mk.mm(mk.PD[:, 0:128], kT[:, jc], qT[:, qc],
                      r=[kT.res(jc), qT.res(qc)], w=[whole(mk.PD)],
                      tag=tag + "_sc")
                scT = scT2[j % 2]
                p._add("vector", lambda eng, h=h, scT=scT: eng.tensor_tensor(
                    scT[:, :], mk.PD[:, 0:128], dmT[:, h * 128:(h + 1) * 128],
                    ALU.mult),
                    r=[whole(mk.PD), dmT.res(slice(h * 128, (h + 1) * 128))],
                    w=[scT.res()], kind="op", tag=tag + "_msk")
                mk.mm(mk.PE_[:, 0:128], vb[:, jc], scT[:, :],
                      r=[vb.res(jc), scT.res()], w=[whole(mk.PE_)],
                      start=True, stop=False, tag=tag + "_intra")
                mk.mm(mk.PE_[:, 0:128], Sbf[:, :], qs[:, :],
                      r=[Sbf.res(), qs.res()], w=[whole(mk.PE_)],
                      start=False, stop=True, tag=tag + "_inter")
                p._add("vector", lambda eng, oc=oc: eng.tensor_copy(
                    attnT[:, oc], mk.PE_[:, 0:128]),
                    r=[whole(mk.PE_)], w=[attnT.res(oc)], kind="op",
                    tag=tag + "_oevac")
                pkv = mk.PC if j % 2 == 0 else mk.PF
                kv_chunk(hl, h, j, pkv)
                p._add("vector", lambda eng, h=h, pkv=pkv: eng.scalar_tensor_tensor(
                    S[:, :], S[:, :], float(lam_c[h]), pkv[:, 0:128],
                    ALU.mult, ALU.add),
                    r=[S.res(), whole(pkv)], w=[S.res()], kind="op",
                    tag=tag + "_scan2")
    if not os.environ.get("KEEPBUFS"):
        for t in (kT, vb, S, Sbf, A_sb, scT2[0], scT2[1], qT, qs2[0], qs2[1],
              ktmp2[0], ktmp2[1], Ae, coefv):
            t.free()
        for wbv in wbuf:
            wbv.free()



def load_wcol(mk, dram_ap, col0, width, kblocks, dst, queue="gpsimd", tag="wld",
              src_res=None, split=1):
    per = (kblocks + split - 1) // split
    for s0 in range(0, kblocks, per):
        s1 = min(kblocks, s0 + per)
        src = dram_ap[s0 * 128:s1 * 128, col0:col0 + width].rearrange(
            "(kb p) f -> p kb f", kb=s1 - s0)
        dstap = dst[:, s0 * width:s1 * width].rearrange(
            "p (kb f) -> p kb f", kb=s1 - s0)
        mk.p._add(queue, lambda eng, src=src, dstap=dstap: eng.dma_start(
            out=dstap, in_=src),
            r=[src_res or whole(dram_ap.tensor)],
            w=[dst.res(slice(s0 * width, s1 * width))], kind="dma", tag=tag)


def trf32(mk, out_ap, in_ap, r, w, tag="trf"):
    idv = mk.const("identf32", np.eye(128, dtype=np.float32))
    return mk.p._add("tensor",
                     lambda eng, i=idv: eng.transpose(out_ap, in_ap, i[0:128, :]),
                     list(r) + [idv.res()], w, "op", tag)


def ffn_expert(mk, bufs, gate_ap, up_ap, down_ap, fblocks, cb_view, dst_update,
               gcol0, tag2, src_res_g=None, src_res_u=None, src_res_d=None,
               ucol0=None):
    if ucol0 is None:
        ucol0 = gcol0
    for f0 in range(0, fblocks, FB0):
        fpass = min(FB0, fblocks - f0)
        _ffn_pass(mk, bufs, gate_ap, up_ap, down_ap, f0, fpass, fblocks,
                  cb_view, dst_update, gcol0, ucol0, tag2, src_res_g,
                  src_res_u, src_res_d)

def _ffn_pass(mk, bufs, gate_ap, up_ap, down_ap, f0, fpass, fblocks, cb_view,
              dst_update, gcol0, ucol0, tag2, src_res_g, src_res_u,
              src_res_d):
    p = mk.p
    xnT, h_all, sg, t1, wg, wu, wd = (bufs["xnT"], bufs["h"], bufs["sg"],
                                      bufs["t1"], bufs["wg"], bufs["wu"],
                                      bufs["wd"])
    for fb in range(fpass):
        wgv, wuv = wg[fb % 2], wu[fb % 2]
        load_wcol(mk, gate_ap, gcol0 + (f0 + fb) * 128, 128, DB, wgv,
                  tag=tag2 + "_g", src_res=src_res_g)
        load_wcol(mk, up_ap, ucol0 + (f0 + fb) * 128, 128, DB, wuv,
                  tag=tag2 + "_u", src_res=src_res_u)
        for th in range(2):
            pg = mk.PC if th == 0 else mk.PD
            pu = mk.PE_ if th == 0 else mk.PF
            for kb in range(DB):
                rc = slice(kb * 1024 + th * 512, kb * 1024 + (th + 1) * 512)
                mk.mm(pg[:, :], wgv[:, kb * 128:(kb + 1) * 128], xnT[:, rc],
                      r=[wgv.res(slice(kb * 128, (kb + 1) * 128)), xnT.res(rc)],
                      w=[whole(pg)], start=(kb == 0), stop=(kb == DB - 1),
                      tag=tag2 + "_gm")
                mk.mm(pu[:, :], wuv[:, kb * 128:(kb + 1) * 128], xnT[:, rc],
                      r=[wuv.res(slice(kb * 128, (kb + 1) * 128)), xnT.res(rc)],
                      w=[whole(pu)], start=(kb == 0), stop=(kb == DB - 1),
                      tag=tag2 + "_um")
            p._add("scalar", lambda eng, pg=pg: eng.activation(
                sg[:, :], pg[:, :], AF.Silu),
                r=[whole(pg)], w=[sg.res()], kind="op", tag=tag2 + "_silu")
            hc = slice(fb * 1024 + th * 512, fb * 1024 + (th + 1) * 512)
            if cb_view is not None:
                p._add("vector", lambda eng, pu=pu: eng.tensor_tensor(
                    t1[:, :], pu[:, :], sg[:, :], ALU.mult),
                    r=[whole(pu), sg.res()], w=[t1.res()], kind="op",
                    tag=tag2 + "_hu")
                p._add("vector", lambda eng, hc=hc, th=th: eng.tensor_tensor(
                    h_all[:, hc], t1[:, :],
                    cb_view[:, th * 512:(th + 1) * 512], ALU.mult),
                    r=[t1.res(), cb_view.res(slice(th * 512, (th + 1) * 512))],
                    w=[h_all.res(hc)], kind="op", tag=tag2 + "_hc")
            else:
                p._add("vector", lambda eng, pu=pu, hc=hc: eng.tensor_tensor(
                    h_all[:, hc], pu[:, :], sg[:, :], ALU.mult),
                    r=[whole(pu), sg.res()], w=[h_all.res(hc)], kind="op",
                    tag=tag2 + "_hu")
    for db in range(DB):
        wdv = wd[db % 2]
        src = down_ap[f0 * 128:(f0 + fpass) * 128, db * 128:(db + 1) * 128]\
            .rearrange("(kb p) f -> p kb f", kb=fpass)
        dstap = wdv[:, 0:fpass * 128].rearrange("p (kb f) -> p kb f", kb=fpass)
        mk.p._add("gpsimd", lambda eng, src=src, dstap=dstap: eng.dma_start(
            out=dstap, in_=src),
            r=[src_res_d or whole(down_ap.tensor)], w=[wdv.res()],
            kind="dma", tag=tag2 + "_d")
        for th in range(2):
            pd_ = mk.PC if th == 0 else mk.PD
            for fb in range(fpass):
                rc = slice(fb * 1024 + th * 512, fb * 1024 + (th + 1) * 512)
                mk.mm(pd_[:, :], wdv[:, fb * 128:(fb + 1) * 128], h_all[:, rc],
                      r=[wdv.res(slice(fb * 128, (fb + 1) * 128)),
                         h_all.res(rc)],
                      w=[whole(pd_)], start=(fb == 0), stop=(fb == fpass - 1),
                      tag=tag2 + "_dm")
            dc = slice(db * 1024 + th * 512, db * 1024 + (th + 1) * 512)
            p._add("vector", lambda eng, pd_=pd_, dc=dc: eng.tensor_tensor(
                dst_update[:, dc], dst_update[:, dc], pd_[:, :], ALU.add),
                r=[dst_update.res(dc), whole(pd_)], w=[dst_update.res(dc)],
                kind="op", tag=tag2 + "_dacc")



def build_l0_finish(mk, xnT, attnT, x_rows_ap, x_acc, woutT_ap, bout_c,
                    wrouter_ap, weg_ap, weu_ap, wed_ap, wsg_ap, wsu_ap, wsd_ap,
                    tag="l0f", stages=("init", "outproj", "router", "moe", "shared")):
    p, nc = mk.p, mk.nc
    import os

    xt = mk.f32.alloc(1024, tag + "_xt")
    xt2 = mk.f32.alloc(1024, tag + "_xt2")
    for kb in range(DB):
        buf = xt if kb % 2 == 0 else xt2
        p._add("sync", lambda eng, kb=kb, b=buf: eng.dma_start(
            out=b[:, :], in_=x_rows_ap[kb * 128:(kb + 1) * 128, :]),
            r=[whole(x_rows_ap.tensor)], w=[buf.res()], kind="dma",
            tag=tag + "_ldx")
        dc = slice(kb * 1024, (kb + 1) * 1024)
        p._add("scalar", lambda eng, b=buf, dc=dc, kb=kb: eng.activation(
            x_acc[:, dc], b[:, :], AF.Identity, bias=bout_c[:, kb:kb + 1]),
            r=[buf.res(), bout_c.res(slice(kb, kb + 1))], w=[x_acc.res(dc)],
            kind="op", tag=tag + "_xev")
    xt.free(); xt2.free()

    wo = [mk.bf.alloc(DB * 128, tag + f"_wo{i}") for i in range(2)]
    lvl = int(os.environ.get("OUTPROJ_LVL", "3"))
    if os.environ.get("EXTRA_DMA"):
        spare = mk.bf.alloc(2048, "spare_nofree")
        mode_ = os.environ["EXTRA_DMA"]
        if mode_ == "memset":
            p._add("gpsimd", lambda eng: eng.memset(spare[:, 0:128], 0.0),
                   r=[], w=[spare.res(slice(0, 128))], kind="op",
                   tag=tag + "_extramemset")
        elif mode_ == "sync":
            p._add("sync", lambda eng: eng.dma_start(
                out=spare[:, 0:128], in_=woutT_ap[0:128, 0:128]),
                r=[whole(woutT_ap.tensor)], w=[spare.res(slice(0, 128))],
                kind="dma", tag=tag + "_extrasync")
        elif mode_ == "plain":
            p._add("gpsimd", lambda eng: eng.dma_start(
                out=spare[:, 0:128], in_=woutT_ap[0:128, 0:128]),
                r=[whole(woutT_ap.tensor)], w=[spare.res(slice(0, 128))],
                kind="dma", tag=tag + "_extraplain")
        lvl = 0
    wsplit = int(os.environ.get("WSPLIT", "1"))
    if "outproj" in stages:
        for fb in range(DB):
            if lvl < 1:
                break
            wv = wo[fb % 2]
            load_wcol(mk, woutT_ap, fb * 128, 128, DB, wv, tag=tag + "_wodma", split=wsplit)
            if lvl < 2:
                continue
            for th in range(2):
                ps = mk.PC if th == 0 else mk.PD
                for ab in range(DB):
                    rc = slice(ab * 1024 + th * 512, ab * 1024 + (th + 1) * 512)
                    mk.mm(ps[:, :], wv[:, ab * 128:(ab + 1) * 128], attnT[:, rc],
                          r=[wv.res(slice(ab * 128, (ab + 1) * 128)), attnT.res(rc)],
                          w=[whole(ps)], start=(ab == 0), stop=(ab == DB - 1),
                          tag=tag + "_wo")
                dc = slice(fb * 1024 + th * 512, fb * 1024 + (th + 1) * 512)
                if lvl >= 3:
                    p._add("vector", lambda eng, ps=ps, dc=dc: eng.tensor_tensor(
                        x_acc[:, dc], x_acc[:, dc], ps[:, :], ALU.add),
                        r=[x_acc.res(dc), whole(ps)], w=[x_acc.res(dc)], kind="op",
                        tag=tag + "_woacc")
    for w_ in wo:
        w_.free()
    attnT.free()
    if "router" not in stages:
        return

    wr = mk.bf.alloc(DB * 8, tag + "_wr")
    load_wcol(mk, wrouter_ap, 0, E0, DB, wr, tag=tag + "_wrdma")
    logit = mk.f32.alloc(16, tag + "_lg")
    prob = mk.f32.alloc(8, tag + "_pr")
    rk = mk.f32.alloc(8, tag + "_rk")
    cmb = mk.bf.alloc(8, tag + "_cmb")
    cT_all = mk.bf.alloc(1024, tag + "_cT")
    for i in range(8):
        ps = mk.PE_
        for kb in range(DB):
            mk.mm(ps[:, 0:E0], xnT[:, kb * 1024 + i * 128: kb * 1024 + (i + 1) * 128],
                  wr[:, kb * 8:(kb + 1) * 8],
                  r=[xnT.res(slice(kb * 1024 + i * 128, kb * 1024 + (i + 1) * 128)),
                     wr.res(slice(kb * 8, (kb + 1) * 8))],
                  w=[whole(ps)], start=(kb == 0), stop=(kb == DB - 1),
                  tag=tag + "_rt")
        p._add("vector", lambda eng, ps=ps: eng.reduce_max(
            logit[:, 8:9], ps[:, 0:E0], axis=AX.X, negate=True),
            r=[whole(ps)], w=[logit.res(slice(8, 9))], kind="op", tag=tag)
        p._add("scalar", lambda eng, ps=ps: eng.activation(
            logit[:, 0:E0], ps[:, 0:E0], AF.Exp, bias=logit[:, 8:9],
            accum_out=logit[:, 9:10]),
            r=[whole(ps), logit.res(slice(8, 9))],
            w=[logit.res(slice(0, E0)), logit.res(slice(9, 10))], kind="op",
            tag=tag + "_exp")
        p._add("vector", lambda eng: eng.reciprocal(
            logit[:, 10:11], logit[:, 9:10]),
            r=[logit.res(slice(9, 10))], w=[logit.res(slice(10, 11))],
            kind="op", tag=tag)
        p._add("vector", lambda eng: eng.tensor_scalar(
            prob[:, 0:E0], logit[:, 0:E0], logit[:, 10:11], None, ALU.mult),
            r=[logit.res(slice(0, E0)), logit.res(slice(10, 11))],
            w=[prob.res()], kind="op", tag=tag)
        for e in range(E0):
            p._add("vector", lambda eng, e=e: eng.tensor_scalar(
                rk[:, 0:E0], prob[:, 0:E0], prob[:, e:e + 1], None, ALU.is_gt),
                r=[prob.res()], w=[rk.res(slice(0, E0))], kind="op", tag=tag)
            p._add("vector", lambda eng: eng.reduce_sum(
                logit[:, 11:12], rk[:, 0:E0], axis=AX.X),
                r=[rk.res(slice(0, E0))], w=[logit.res(slice(11, 12))],
                kind="op", tag=tag)
            if e > 0:
                p._add("vector", lambda eng, e=e: eng.tensor_scalar(
                    rk[:, 0:e], prob[:, 0:e], prob[:, e:e + 1], None,
                    ALU.is_equal),
                    r=[prob.res()], w=[rk.res(slice(0, e))], kind="op", tag=tag)
                p._add("vector", lambda eng, e=e: eng.reduce_sum(
                    logit[:, 12:13], rk[:, 0:e], axis=AX.X),
                    r=[rk.res(slice(0, e))], w=[logit.res(slice(12, 13))],
                    kind="op", tag=tag)
                p._add("vector", lambda eng: eng.tensor_tensor(
                    logit[:, 11:12], logit[:, 11:12], logit[:, 12:13], ALU.add),
                    r=[logit.res(slice(11, 13))], w=[logit.res(slice(11, 12))],
                    kind="op", tag=tag)
            p._add("vector", lambda eng: eng.tensor_scalar(
                logit[:, 13:14], logit[:, 11:12], 2.0, None, ALU.is_lt),
                r=[logit.res(slice(11, 12))], w=[logit.res(slice(13, 14))],
                kind="op", tag=tag)
            p._add("vector", lambda eng, e=e: eng.tensor_scalar(
                cmb[:, e:e + 1], prob[:, e:e + 1], logit[:, 13:14], None,
                ALU.mult),
                r=[prob.res(), logit.res(slice(13, 14))],
                w=[cmb.res(slice(e, e + 1))], kind="op", tag=tag)
        pt = mk.PT0 if i % 2 == 0 else mk.PT1
        mk.tr(pt[0:E0, 0:128], cmb[:, 0:E0], r=[cmb.res()], w=[whole(pt)],
              tag=tag + "_ctr")
        p._add("vector", lambda eng, pt=pt, i=i: eng.tensor_copy(
            cT_all.ap(slice(0, E0), slice(i * 128, (i + 1) * 128)),
            pt[0:E0, 0:128]),
            r=[whole(pt)], w=[cT_all.res(slice(i * 128, (i + 1) * 128))],
            kind="op", tag=tag + "_cte")
    wr.free(); logit.free(); prob.free(); rk.free()

    sel = np.zeros((E0, E0 * 128), np.float32)
    for e in range(E0):
        sel[e, e * 128:(e + 1) * 128] = 1.0
    selc = mk.const("l0_sel", sel, BF16)

    h_all = mk.bf.alloc(FB0 * 1024, tag + "_h")
    cb = mk.bf.alloc(1024, tag + "_cb")
    sg = mk.bf.alloc(512, tag + "_sg")
    t1 = mk.bf.alloc(512, tag + "_t1")
    wg = [mk.bf.alloc(DB * 128, tag + f"_wg{i}") for i in range(2)]
    wu = [mk.bf.alloc(DB * 128, tag + f"_wu{i}") for i in range(2)]
    wd = [mk.bf.alloc(FB0 * 128, tag + f"_wd{i}") for i in range(2)]
    ffnbufs = {"xnT": xnT, "h": h_all, "sg": sg, "t1": t1, "wg": wg, "wu": wu,
               "wd": wd}

    for e in (range(E0) if "moe" in stages else []):
        for th in range(2):
            ps = mk.PF
            mk.mm(ps[:, :], selc[0:E0, e * 128:(e + 1) * 128],
                  cT_all[0:E0, th * 512:(th + 1) * 512],
                  r=[selc.res(slice(e * 128, (e + 1) * 128)),
                     cT_all.res(slice(th * 512, (th + 1) * 512))],
                  w=[whole(ps)], tag=tag + "_cbm")
            p._add("vector", lambda eng, ps=ps, th=th: eng.tensor_copy(
                cb[:, th * 512:(th + 1) * 512], ps[:, :]),
                r=[whole(ps)], w=[cb.res(slice(th * 512, (th + 1) * 512))],
                kind="op", tag=tag + "_cbe")
        ffn_expert(mk, ffnbufs, weg_ap[e], weu_ap[e], wed_ap[e], FB0, cb,
                   x_acc, 0, f"{tag}_e{e}",
                   src_res_g=whole(weg_ap[e].tensor),
                   src_res_u=whole(weu_ap[e].tensor),
                   src_res_d=whole(wed_ap[e].tensor))

    if "shared" in stages:
        ffn_expert(mk, ffnbufs, wsg_ap, wsu_ap, wsd_ap, SB0, None, x_acc, 0,
                   tag + "_sh")

    for t in (h_all, cb, sg, t1, cT_all):
        t.free()
    for lst in (wg, wu, wd):
        for w_ in lst:
            w_.free()




def acc_loader(mk, x_acc, tag="lnacc"):
    def loader(i, buf):
        for kb in range(DB):
            ps = mk.PA if kb % 2 == 0 else mk.PB
            sc = slice(kb * 1024 + i * 128, kb * 1024 + (i + 1) * 128)
            trf32(mk, ps[:, 0:128], x_acc[:, sc],
                  r=[x_acc.res(sc)], w=[whole(ps)], tag=tag + "_tr")
            mk.p._add("vector", lambda eng, ps=ps, kb=kb, b=buf: eng.tensor_copy(
                b[:, kb * 128:(kb + 1) * 128], ps[:, 0:128]),
                r=[whole(ps)], w=[buf.res(slice(kb * 128, (kb + 1) * 128))],
                kind="op", tag=tag + "_ev")
    return loader


def build_l1_attn(mk, xn1T, x_acc, wqT_ap, wkT_ap, wvT_ap, woT_ap,
                  vemb0_ap, vemb1_ap, tau_ap, kvoff_ap, qtb_ap, cc_rg,
                  tag="l1a"):
    p, nc = mk.p, mk.nc
    SEQ = N
    NT = SEQ // 128

    maskc = mk.const("l1_mask", np.tril(np.ones((128, 128), np.float32), -1), BF16)
    negbig = mk.const("l1_negbig",
                      np.triu(np.full((128, 128), -1e30, np.float32), 0), BF16)
    kwT_lo = mk.bf.alloc(SEQ, tag + "_kwlo")
    kwT_hi = mk.bf.alloc(SEQ, tag + "_kwhi")
    qwT_lo = mk.bf.alloc(SEQ, tag + "_qwlo")
    qwT_hi = mk.bf.alloc(SEQ, tag + "_qwhi")
    s_row2 = [mk.bf.alloc(SEQ, tag + f"_srow{i}") for i in range(2)]
    ptb2 = [mk.bf.alloc(1024, tag + f"_ptb{i}") for i in range(2)]
    onesf = mk.const("l1_onesf", np.ones((1, 128), np.float32))

    taut = mk.f32.alloc(8, tag + "_tau")
    p._add("sync", lambda eng: eng.dma_start(out=taut[0:1, 0:1], in_=tau_ap),
           r=[whole(tau_ap.tensor)], w=[taut.res(slice(0, 1))], kind="dma", tag=tag)
    p._add("vector", lambda eng: eng.reciprocal(taut[0:1, 1:2], taut[0:1, 0:1]),
           r=[taut.res(slice(0, 1))], w=[taut.res(slice(1, 2))], kind="op", tag=tag)
    invt = mk.f32.alloc(4, tag + "_invt")
    p._add("tensor", lambda eng: eng.matmul(
        mk.PF[:, 0:1], onesf[0:1, :], taut[0:1, 1:2], start=True, stop=True),
        [onesf.res(), taut.res(slice(1, 2))], [whole(mk.PF)], "op", tag)
    p._add("vector", lambda eng: eng.tensor_copy(invt[:, 0:1], mk.PF[:, 0:1]),
           r=[whole(mk.PF)], w=[invt.res(slice(0, 1))], kind="op", tag=tag)
    p._add("vector", lambda eng: eng.tensor_scalar(
        invt[:, 1:2], invt[:, 0:1], 1.0 / 16.0, None, ALU.mult),
        r=[invt.res(slice(0, 1))], w=[invt.res(slice(1, 2))], kind="op", tag=tag)
    wq = mk.bf.alloc(DB * 128, tag + "_wq")
    wk = mk.bf.alloc(DB * 32, tag + "_wk")
    wv_ = mk.bf.alloc(DB * 32, tag + "_wv")
    load_wcol(mk, wqT_ap, 0, 128, DB, wq, tag=tag + "_wq")
    load_wcol(mk, wkT_ap, 0, 32, DB, wk, tag=tag + "_wk")
    load_wcol(mk, wvT_ap, 0, 32, DB, wv_, tag=tag + "_wv")
    qs_sb = mk.bf.alloc(1024, tag + "_qs")
    kT_sb = mk.bf.alloc(1024, tag + "_kT")
    vT_sb = mk.bf.alloc(1024, tag + "_vT")
    for th in range(2):
        pq = mk.PA if th == 0 else mk.PB
        pk = mk.PC if th == 0 else mk.PD
        pv = mk.PE_ if th == 0 else mk.PF
        for kb in range(DB):
            rc = slice(kb * 1024 + th * 512, kb * 1024 + (th + 1) * 512)
            mk.mm(pq[:, :], wq[:, kb * 128:(kb + 1) * 128], xn1T[:, rc],
                  r=[wq.res(slice(kb * 128, (kb + 1) * 128)), xn1T.res(rc)],
                  w=[whole(pq)], start=(kb == 0), stop=(kb == DB - 1), tag=tag + "_q")
            mk.mm(pk[0:32, :], wk[:, kb * 32:(kb + 1) * 32], xn1T[:, rc],
                  r=[wk.res(slice(kb * 32, (kb + 1) * 32)), xn1T.res(rc)],
                  w=[whole(pk)], start=(kb == 0), stop=(kb == DB - 1), tag=tag + "_k")
            mk.mm(pv[0:32, :], wv_[:, kb * 32:(kb + 1) * 32], xn1T[:, rc],
                  r=[wv_.res(slice(kb * 32, (kb + 1) * 32)), xn1T.res(rc)],
                  w=[whole(pv)], start=(kb == 0), stop=(kb == DB - 1), tag=tag + "_v")
        tc = slice(th * 512, (th + 1) * 512)
        p._add("scalar", lambda eng, pq=pq, tc=tc: eng.activation(
            qs_sb[:, tc], pq[:, :], AF.Tanh, scale=invt[:, 0:1]),
            r=[whole(pq), invt.res(slice(0, 1))], w=[qs_sb.res(tc)],
            kind="op", tag=tag + "_qevac")
        p._add("scalar", lambda eng, pk=pk, tc=tc: eng.activation(
            kT_sb.ap(slice(0, 32), tc), pk[0:32, :], AF.Tanh,
            scale=invt[0:32, 0:1]),
            r=[whole(pk), invt.res(slice(0, 1))], w=[kT_sb.res(tc)],
            kind="op", tag=tag + "_kevac")
        p._add("scalar", lambda eng, pv=pv, tc=tc: eng.activation(
            vT_sb.ap(slice(0, 32), tc), pv[0:32, :], AF.Sigmoid,
            scale=invt[0:32, 0:1]),
            r=[whole(pv), invt.res(slice(0, 1))], w=[vT_sb.res(tc)],
            kind="op", tag=tag + "_vevac")
    p._add("vector", lambda eng: eng.tensor_scalar(
        qs_sb[:, :], qs_sb[:, :], invt[:, 1:2], None, ALU.mult),
        r=[qs_sb.res(), invt.res(slice(1, 2))], w=[qs_sb.res()], kind="op",
        tag=tag + "_qscale")
    for w_ in (wq, wk, wv_):
        w_.free()
    qTpad = mk.bf.alloc(31 + SEQ, tag + "_qTpad")
    kTpad = mk.bf.alloc(31 + SEQ, tag + "_kTpad")

    class _RowAt:

        def __init__(self, host):
            self.host = host

        def ap(self, psl=None, csl=slice(None)):
            a, b_, _ = csl.indices(SEQ)
            return self.host.pool.t[32:33, self.host.c0 + a:self.host.c0 + b_]

        def res(self, csl=slice(None)):
            a, b_, _ = csl.indices(SEQ)
            return (self.host.pool.t, self.host.c0 + a, self.host.c0 + b_)
    rrow_ap_in = mk.ctx.inp("c_l1_rrow", (1, SEQ), BF16,
                            lambda c: bf16(np.arange(SEQ, dtype=np.float32)[None, :]))
    invi_ap_in = mk.ctx.inp("c_l1_invi", (1, SEQ), BF16,
                            lambda c: bf16((1.0 / (np.arange(SEQ, dtype=np.float64)
                                                   + 1.0)).astype(np.float32)[None, :]))
    rrow = _RowAt(qTpad)
    invi = _RowAt(kTpad)
    p._add("gpsimd", lambda eng: eng.dma_start(out=rrow.ap(), in_=rrow_ap_in),
           r=[whole(rrow_ap_in.tensor)], w=[rrow.res()], kind="dma", tag=tag)
    p._add("gpsimd", lambda eng: eng.dma_start(out=invi.ap(), in_=invi_ap_in),
           r=[whole(invi_ap_in.tensor)], w=[invi.res()], kind="dma", tag=tag)
    p._add("vector", lambda eng: eng.tensor_scalar(
        invi.ap(), invi.ap(), invt.pool.t[32:33, invt.c0:invt.c0 + 1], None,
        ALU.mult),
        r=[invi.res(), invt.res(slice(0, 1))], w=[invi.res()], kind="op", tag=tag)

    cc1_in = mk.ctx.dram(tag + "_cc1in", [TPC, 192], BF16)
    cc1_out = mk.ctx.dram(tag + "_cc1out", [SEQ, 192], BF16)
    stg = mk.bf.alloc(192, tag + "_stg")
    for i in range(8):
        pt = mk.PT0 if i % 2 == 0 else mk.PT1
        mk.tr(pt[:, 0:128], qs_sb[:, i * 128:(i + 1) * 128],
              r=[qs_sb.res(slice(i * 128, (i + 1) * 128))], w=[whole(pt)],
              tag=tag + "_qtr")
        p._add("vector", lambda eng, pt=pt: eng.tensor_copy(
            stg[:, 0:128], pt[:, 0:128]),
            r=[whole(pt)], w=[stg.res(slice(0, 128))], kind="op", tag=tag)
        pt2 = mk.PT1 if i % 2 == 0 else mk.PT0
        mk.tr(pt2[0:128, 0:32], kT_sb.ap(slice(0, 32), slice(i * 128, (i + 1) * 128)),
              r=[kT_sb.res(slice(i * 128, (i + 1) * 128))], w=[whole(pt2)],
              tag=tag + "_ktr", np_=32)
        p._add("vector", lambda eng, pt2=pt2: eng.tensor_copy(
            stg[:, 128:160], pt2[:, 0:32]),
            r=[whole(pt2)], w=[stg.res(slice(128, 160))], kind="op", tag=tag)
        pt3 = mk.PT0 if i % 2 == 0 else mk.PT1
        mk.tr(pt3[0:128, 0:32], vT_sb.ap(slice(0, 32), slice(i * 128, (i + 1) * 128)),
              r=[vT_sb.res(slice(i * 128, (i + 1) * 128))], w=[whole(pt3)],
              tag=tag + "_vtr", np_=32)
        p._add("vector", lambda eng, pt3=pt3: eng.tensor_copy(
            stg[:, 160:192], pt3[:, 0:32]),
            r=[whole(pt3)], w=[stg.res(slice(160, 192))], kind="op", tag=tag)
        p._add("sync", lambda eng, i=i: eng.dma_start(
            out=cc1_in.tensor.ap()[i * 128:(i + 1) * 128, :], in_=stg[:, :]),
            r=[stg.res()], w=[whole(cc1_in.tensor)], kind="dma", tag=tag + "_stg")
    stg.free()
    for t in (qs_sb, kT_sb, vT_sb):
        t.free()
    p._add("gpsimd", lambda eng: eng.collective_compute(
        "AllGather", ALU.bypass, replica_groups=cc_rg,
        ins=[cc1_in.tensor.ap().opt()], outs=[cc1_out.tensor.ap().opt()]),
        r=[whole(cc1_in.tensor)], w=[whole(cc1_out.tensor)], kind="cc",
        tag=tag + "_cc1")

    kvo_sb = nc.alloc_sbuf_tensor(tag + "_kvo", [1, 2], I32)
    p._add("sync", lambda eng: eng.dma_start(out=kvo_sb[0:1, 0:2], in_=kvoff_ap),
           r=[whole(kvoff_ap.tensor)], w=[whole(kvo_sb)], kind="dma", tag=tag)
    ktile_all = mk.bf.alloc(NT * 8, tag + "_ktall")
    p._add("vector", lambda eng: eng.memset(kTpad.ap(slice(0, 8), slice(0, 31)), 0.0),
           r=[], w=[kTpad.res(slice(0, 31))], kind="op", tag=tag)
    regs = {}

    def _kv_reg(eng, which):
        if which not in regs:
            regs[which] = eng.value_load(kvo_sb[0:1, which:which + 1])
        return regs[which]

    def ldk(eng):
        off = _kv_reg(eng, 0)
        srcap = AP(cc1_out.tensor, off, [[192, 128], [128 * 192, NT], [1, 8]])
        dstap = ktile_all[:, :].rearrange("p (j c) -> p j c", j=NT, c=8)
        return eng.dma_start(out=dstap, in_=srcap)
    p._add("sync", ldk, r=[whole(cc1_out.tensor), whole(kvo_sb)],
           w=[ktile_all.res()], kind="dma", tag=tag + "_kld")
    for j in range(NT):
        pt = mk.PT0 if j % 2 == 0 else mk.PT1
        mk.tr(pt[0:8, 0:128], ktile_all[:, j * 8:(j + 1) * 8],
              r=[ktile_all.res(slice(j * 8, (j + 1) * 8))], w=[whole(pt)],
              tag=tag + "_ktr2")
        p._add("vector", lambda eng, pt=pt, j=j: eng.tensor_copy(
            kTpad.ap(slice(0, 8), slice(31 + j * 128, 31 + (j + 1) * 128)),
            pt[0:8, 0:128]),
            r=[whole(pt)], w=[kTpad.res(slice(31 + j * 128, 31 + (j + 1) * 128))],
            kind="op", tag=tag + "_kpe")
    for d in range(16):
        for half, kw in ((0, kwT_lo), (1, kwT_hi)):
            dd = d + 16 * half
            p._add("sync", lambda eng, d=d, dd=dd, kw=kw: eng.dma_start(
                out=kw.ap(slice(d * 8, (d + 1) * 8), slice(0, SEQ)),
                in_=kTpad.ap(slice(0, 8), slice(dd, dd + SEQ))),
                r=[kTpad.res(slice(0, 31 + SEQ))], w=[kw.res()], kind="dma",
                tag=tag + "_kw")
    vs_sb = mk.bf.alloc(NT * 8, tag + "_vs")
    p._add("vector", lambda eng: eng.memset(vs_sb[:, :], 0.0),
           r=[], w=[vs_sb.res()], kind="op", tag=tag)

    def ldv(eng):
        off = _kv_reg(eng, 1)
        srcap = AP(cc1_out.tensor, off + 192, [[192, 128], [128 * 192, NT - 1], [1, 8]])
        dstap = vs_sb[:, 0:(NT - 1) * 8].rearrange("p (j c) -> p j c",
                                                   j=NT - 1, c=8)
        return eng.dma_start(out=dstap, in_=srcap)
    p._add("sync", ldv, r=[whole(cc1_out.tensor), whole(kvo_sb), vs_sb.res()],
           w=[vs_sb.res(slice(0, (NT - 1) * 8))], kind="dma", tag=tag + "_vld")

    def ldv2(eng):
        off = _kv_reg(eng, 1)
        srcap = AP(cc1_out.tensor, off + ((NT - 1) * 128 + 1) * 192,
                   [[192, 127], [1, 8]])
        return eng.dma_start(out=vs_sb.ap(slice(0, 127),
                                          slice((NT - 1) * 8, NT * 8)), in_=srcap)
    p._add("sync", ldv2, r=[whole(cc1_out.tensor), whole(kvo_sb), vs_sb.res()],
           w=[vs_sb.res(slice((NT - 1) * 8, NT * 8))], kind="dma",
           tag=tag + "_vld2")

    cc2_in = mk.ctx.dram(tag + "_cc2in", [4 * SEQ, 8], BF16)
    cc2_out = mk.ctx.dram(tag + "_cc2out", [4 * 4 * SEQ, 8], BF16)
    qtile_all = mk.bf.alloc(NT * 8, tag + "_qtall")
    ostg = mk.bf.alloc(NT * 8, tag + "_ostg")
    st2 = [mk.f32.alloc(4, tag + f"_st{i}") for i in range(2)]
    for hh in range(4):
        p._add("vector", lambda eng: eng.memset(
            qTpad.ap(slice(0, 8), slice(0, 31)), 0.0),
            r=[], w=[qTpad.res(slice(0, 31))], kind="op", tag=tag)
        def ldq(eng, hh=hh):
            off = _kv_reg(eng, 0)
            qoff = (off - 128) * 4 + hh * 8
            srcap = AP(cc1_out.tensor, qoff,
                       [[192, 128], [128 * 192, NT], [1, 8]])
            dstap = qtile_all[:, :].rearrange("p (j c) -> p j c", j=NT, c=8)
            return eng.dma_start(out=dstap, in_=srcap)
        p._add("sync", ldq, r=[whole(cc1_out.tensor), whole(kvo_sb)],
               w=[qtile_all.res()], kind="dma", tag=tag + "_qld")
        for j in range(NT):
            pt = mk.PT0 if j % 2 == 0 else mk.PT1
            mk.tr(pt[0:8, 0:128], qtile_all[:, j * 8:(j + 1) * 8],
                  r=[qtile_all.res(slice(j * 8, (j + 1) * 8))], w=[whole(pt)],
                  tag=tag + "_qtr2")
            p._add("vector", lambda eng, pt=pt, j=j: eng.tensor_copy(
                qTpad.ap(slice(0, 8), slice(31 + j * 128, 31 + (j + 1) * 128)),
                pt[0:8, 0:128]),
                r=[whole(pt)],
                w=[qTpad.res(slice(31 + j * 128, 31 + (j + 1) * 128))],
                kind="op", tag=tag + "_qpe")
        for d in range(16):
            for half, qw in ((0, qwT_lo), (1, qwT_hi)):
                dd = d + 16 * half
                p._add("sync", lambda eng, d=d, dd=dd, qw=qw: eng.dma_start(
                    out=qw.ap(slice(d * 8, (d + 1) * 8), slice(0, SEQ)),
                    in_=qTpad.ap(slice(0, 8), slice(dd, dd + SEQ))),
                    r=[qTpad.res()], w=[qw.res()], kind="dma", tag=tag + "_qw")
        def stageA(qt):
            kmax = (qt + 1) * 128
            nch = (kmax + 511) // 512
            qc = slice(qt * 128, (qt + 1) * 128)
            s_row = s_row2[qt % 2]
            for ch in range(nch):
                ps = mk.PA if ch % 2 == 0 else mk.PB
                nn = min(512, kmax - ch * 512)
                kc = slice(ch * 512, ch * 512 + nn)
                mk.mm(ps[:, 0:nn], qwT_lo[:, qc], kwT_lo[:, kc],
                      r=[qwT_lo.res(qc), kwT_lo.res(kc)], w=[whole(ps)],
                      start=True, stop=False, tag=tag + "_s1")
                mk.mm(ps[:, 0:nn], qwT_hi[:, qc], kwT_hi[:, kc],
                      r=[qwT_hi.res(qc), kwT_hi.res(kc)], w=[whole(ps)],
                      start=False, stop=False, tag=tag + "_s2")
                mk.mm(ps[:, 0:nn], invi.ap(None, qc),
                      rrow.ap(None, kc),
                      r=[invi.res(qc), rrow.res(kc)], w=[whole(ps)],
                      start=False, stop=True, tag=tag + "_sb")
                p._add("scalar", lambda eng, ps=ps, kc=kc, nn=nn,
                       s_row=s_row: eng.activation(
                    s_row[:, kc], ps[:, 0:nn], AF.Copy),
                    r=[whole(ps)], w=[s_row.res(kc)], kind="op",
                    tag=tag + "_sev")

        def stageBC(qt):
            kmax = (qt + 1) * 128
            s_row = s_row2[qt % 2]
            st = st2[qt % 2]
            pvb = mk.PC if qt % 2 == 0 else mk.PF
            dc = slice(qt * 128, kmax)
            p._add("vector", lambda eng, dc=dc, s_row=s_row: eng.tensor_tensor(
                s_row[:, dc], s_row[:, dc], maskc[:, 0:128], ALU.mult),
                r=[s_row.res(dc), maskc.res()], w=[s_row.res(dc)], kind="op",
                tag=tag + "_dm1")
            p._add("vector", lambda eng, dc=dc, s_row=s_row: eng.tensor_tensor(
                s_row[:, dc], s_row[:, dc], negbig[:, 0:128], ALU.add),
                r=[s_row.res(dc), negbig.res()], w=[s_row.res(dc)], kind="op",
                tag=tag + "_dm2")
            p._add("vector", lambda eng, kmax=kmax, s_row=s_row,
                   st=st: eng.reduce_max(
                st[:, 0:1], s_row[:, 0:kmax], axis=AX.X, negate=True),
                r=[s_row.res(slice(0, kmax))], w=[st.res(slice(0, 1))],
                kind="op", tag=tag + "_max")
            p._add("scalar", lambda eng, kmax=kmax, s_row=s_row,
                   st=st: eng.activation(
                s_row[:, 0:kmax], s_row[:, 0:kmax], AF.Exp, bias=st[:, 0:1],
                accum_out=st[:, 1:2]),
                r=[s_row.res(slice(0, kmax)), st.res(slice(0, 1))],
                w=[s_row.res(slice(0, kmax)), st.res(slice(1, 2))], kind="op",
                tag=tag + "_exp")
            if qt == 0:
                p._add("vector", lambda eng, s_row=s_row: eng.memset(
                    s_row.ap(slice(0, 1), slice(0, 128)), 0.0),
                    r=[], w=[s_row.res(slice(0, 128))], kind="op",
                    tag=tag + "_r0")
            p._add("vector", lambda eng, st=st: eng.reciprocal(
                st[:, 2:3], st[:, 1:2]),
                r=[st.res(slice(1, 2))], w=[st.res(slice(2, 3))], kind="op",
                tag=tag + "_rec")
            for g0 in range(0, qt + 1, 8):
                g1 = min(qt + 1, g0 + 8)
                gi = (g0 // 8) % 2
                pt = mk.PT0 if gi == 0 else mk.PT1
                pb = ptb2[gi]
                for kb in range(g0, g1):
                    mk.tr(pt[:, (kb - g0) * 128:(kb - g0 + 1) * 128],
                          s_row[:, kb * 128:(kb + 1) * 128],
                          r=[s_row.res(slice(kb * 128, (kb + 1) * 128))],
                          w=[(pt, (kb - g0) * 128, (kb - g0 + 1) * 128)],
                          tag=tag + "_ptr")
                nn = (g1 - g0) * 128
                p._add("vector", lambda eng, pt=pt, pb=pb, nn=nn: eng.tensor_copy(
                    pb[:, 0:nn], pt[:, 0:nn]),
                    r=[whole(pt)], w=[pb.res(slice(0, nn))], kind="op",
                    tag=tag + "_pte")
                for kb in range(g0, g1):
                    mk.mm(pvb[:, 0:8], pb[:, (kb - g0) * 128:(kb - g0 + 1) * 128],
                          vs_sb[:, kb * 8:(kb + 1) * 8],
                          r=[pb.res(slice((kb - g0) * 128, (kb - g0 + 1) * 128)),
                             vs_sb.res(slice(kb * 8, (kb + 1) * 8))],
                          w=[whole(pvb)], start=(kb == 0), stop=(kb == qt),
                          tag=tag + "_pv")
            p._add("scalar", lambda eng, qt=qt, pvb=pvb, st=st: eng.activation(
                ostg[:, qt * 8:(qt + 1) * 8], pvb[:, 0:8], AF.Identity,
                scale=st[:, 2:3]),
                r=[whole(pvb), st.res(slice(2, 3))],
                w=[ostg.res(slice(qt * 8, (qt + 1) * 8))], kind="op",
                tag=tag + "_oev")

        stageA(0)
        for qt in range(NT):
            if qt + 1 < NT:
                stageA(qt + 1)
            stageBC(qt)
        srcap = ostg[:, :].rearrange("p (q v) -> p q v", q=NT, v=8)
        dstap = AP(cc2_in.tensor, hh * SEQ * 8, [[8, 128], [128 * 8, NT], [1, 8]])
        p._add("sync", lambda eng, srcap=srcap, dstap=dstap: eng.dma_start(
            out=dstap, in_=srcap),
            r=[ostg.res()], w=[whole(cc2_in.tensor)], kind="dma",
            tag=tag + "_ogw")
    for t in (qTpad, qwT_lo, qwT_hi, qtile_all, s_row2[0], s_row2[1],
              ptb2[0], ptb2[1], ostg, kwT_lo, kwT_hi, vs_sb, ktile_all):
        t.free()
    p._add("gpsimd", lambda eng: eng.collective_compute(
        "AllGather", ALU.bypass, replica_groups=cc_rg,
        ins=[cc2_in.tensor.ap().opt()], outs=[cc2_out.tensor.ap().opt()]),
        r=[whole(cc2_in.tensor)], w=[whole(cc2_out.tensor)], kind="cc",
        tag=tag + "_cc2")

    vemb = mk.f32.alloc(4, tag + "_vemb")
    p._add("sync", lambda eng: eng.dma_start(out=vemb[:, 0:1], in_=vemb0_ap),
           r=[whole(vemb0_ap.tensor)], w=[vemb.res(slice(0, 1))], kind="dma",
           tag=tag)
    p._add("sync", lambda eng: eng.dma_start(out=vemb[:, 1:2], in_=vemb1_ap),
           r=[whole(vemb1_ap.tensor)], w=[vemb.res(slice(1, 2))], kind="dma",
           tag=tag)
    p._add("vector", lambda eng: eng.tensor_tensor(
        vemb[:, 2:3], vemb[:, 1:2], vemb[:, 0:1], ALU.subtract),
        r=[vemb.res(slice(0, 2))], w=[vemb.res(slice(2, 3))], kind="op",
        tag=tag)
    qtb_sb = nc.alloc_sbuf_tensor(tag + "_qtb", [1, 1], I32)
    p._add("sync", lambda eng: eng.dma_start(out=qtb_sb[0:1, 0:1], in_=qtb_ap),
           r=[whole(qtb_ap.tensor)], w=[whole(qtb_sb)], kind="dma", tag=tag)
    amix = mk.bf.alloc(128, tag + "_amix")
    attn1 = mk.bf.alloc(1024, tag + "_attn1")
    for ti in range(8):
        def lda(eng, ti=ti):
            if "qtb" not in regs:
                regs["qtb"] = eng.value_load(qtb_sb[0:1, 0:1])
            qb = regs["qtb"]
            off = (qb + ti * 128) * 8
            srcap = AP(cc2_out.tensor, off,
                       [[8, 128], [4 * SEQ * 8, 4], [SEQ * 8, 4], [1, 8]])
            dstap = amix[:, :].rearrange("p (r hh v) -> p r hh v", r=4, hh=4, v=8)
            return eng.dma_start(out=dstap, in_=srcap)
        p._add("sync", lda, r=[whole(cc2_out.tensor), whole(qtb_sb)],
               w=[amix.res()], kind="dma", tag=tag + "_ald")
        pt = mk.PT0 if ti % 2 == 0 else mk.PT1
        mk.tr(pt[:, 0:128], amix[:, :], r=[amix.res()], w=[whole(pt)],
              tag=tag + "_atr")
        p._add("vector", lambda eng, pt=pt, ti=ti: eng.tensor_copy(
            attn1[:, ti * 128:(ti + 1) * 128], pt[:, 0:128]),
            r=[whole(pt)], w=[attn1.res(slice(ti * 128, (ti + 1) * 128))],
            kind="op", tag=tag + "_ate")
    import os as _os
    if _os.environ.get("DBG"):
        raw_o = mk.ctx.out("raw_o", (128, 1024), F32)
        rawf = mk.f32.alloc(1024, tag + "_rawf")
        p._add("vector", lambda eng: eng.tensor_copy(rawf[:, :], attn1[:, :]),
               r=[attn1.res()], w=[rawf.res()], kind="op", tag=tag + "_rawc")
        p._add("sync", lambda eng: eng.dma_start(out=raw_o, in_=rawf[:, :]),
               r=[rawf.res()], w=[whole(raw_o.tensor)], kind="dma",
               tag=tag + "_rawo")
        rawf.free()
        mk._raw_out = raw_o
    p._add("vector", lambda eng: eng.tensor_scalar(
        attn1[:, :], attn1[:, :], vemb[:, 2:3], vemb[:, 0:1],
        ALU.mult, ALU.add),
        r=[attn1.res(), vemb.res(slice(0, 3))], w=[attn1.res()], kind="op",
        tag=tag + "_vemb")
    woSB = mk.bf.alloc(2048, tag + "_wo")
    load_wcol(mk, woT_ap, 0, 2048, 1, woSB, tag=tag + "_wodma")
    for fb in range(DB):
        for th in range(2):
            ps = mk.PD if th == 0 else mk.PE_
            mk.mm(ps[:, :], woSB[:, fb * 128:(fb + 1) * 128],
                  attn1[:, th * 512:(th + 1) * 512],
                  r=[woSB.res(slice(fb * 128, (fb + 1) * 128)),
                     attn1.res(slice(th * 512, (th + 1) * 512))],
                  w=[whole(ps)], tag=tag + "_wom")
            dc = slice(fb * 1024 + th * 512, fb * 1024 + (th + 1) * 512)
            p._add("vector", lambda eng, ps=ps, dc=dc: eng.tensor_tensor(
                x_acc[:, dc], x_acc[:, dc], ps[:, :], ALU.add),
                r=[x_acc.res(dc), whole(ps)], w=[x_acc.res(dc)], kind="op",
                tag=tag + "_woacc")
    for t in (amix, attn1, woSB, taut, invt, st2[0], st2[1], vemb):
        t.free()



def build_l1_ffn(mk, xn1T, x_acc, gw_ap, ebias_ap, sw1_ap, sw2_ap,
                 rw1_aps, rw2_aps, cc_all, tag="l1f"):
    p, nc = mk.p, mk.nc
    onesf = mk.const("l1_onesf", np.ones((1, 128), np.float32))
    ones128b = mk.const("l1f_ones128", np.ones((128, 1), np.float32), BF16)

    gwv = mk.bf.alloc(DB * E1, tag + "_gw")
    load_wcol(mk, gw_ap, 0, E1, DB, gwv, tag=tag + "_gw")
    eb = mk.f32.alloc(16, tag + "_eb")
    ebb = mk.bf.alloc(16, tag + "_ebb")
    p._add("sync", lambda eng: eng.dma_start(out=eb.ap(slice(0, 1), slice(0, E1)),
                                             in_=ebias_ap),
           r=[whole(ebias_ap.tensor)], w=[eb.res(slice(0, E1))], kind="dma",
           tag=tag)
    p._add("vector", lambda eng: eng.tensor_copy(
        ebb.ap(slice(0, 1), slice(0, E1)), eb.ap(slice(0, 1), slice(0, E1))),
        r=[eb.res(slice(0, E1))], w=[ebb.res(slice(0, E1))], kind="op", tag=tag)
    p._add("vector", lambda eng: eng.tensor_copy(
        eb.ap(slice(0, 1), slice(E1, 2 * E1)), ebb.ap(slice(0, 1), slice(0, E1))),
        r=[ebb.res(slice(0, E1))], w=[eb.res(slice(E1, 2 * E1))], kind="op",
        tag=tag)
    p._add("vector", lambda eng: eng.tensor_tensor(
        eb.ap(slice(0, 1), slice(0, E1)), eb.ap(slice(0, 1), slice(0, E1)),
        eb.ap(slice(0, 1), slice(E1, 2 * E1)), ALU.subtract),
        r=[eb.res(slice(0, 2 * E1))], w=[eb.res(slice(0, E1))], kind="op",
        tag=tag)
    p._add("vector", lambda eng: eng.tensor_copy(
        ebb.ap(slice(0, 1), slice(8, 8 + E1)), eb.ap(slice(0, 1), slice(0, E1))),
        r=[eb.res(slice(0, E1))], w=[ebb.res(slice(8, 8 + E1))], kind="op",
        tag=tag)
    onesb_row = mk.const("l1f_onesrow", np.ones((1, 128), np.float32), BF16)

    aff_own = mk.f32.alloc(64, tag + "_aff")
    mask_own = mk.f32.alloc(64, tag + "_mask")
    s_own = mk.f32.alloc(64, tag + "_sown")
    rk = mk.f32.alloc(16, tag + "_rk")
    cc3_in = mk.ctx.dram(tag + "_cc3in", [TPC, 8], F32)
    cc3_out = mk.ctx.dram(tag + "_cc3out", [B * N, 8], F32)
    p._add("vector", lambda eng: eng.memset(mask_own[:, :], 0.0),
           r=[], w=[mask_own.res()], kind="op", tag=tag)
    p._add("vector", lambda eng: eng.memset(s_own[:, :], -1.0),
           r=[], w=[s_own.res()], kind="op", tag=tag)
    for i in range(8):
        ps = mk.PE_
        for kb in range(DB):
            mk.mm(ps[:, 0:E1],
                  xn1T[:, kb * 1024 + i * 128: kb * 1024 + (i + 1) * 128],
                  gwv[:, kb * E1:(kb + 1) * E1],
                  r=[xn1T.res(slice(kb * 1024 + i * 128, kb * 1024 + (i + 1) * 128)),
                     gwv.res(slice(kb * E1, (kb + 1) * E1))],
                  w=[whole(ps)], start=(kb == 0), stop=False, tag=tag + "_gl")
        mk.mm(ps[:, 0:E1], onesb_row[0:1, :], ebb.ap(slice(0, 1), slice(0, E1)),
              r=[onesb_row.res(), ebb.res(slice(0, E1))], w=[whole(ps)],
              start=False, stop=False, tag=tag + "_gb1")
        mk.mm(ps[:, 0:E1], onesb_row[0:1, :], ebb.ap(slice(0, 1), slice(8, 8 + E1)),
              r=[onesb_row.res(), ebb.res(slice(8, 8 + E1))], w=[whole(ps)],
              start=False, stop=True, tag=tag + "_gb2")
        ac = slice(i * 8, i * 8 + E1)
        p._add("scalar", lambda eng, ps=ps, ac=ac: eng.activation(
            aff_own[:, ac], ps[:, 0:E1], AF.Sigmoid),
            r=[whole(ps)], w=[aff_own.res(ac)], kind="op", tag=tag + "_sig")
        for e in range(E1):
            p._add("vector", lambda eng, e=e, ac=ac: eng.tensor_scalar(
                rk[:, 0:E1], aff_own[:, ac], aff_own[:, i * 8 + e:i * 8 + e + 1],
                None, ALU.is_gt),
                r=[aff_own.res(ac)], w=[rk.res(slice(0, E1))], kind="op",
                tag=tag)
            p._add("vector", lambda eng: eng.reduce_sum(
                rk[:, 8:9], rk[:, 0:E1], axis=AX.X),
                r=[rk.res(slice(0, E1))], w=[rk.res(slice(8, 9))], kind="op",
                tag=tag)
            if e > 0:
                p._add("vector", lambda eng, e=e, i=i: eng.tensor_scalar(
                    rk[:, 0:e], aff_own[:, i * 8:i * 8 + e],
                    aff_own[:, i * 8 + e:i * 8 + e + 1], None, ALU.is_equal),
                    r=[aff_own.res(ac)], w=[rk.res(slice(0, e))], kind="op",
                    tag=tag)
                p._add("vector", lambda eng, e=e: eng.reduce_sum(
                    rk[:, 9:10], rk[:, 0:e], axis=AX.X),
                    r=[rk.res(slice(0, e))], w=[rk.res(slice(9, 10))],
                    kind="op", tag=tag)
                p._add("vector", lambda eng: eng.tensor_tensor(
                    rk[:, 8:9], rk[:, 8:9], rk[:, 9:10], ALU.add),
                    r=[rk.res(slice(8, 10))], w=[rk.res(slice(8, 9))],
                    kind="op", tag=tag)
            p._add("vector", lambda eng, e=e, i=i: eng.tensor_scalar(
                mask_own[:, i * 8 + e:i * 8 + e + 1], rk[:, 8:9], 2.0, None,
                ALU.is_lt),
                r=[rk.res(slice(8, 9))],
                w=[mask_own.res(slice(i * 8 + e, i * 8 + e + 1))], kind="op",
                tag=tag)
        p._add("vector", lambda eng, ac=ac: eng.scalar_tensor_tensor(
            s_own[:, ac], aff_own[:, ac], 1.0, mask_own[:, ac],
            ALU.add, ALU.mult),
            r=[aff_own.res(ac), mask_own.res(ac)], w=[s_own.res(ac)],
            kind="op", tag=tag)
        p._add("vector", lambda eng, ac=ac: eng.tensor_scalar(
            s_own[:, ac], s_own[:, ac], -1.0, None, ALU.add),
            r=[s_own.res(ac)], w=[s_own.res(ac)], kind="op", tag=tag)
        p._add("sync", lambda eng, i=i: eng.dma_start(
            out=cc3_in.tensor.ap()[i * 128:(i + 1) * 128, :],
            in_=s_own[:, i * 8:(i + 1) * 8]),
            r=[s_own.res(slice(i * 8, (i + 1) * 8))],
            w=[whole(cc3_in.tensor)], kind="dma", tag=tag + "_sw")
    gwv.free()
    p._add("gpsimd", lambda eng: eng.collective_compute(
        "AllGather", ALU.bypass, replica_groups=cc_all,
        ins=[cc3_in.tensor.ap().opt()], outs=[cc3_out.tensor.ap().opt()]),
        r=[whole(cc3_in.tensor)], w=[whole(cc3_out.tensor)], kind="cc",
        tag=tag + "_cc3")

    sall = mk.f32.alloc(512, tag + "_sall")
    cntb = mk.bf.alloc(512, tag + "_cntb")
    bis = mk.f32.alloc(64, tag + "_bis")
    srcap = AP(cc3_out.tensor, 0, [[8, 128], [1024, 64], [1, 8]])
    p._add("sync", lambda eng: eng.dma_start(
        out=sall[:, :].rearrange("p (g e) -> p g e", g=64, e=8), in_=srcap),
        r=[whole(cc3_out.tensor)], w=[sall.res()], kind="dma", tag=tag + "_sld")
    p._add("vector", lambda eng: eng.memset(bis[0:1, 0:8], -2.0),
           r=[], w=[bis.res(slice(0, 8))], kind="op", tag=tag)
    p._add("vector", lambda eng: eng.memset(bis[0:1, 8:16], 1.0),
           r=[], w=[bis.res(slice(8, 16))], kind="op", tag=tag)
    for it in range(BISECT_ITERS):
        p._add("vector", lambda eng: eng.tensor_tensor(
            bis[0:1, 16:24], bis[0:1, 0:8], bis[0:1, 8:16], ALU.add),
            r=[bis.res(slice(0, 16))], w=[bis.res(slice(16, 24))], kind="op",
            tag=tag)
        p._add("vector", lambda eng: eng.tensor_scalar(
            bis[0:1, 16:24], bis[0:1, 16:24], 0.5, None, ALU.mult),
            r=[bis.res(slice(16, 24))], w=[bis.res(slice(16, 24))], kind="op",
            tag=tag)
        p._add("vector", lambda eng: eng.tensor_copy(
            ebb.ap(slice(0, 1), slice(0, 8)), bis[0:1, 16:24]),
            r=[bis.res(slice(16, 24))], w=[ebb.res(slice(0, 8))], kind="op",
            tag=tag)
        mk.mm(mk.PF[:, 0:8], onesf[0:1, :], bis[0:1, 16:24],
              r=[onesf.res(), bis.res(slice(16, 24))], w=[whole(mk.PF)],
              tag=tag + "_thb")
        thb = AP(mk.PF, 0, [[512, 128], [0, 64], [1, 8]])
        p._add("vector", lambda eng, thb=thb: eng.tensor_tensor(
            cntb[:, :].rearrange("p (g e) -> p g e", g=64, e=8),
            sall[:, :].rearrange("p (g e) -> p g e", g=64, e=8),
            thb, ALU.is_gt),
            r=[sall.res(), whole(mk.PF)], w=[cntb.res()], kind="op",
            tag=tag + "_cmp")
        mk.mm(mk.PE_[0:1, :], ones128b[0:128, 0:1], cntb[:, :],
              r=[ones128b.res(), cntb.res()], w=[whole(mk.PE_)],
              tag=tag + "_csum")
        p._add("vector", lambda eng: eng.reduce_sum(
            bis[0:1, 32:40],
            AP(mk.PE_, 0, [[512, 1], [1, 8], [8, 64]]), axis=AX.X),
            r=[whole(mk.PE_)], w=[bis.res(slice(32, 40))], kind="op",
            tag=tag + "_gred")
        p._add("vector", lambda eng: eng.tensor_scalar(
            bis[0:1, 40:48], bis[0:1, 32:40], float(CAP1), None, ALU.is_ge),
            r=[bis.res(slice(32, 40))], w=[bis.res(slice(40, 48))], kind="op",
            tag=tag)
        p._add("vector", lambda eng: eng.tensor_tensor(
            bis[0:1, 48:56], bis[0:1, 16:24], bis[0:1, 0:8], ALU.subtract),
            r=[bis.res(slice(0, 24))], w=[bis.res(slice(48, 56))], kind="op",
            tag=tag)
        p._add("vector", lambda eng: eng.tensor_tensor(
            bis[0:1, 48:56], bis[0:1, 48:56], bis[0:1, 40:48], ALU.mult),
            r=[bis.res(slice(40, 56))], w=[bis.res(slice(48, 56))], kind="op",
            tag=tag)
        p._add("vector", lambda eng: eng.tensor_tensor(
            bis[0:1, 0:8], bis[0:1, 0:8], bis[0:1, 48:56], ALU.add),
            r=[bis.res(slice(0, 8)), bis.res(slice(48, 56))],
            w=[bis.res(slice(0, 8))], kind="op", tag=tag)
        p._add("vector", lambda eng: eng.tensor_tensor(
            bis[0:1, 48:56], bis[0:1, 16:24], bis[0:1, 8:16], ALU.subtract),
            r=[bis.res(slice(8, 24))], w=[bis.res(slice(48, 56))], kind="op",
            tag=tag)
        p._add("vector", lambda eng: eng.scalar_tensor_tensor(
            bis[0:1, 56:64], bis[0:1, 40:48], 1.0, bis[0:1, 48:56],
            ALU.subtract, ALU.mult),
            r=[bis.res(slice(40, 56))], w=[bis.res(slice(56, 64))], kind="op",
            tag=tag)
        p._add("vector", lambda eng: eng.tensor_tensor(
            bis[0:1, 8:16], bis[0:1, 8:16], bis[0:1, 56:64], ALU.subtract),
            r=[bis.res(slice(8, 16)), bis.res(slice(56, 64))],
            w=[bis.res(slice(8, 16))], kind="op", tag=tag)
    sall.free(); cntb.free()

    wT_all = mk.bf.alloc(1024, tag + "_wT")
    wtile = mk.f32.alloc(8, tag + "_wtile")
    wtb = mk.bf.alloc(8, tag + "_wtb")
    mk.mm(mk.PF[:, 0:8], onesf[0:1, :], bis[0:1, 0:8],
          r=[onesf.res(), bis.res(slice(0, 8))], w=[whole(mk.PF)],
          tag=tag + "_lob")
    lob = mk.f32.alloc(8, tag + "_lob")
    p._add("vector", lambda eng: eng.tensor_copy(lob[:, :], mk.PF[:, 0:8]),
           r=[whole(mk.PF)], w=[lob.res()], kind="op", tag=tag)
    for i in range(8):
        ac = slice(i * 8, (i + 1) * 8)
        p._add("vector", lambda eng, ac=ac: eng.tensor_tensor(
            wtile[:, :], s_own[:, ac], lob[:, :], ALU.is_gt),
            r=[s_own.res(ac), lob.res()], w=[wtile.res()], kind="op", tag=tag)
        p._add("vector", lambda eng, ac=ac: eng.tensor_tensor(
            wtile[:, :], wtile[:, :], s_own[:, ac], ALU.mult),
            r=[wtile.res(), s_own.res(ac)], w=[wtile.res()], kind="op", tag=tag)
        p._add("vector", lambda eng, ac=ac: eng.tensor_tensor(
            wtile[:, :], wtile[:, :], mask_own[:, ac], ALU.mult),
            r=[wtile.res(), mask_own.res(ac)], w=[wtile.res()], kind="op",
            tag=tag)
        p._add("vector", lambda eng: eng.tensor_copy(wtb[:, :], wtile[:, :]),
               r=[wtile.res()], w=[wtb.res()], kind="op", tag=tag)
        pt = mk.PT0 if i % 2 == 0 else mk.PT1
        mk.tr(pt[0:8, 0:128], wtb[:, 0:8], r=[wtb.res()], w=[whole(pt)],
              tag=tag + "_wtr")
        p._add("vector", lambda eng, pt=pt, i=i: eng.tensor_copy(
            wT_all.ap(slice(0, 8), slice(i * 128, (i + 1) * 128)),
            pt[0:8, 0:128]),
            r=[whole(pt)], w=[wT_all.res(slice(i * 128, (i + 1) * 128))],
            kind="op", tag=tag + "_wte")
    wtile.free(); wtb.free(); lob.free()
    for t in (aff_own, mask_own, s_own, rk, eb, ebb, bis):
        t.free()

    sel = np.zeros((8, 8 * 128), np.float32)
    for e in range(8):
        sel[e, e * 128:(e + 1) * 128] = 1.0
    selc = mk.const("l1_sel", sel, BF16)

    h_all = mk.bf.alloc(FB0 * 1024, tag + "_h")
    cb = mk.bf.alloc(1024, tag + "_cb")
    sg = mk.bf.alloc(512, tag + "_sg")
    t1 = mk.bf.alloc(512, tag + "_t1")
    wg = [mk.bf.alloc(DB * 128, tag + f"_wg{i}") for i in range(2)]
    wu = [mk.bf.alloc(DB * 128, tag + f"_wu{i}") for i in range(2)]
    wd = [mk.bf.alloc(FB0 * 128, tag + f"_wd{i}") for i in range(2)]
    ffnbufs = {"xnT": xn1T, "h": h_all, "sg": sg, "t1": t1, "wg": wg,
               "wu": wu, "wd": wd}
    FB1 = F1 // 128
    ffn_expert(mk, ffnbufs, sw1_ap, sw1_ap, sw2_ap, FB1, None, x_acc,
               F1, tag + "_sh", ucol0=0)
    for e in range(E1):
        for th in range(2):
            ps = mk.PF
            mk.mm(ps[:, :], selc[0:8, e * 128:(e + 1) * 128],
                  wT_all.ap(slice(0, 8), slice(th * 512, (th + 1) * 512)),
                  r=[selc.res(slice(e * 128, (e + 1) * 128)),
                     wT_all.res(slice(th * 512, (th + 1) * 512))],
                  w=[whole(ps)], tag=tag + "_cbm")
            p._add("vector", lambda eng, ps=ps, th=th: eng.tensor_copy(
                cb[:, th * 512:(th + 1) * 512], ps[:, :]),
                r=[whole(ps)], w=[cb.res(slice(th * 512, (th + 1) * 512))],
                kind="op", tag=tag + "_cbe")
        ffn_expert(mk, ffnbufs, rw1_aps[e], rw1_aps[e], rw2_aps[e], FB1, cb,
                   x_acc, F1, f"{tag}_e{e}", ucol0=0,
                   src_res_g=whole(rw1_aps[e].tensor),
                   src_res_u=whole(rw1_aps[e].tensor),
                   src_res_d=whole(rw2_aps[e].tensor))
    for t in (h_all, cb, sg, t1, wT_all):
        t.free()
    for lst in (wg, wu, wd):
        for w_ in lst:
            w_.free()



def build_full(mk, inputs):
    ctx = mk.ctx
    x = inputs["L0_attn_x"]
    xr = x.reshape(B * N // TPC, TPC, D)
    x_ap = ctx.inp("x_rows", (TPC, D), F32, lambda c: xr[c])
    xT_ap = ctx.inp("xT_rows", (D, TPC), F32,
                    lambda c: np.ascontiguousarray(xr[c].T))
    wqkvT = np.ascontiguousarray(inputs["L0_attn_w_qkv"].T)
    wq_ap = ctx.inp("wqkvT", (D, 3 * D), F32, lambda c: wqkvT)
    bq_c = mk.const("bqkv", np.ascontiguousarray(
        inputs["L0_attn_b_qkv"].reshape(48, 128).T))
    woutT = np.ascontiguousarray(inputs["L0_attn_w_out"].T)
    woutT_ap = ctx.inp("woutT", (D, D), F32, lambda c: woutT)
    bout_c = mk.const("bout", np.ascontiguousarray(
        inputs["L0_attn_b_out"].reshape(16, 128).T))
    wrouter_ap = ctx.inp("wrouter", (D, E0), F32,
                         lambda c: inputs["L0_ffn_W_router"])
    weg, weu, wed = [], [], []
    for e in range(E0):
        weg.append(ctx.inp(f"weg{e}", (D, DE0), F32,
                           lambda c, e=e: inputs["L0_ffn_We_gate"][e]))
        weu.append(ctx.inp(f"weu{e}", (D, DE0), F32,
                           lambda c, e=e: inputs["L0_ffn_We_up"][e]))
        wed.append(ctx.inp(f"wed{e}", (DE0, D), F32,
                           lambda c, e=e: inputs["L0_ffn_We_down"][e]))
    wsg_ap = ctx.inp("wsg", (D, DS0), F32, lambda c: inputs["L0_ffn_Ws_gate"])
    wsu_ap = ctx.inp("wsu", (D, DS0), F32, lambda c: inputs["L0_ffn_Ws_up"])
    wsd_ap = ctx.inp("wsd", (DS0, D), F32, lambda c: inputs["L0_ffn_Ws_down"])
    wqT1 = ctx.inp("wqT1", (D, 128), F32,
                   lambda c: np.ascontiguousarray(inputs["L1_attn_Wq"].T))
    wkT1 = ctx.inp("wkT1", (D, 32), F32,
                   lambda c: np.ascontiguousarray(inputs["L1_attn_Wk"].T))
    wvT1 = ctx.inp("wvT1", (D, 32), F32,
                   lambda c: np.ascontiguousarray(inputs["L1_attn_Wv"].T))
    woT1 = ctx.inp("woT1", (128, D), F32,
                   lambda c: np.ascontiguousarray(inputs["L1_attn_Wo"].T))
    ve0 = ctx.inp("ve0", (128, 1), F32,
                  lambda c: np.asarray(inputs["L1_attn_v_emb0"]).reshape(128, 1))
    ve1 = ctx.inp("ve1", (128, 1), F32,
                  lambda c: np.asarray(inputs["L1_attn_v_emb1"]).reshape(128, 1))
    tau_ap = ctx.inp("tau", (1, 1), F32,
                     lambda c: np.asarray(inputs["L1_attn_tau"],
                                          np.float32).reshape(1, 1))
    kvoff = ctx.inp("kvoff", (1, 2), I32,
                    lambda c: np.array([[128 + (c % 4) * 8, 160 + (c % 4) * 8]],
                                       np.int32))
    qtb = ctx.inp("qtb", (1, 1), I32,
                  lambda c: np.array([[(c % 4) * 1024]], np.int32))
    gw_ap = ctx.inp("gw1", (D, E1), F32, lambda c: inputs["L1_ffn_gate_w"])
    ebias_ap = ctx.inp("ebias1", (1, E1), F32,
                       lambda c: np.asarray(inputs["L1_ffn_expert_bias"])
                       .reshape(1, E1))
    sw1_ap = ctx.inp("sw1", (D, 2 * F1), F32,
                     lambda c: inputs["L1_ffn_shared_w1"][0])
    sw2_ap = ctx.inp("sw2", (F1, D), F32,
                     lambda c: inputs["L1_ffn_shared_w2"][0])
    rw1, rw2 = [], []
    for e in range(E1):
        rw1.append(ctx.inp(f"rw1_{e}", (D, 2 * F1), F32,
                           lambda c, e=e: inputs["L1_ffn_routed_w1"][e]))
        rw2.append(ctx.inp(f"rw2_{e}", (F1, D), F32,
                           lambda c, e=e: inputs["L1_ffn_routed_w2"][e]))

    xnT = mk.bf.alloc(DB * 1024, "xnT")
    attnT = mk.bf.alloc(H0 * 1024, "attnT")
    x_acc = mk.f32.alloc(DB * 1024, "x_acc")
    build_ln_xnT(mk, x_ap, xnT)
    build_lightning(mk, xnT, wq_ap, bq_c, attnT, [list(range(NCORE))])
    build_l0_finish(mk, xnT, attnT, xT_ap, x_acc, woutT_ap, bout_c, wrouter_ap,
                    weg, weu, wed, wsg_ap, wsu_ap, wsd_ap)
    xnT.free()
    mk.free_const("lg_dq")
    mk.free_const("lg_dmT")
    mk.free_const("l0_sel")
    xn1T = mk.bf.alloc(DB * 1024, "xn1T")
    build_ln_xnT(mk, None, xn1T, tag="ln1", loader=acc_loader(mk, x_acc))
    bgrp = [[0, 1, 2, 3], [4, 5, 6, 7]]
    build_l1_attn(mk, xn1T, x_acc, wqT1, wkT1, wvT1, woT1, ve0, ve1, tau_ap,
                  kvoff, qtb, bgrp)
    build_l1_ffn(mk, xn1T, x_acc, gw_ap, ebias_ap, sw1_ap, sw2_ap, rw1, rw2,
                 [list(range(NCORE))])
    y_o = ctx.out("yT_o", (D, TPC), F32)
    p = mk.p
    for kb in range(DB):
        p._add("sync", lambda eng, kb=kb: eng.dma_start(
            out=y_o[kb * 128:(kb + 1) * 128, :],
            in_=x_acc[:, kb * 1024:(kb + 1) * 1024]),
            r=[x_acc.res(slice(kb * 1024, (kb + 1) * 1024))],
            w=[whole(y_o.tensor)], kind="dma", tag="yout")
    p._add("sync", lambda eng: eng.nop(), r=[whole(y_o.tensor)], w=[],
           kind="op", tag="final")
    return y_o




def kernel(**inputs):
    nc = bass.Bass("TRN2", target_bir_lowering=False)
    mk = MK(nc)
    build_full(mk, inputs)
    mk.p.finalize()
    res = run_bass_kernel_spmd(nc, mk.ctx.in_maps(), core_ids=list(range(NCORE)))
    out = np.concatenate([res.results[c]["yT_o"].T for c in range(NCORE)],
                         axis=0).reshape(B, N, D)
    return np.ascontiguousarray(out, dtype=np.float32)
